# version 08
# speedup vs baseline: 1.6333x; 1.1989x over previous
import numpy as np
from contextlib import ExitStack

import concourse.bass as bass
import concourse.mybir as mybir
from concourse.bass_utils import run_bass_kernel_spmd

F32 = mybir.dt.float32
BF16 = mybir.dt.bfloat16
I32 = mybir.dt.int32
AF = mybir.ActivationFunctionType
ALU = mybir.AluOpType
AX = mybir.AxisListType

B, L, D = 8, 4096, 1024
H, DH = 4, 256
NCH = D // 128
TT = L // 128
N_CORES = 8
ENGINES = ("sync", "scalar", "vector", "tensor", "gpsimd")


class Emitter:

    def __init__(self):
        self.ops = {e: [] for e in ENGINES}
        self.counts = {}

    def op(self, eng, fn, waits=None, incs=None):
        incs = list(incs or [])
        for s, k in incs:
            self.counts[id(s)] = self.counts.get(id(s), 0) + k
        self.ops[eng].append((fn, list(waits or []), incs))

    def n(self, sem):
        return self.counts.get(id(sem), 0)

    def emit(self, blk, order_sems=None):
        ordered = set(order_sems) if order_sems else set()
        trans = {}
        if order_sems:
            for eng in ENGINES:
                for idx, (fn, waits, incs) in enumerate(self.ops[eng]):
                    for s, k in incs:
                        if eng in ordered:
                            hist = trans.setdefault(id(s), [])
                            prev = hist[-1][0] if hist else 0
                            hist.append((prev + k, eng, idx))

        def xlate(s, v):
            if not order_sems or id(s) not in trans:
                return (s, v)
            for cum, eng, idx in trans[id(s)]:
                if cum >= v:
                    return (order_sems[eng], idx + 1)
            raise AssertionError("wait beyond produced count")

        def mk(eng_name):
            def body(e):
                osem = order_sems.get(eng_name) if order_sems else None
                for idx, (fn, waits, incs) in enumerate(self.ops[eng_name]):
                    for s, v in waits:
                        if v > 0:
                            s2, v2 = xlate(s, v)
                            e.wait_ge(s2, v2)
                    if osem is not None and idx > 0:
                        e.wait_ge(osem, idx)
                    ins = fn(e)
                    if osem is not None:
                        if ins is None:
                            ins = e.nop()
                        ins.then_inc(osem, 1)
                        continue
                    for s, k in incs:
                        if ins is None:
                            ins = e.nop()
                        ins.then_inc(s, k)
            return body
        for eng_name, reg in (("sync", blk.sync), ("scalar", blk.scalar),
                              ("vector", blk.vector), ("tensor", blk.tensor),
                              ("gpsimd", blk.gpsimd)):
            if self.ops[eng_name]:
                reg(mk(eng_name))



class LoopEmitter:

    def __init__(self, n_iter, self_wait=("sync", "gpsimd", "scalar",
                                           "vector", "tensor"), elide=False):
        self.ops = {e: [] for e in ENGINES}
        self.n_iter = n_iter
        self.self_wait = set(self_wait)
        self.elide = elide

    def op(self, eng, fn, waits=None):
        self.ops[eng].append((fn, list(waits or [])))
        return (eng, len(self.ops[eng]) - 1)

    def emit(self, blk, prog, pre=None, post_self=()):
        active = [e for e in ENGINES if self.ops[e]]
        osem = {e: prog.sem() for e in active}
        q = {e: (16 if e in ("sync", "gpsimd") else 1) for e in active}
        total = {e: len(self.ops[e]) for e in active}

        def mk(eng_name):
            def body(e):
                for s, v in (pre or {}).get(eng_name, []):
                    if v > 0:
                        e.wait_ge(s, v)
                base = {oe: e.alloc_register(f"b_{oe}") for oe in active}
                scr = e.alloc_register("scr")
                with e.Fori(0, self.n_iter) as i:
                    for oe in active:
                        e.reg_alu(base[oe], i, q[oe] * total[oe], ALU.mult)
                        if oe != eng_name:
                            e.wait_ge(osem[oe], base[oe])
                    for idx, (fn, waits) in enumerate(self.ops[eng_name]):
                        for weng, widx in waits:
                            if (self.elide and weng == eng_name
                                    and weng != "sync" and idx - widx >= 2):
                                continue
                            e.reg_alu(scr, base[weng], q[weng] * (widx + 1),
                                      ALU.add)
                            e.wait_ge(osem[weng], scr)
                        if eng_name in self.self_wait or idx == 0:
                            if idx > 0:
                                e.reg_alu(scr, base[eng_name],
                                          q[eng_name] * idx, ALU.add)
                                e.wait_ge(osem[eng_name], scr)
                            else:
                                e.wait_ge(osem[eng_name], base[eng_name])
                        ins = fn(e, i)
                        if ins is None:
                            ins = e.nop()
                        ins.then_inc(osem[eng_name], q[eng_name])
                if eng_name in post_self:
                    e.wait_ge(osem[eng_name],
                              q[eng_name] * total[eng_name] * self.n_iter)
            return body
        for eng_name, reg in (("sync", blk.sync), ("scalar", blk.scalar),
                              ("vector", blk.vector), ("tensor", blk.tensor),
                              ("gpsimd", blk.gpsimd)):
            if self.ops.get(eng_name):
                reg(mk(eng_name))
        return osem, q, total


class Prog:
    def __init__(self):
        self.nc = bass.Bass("TRN2", target_bir_lowering=False, debug=False)
        self.ctx = ExitStack()
        self._n = 0
        self.em = Emitter()

    def name(self, pfx):
        self._n += 1
        return f"{pfx}_{self._n}"

    def sb(self, shape, dt=F32):
        return self.ctx.enter_context(self.nc.sbuf_tensor(self.name("sb"), shape, dt))

    def ps(self, shape, dt=F32):
        return self.ctx.enter_context(self.nc.psum_tensor(self.name("ps"), shape, dt))

    def sem(self):
        return self.ctx.enter_context(self.nc.semaphore(self.name("s")))

    def din(self, name, shape, dt=F32):
        return self.nc.dram_tensor(name, shape, dt, kind="ExternalInput").ap()

    def dout(self, name, shape, dt=F32):
        return self.nc.dram_tensor(name, shape, dt, kind="ExternalOutput").ap()

    def reg_consts(self, em, values, lc_sem):
        cin = self.din("cvals", [128, len(values)], F32)
        ct = self.sb([128, len(values)])
        em.op("sync", lambda e: e.dma_start(out=ct[:], in_=cin),
              incs=[(lc_sem, 16)])
        for k, v in enumerate(values):
            self.nc.const_aps.register(mybir.dt.float32, float(v), ct[:, k:k + 1])
        return values

    def finish(self, sim_order=False):
        with self.ctx:
            order_sems = None
            if sim_order:
                order_sems = {e: self.sem()
                              for e in ("scalar", "vector", "tensor", "gpsimd")
                              if self.em.ops[e] and not any(
                                  True for fn, w, incs in self.em.ops[e]
                                  for s, k in incs if k >= 16)}
            blk = self.ctx.enter_context(self.nc.Block())
            self.em.emit(blk, order_sems)
        return self.nc


def build_k1(sim_order=False):
    p = Prog()
    em = p.em
    x_in = p.din("x", [L, D], F32)
    wT_in = {n: p.din(n, [D, D], F32) for n in ("wqT", "wkT", "wvT")}
    wbT_in = p.din("wbT", [128, 4 * NCH], F32)
    cw_in = {n: p.din(n, [128, NCH * 4], F32) for n in ("qcw", "kcw", "vcw")}
    ident_in = p.din("ident", [128, 128], BF16)

    xln_out = p.dout("xln_fm", [NCH, 128, L], BF16)
    qkv_out = {"wqT": p.dout("q_fm", [NCH, 128, L], F32),
               "wkT": p.dout("k_fm", [NCH, 128, L], F32),
               "wvT": p.dout("v_fm", [NCH, 128, L], F32)}
    cw_of = {"wqT": "qcw", "wkT": "kcw", "wvT": "vcw"}
    beta_out = p.dout("beta", [4, L], F32)

    ident = p.sb([128, 128], BF16)
    cw = {n: p.sb([128, NCH * 4]) for n in cw_in}
    wbTb = p.sb([128, 4 * NCH], BF16)
    wbTs = p.sb([128, 4 * NCH], F32)
    xln_fm = [p.sb([128, L], BF16) for _ in range(NCH)]

    xt = [p.sb([128, D]) for _ in range(2)]
    sqj = p.sb([128, D])
    stat = [{k: p.sb([128, 1]) for k in ("sum", "sq", "nm", "m", "var", "rstd")}
            for _ in range(2)]
    xln_tm = [p.sb([128, D], BF16) for _ in range(2)]
    pt = [p.ps([128, 128], BF16) for _ in range(2)]

    wstage = [p.sb([128, D]) for _ in range(2)]
    wbf = [p.sb([128, D], BF16) for _ in range(NCH)]
    pmm = [p.ps([128, 512]) for _ in range(2)]
    qpre = p.sb([128, L])
    accv = p.sb([128, L])
    pbeta = p.ps([4, 512])
    betas = p.sb([4, L])

    lc = p.sem()
    ldx = [p.sem(), p.sem()]
    lw2 = [p.sem(), p.sem()]
    sca = p.sem()
    vec = p.sem()
    pe = p.sem()
    tr = p.sem()
    qe = p.sem()
    stq = p.sem()
    stx = p.sem()

    em.op("sync", lambda e: e.dma_start(out=ident[:], in_=ident_in), incs=[(lc, 16)])
    for n in cw_in:
        em.op("sync", lambda e, n=n: e.dma_start(out=cw[n][:], in_=cw_in[n]), incs=[(lc, 16)])
    em.op("sync", lambda e: e.dma_start(out=wbTs[:], in_=wbT_in), incs=[(lc, 16)])
    consts_ld = em.n(lc)
    em.op("scalar", lambda e: e.copy(out=wbTb[:], in_=wbTs[:]),
          waits=[(lc, consts_ld)], incs=[(sca, 1)])

    xt_free = [(0, 0)] * 2
    tm_free = [0] * 2
    for i in range(TT):
        buf = i % 2
        vfree, sfree = xt_free[buf]
        em.op("sync",
              lambda e, i=i, buf=buf: e.dma_start(
                  out=xt[buf][:], in_=x_in[i * 128:(i + 1) * 128, :]),
              waits=[(vec, vfree), (sca, sfree)], incs=[(ldx[buf], 16)])
        ld_i = em.n(ldx[buf])
        em.op("vector", lambda e, buf=buf: e.tensor_reduce(
            out=stat[buf]["sum"][:], in_=xt[buf][:], axis=AX.X, op=ALU.add),
            waits=[(ldx[buf], ld_i)])
        em.op("scalar", lambda e, buf=buf: e.activation(
            out=sqj[:], in_=xt[buf][:], func=AF.Square, accum_out=stat[buf]["sq"][:]),
            waits=[(ldx[buf], ld_i)], incs=[(sca, 1)])
        sq_done = em.n(sca)
        em.op("vector", lambda e, buf=buf: e.tensor_scalar_mul(
            out=stat[buf]["nm"][:], in0=stat[buf]["sum"][:], scalar1=-1.0 / D))
        em.op("vector", lambda e, buf=buf: e.tensor_scalar_mul(
            out=stat[buf]["m"][:], in0=stat[buf]["sum"][:], scalar1=1.0 / D))
        em.op("vector", lambda e, buf=buf: e.scalar_tensor_tensor(
            out=stat[buf]["var"][:], in0=stat[buf]["nm"][:], scalar=stat[buf]["m"][:, 0:1],
            in1=stat[buf]["sq"][:], op0=ALU.mult, op1=ALU.add),
            waits=[(sca, sq_done)])
        em.op("vector", lambda e, buf=buf: e.tensor_scalar(
            out=stat[buf]["var"][:], in0=stat[buf]["var"][:], scalar1=1.0 / D,
            scalar2=1e-5, op0=ALU.mult, op1=ALU.add), incs=[(vec, 1)])
        var_done = em.n(vec)
        em.op("scalar", lambda e, buf=buf: e.activation(
            out=stat[buf]["rstd"][:], in_=stat[buf]["var"][:], func=AF.Sqrt),
            waits=[(vec, var_done)], incs=[(sca, 1)])
        sd_done = em.n(sca)
        em.op("vector", lambda e: e.nop(), waits=[(sca, sd_done)])
        em.op("vector", lambda e, buf=buf: e.reciprocal(
            out=stat[buf]["rstd"][:], in_=stat[buf]["rstd"][:]))
        em.op("vector", lambda e, buf=buf: e.tensor_scalar(
            out=xln_tm[buf][:], in0=xt[buf][:], scalar1=stat[buf]["nm"][:, 0:1],
            scalar2=stat[buf]["rstd"][:, 0:1], op0=ALU.add, op1=ALU.mult),
            waits=[(pe, tm_free[buf])], incs=[(vec, 1)])
        xt_free[buf] = (em.n(vec), sq_done)
        tm_ready = em.n(vec)
        for g in range(NCH):
            k = i * NCH + g
            bb = k % 2
            w = [(vec, tm_ready), (lc, consts_ld)]
            if k >= 2:
                w.append((tr, k - 1))
            em.op("tensor", lambda e, buf=buf, g=g, bb=bb: e.transpose(
                out=pt[bb][:], in_=xln_tm[buf][:, g * 128:(g + 1) * 128],
                identity=ident[:]),
                waits=w, incs=[(pe, 1)])
            em.op("vector", lambda e, i=i, g=g, bb=bb: e.tensor_copy(
                xln_fm[g][:, i * 128:(i + 1) * 128], pt[bb][:]),
                waits=[(pe, em.n(pe))], incs=[(tr, 1)])
        tm_free[buf] = em.n(pe)
    ln_tr = em.n(tr)

    for g in range(NCH):
        em.op("sync", lambda e, g=g: e.dma_start(out=xln_out[g], in_=xln_fm[g][:]),
              waits=[(tr, ln_tr)], incs=[(stx, 16)])

    wstage_free = [0, 0]
    for wname in ("wqT", "wkT", "wvT"):
        conv_name = cw_of[wname]
        for g in range(NCH):
            w = []
            if wstage_free[g % 2] > 0:
                w.append((sca, wstage_free[g % 2]))
            cw2 = [(pe, em.n(pe))] if g == 0 else []
            em.op("gpsimd", lambda e, wname=wname, g=g: e.dma_start(
                out=wstage[g % 2][:], in_=wT_in[wname][g * 128:(g + 1) * 128, :]),
                waits=w, incs=[(lw2[g % 2], 16)])
            em.op("scalar", lambda e, g=g: e.copy(out=wbf[g][:], in_=wstage[g % 2][:]),
                  waits=[(lw2[g % 2], em.n(lw2[g % 2]))] + cw2, incs=[(sca, 1)])
            wstage_free[g % 2] = em.n(sca)
        conv_done = em.n(sca)
        for gc in range(NCH):
            for t8 in range(8):
                grp = gc * 8 + t8
                w = [(sca, conv_done)]
                if gc == 0 and t8 == 0:
                    w.append((tr, ln_tr))
                if grp >= 2:
                    w.append((qe, em.n(qe) - 1))
                for gd in range(NCH):
                    em.op("tensor", lambda e, gd=gd, gc=gc, t8=t8, grp=grp: e.matmul(
                        pmm[grp % 2][:],
                        wbf[gd][:, gc * 128:(gc + 1) * 128],
                        xln_fm[gd][:, t8 * 512:(t8 + 1) * 512],
                        start=(gd == 0), stop=(gd == NCH - 1)),
                        waits=(w if gd == 0 else None),
                        incs=([(pe, 1)] if gd == NCH - 1 else None))
                em.op("vector", lambda e, t8=t8, grp=grp: e.tensor_copy(
                    qpre[:, t8 * 512:(t8 + 1) * 512], pmm[grp % 2][:]),
                    waits=[(pe, em.n(pe))], incs=[(qe, 1)])
            evac_all = em.n(qe)
            w0 = [(qe, evac_all)]
            if em.n(stq) > 0:
                w0.append((stq, em.n(stq)))
            em.op("vector", lambda e, conv_name=conv_name, gc=gc: e.tensor_scalar_mul(
                out=accv[:], in0=qpre[:], scalar1=cw[conv_name][:, gc * 4 + 3:gc * 4 + 4]),
                waits=w0)
            for s in (1, 2, 3):
                em.op("vector", lambda e, conv_name=conv_name, gc=gc, s=s:
                      e.scalar_tensor_tensor(
                          out=accv[:, s:L], in0=qpre[:, 0:L - s],
                          scalar=cw[conv_name][:, gc * 4 + 3 - s:gc * 4 + 4 - s],
                          in1=accv[:, s:L], op0=ALU.mult, op1=ALU.add))
            em.op("vector", lambda e: e.nop(), incs=[(vec, 1)])
            em.op("scalar", lambda e: e.activation(out=accv[:], in_=accv[:], func=AF.Silu),
                  waits=[(vec, em.n(vec))], incs=[(sca, 1)])
            em.op("sync", lambda e, wname=wname, gc=gc: e.dma_start(
                out=qkv_out[wname][gc], in_=accv[:]),
                waits=[(sca, em.n(sca))], incs=[(stq, 16)])

    for t8 in range(8):
        w = [(sca, 1)]
        if t8 >= 1:
            w.append((sca, em.n(sca)))
        for gd in range(NCH):
            em.op("tensor", lambda e, gd=gd, t8=t8: e.matmul(
                pbeta[:], wbTb[:, gd * 4:(gd + 1) * 4],
                xln_fm[gd][:, t8 * 512:(t8 + 1) * 512],
                start=(gd == 0), stop=(gd == NCH - 1)),
                waits=(w if gd == 0 else None),
                incs=([(pe, 1)] if gd == NCH - 1 else None))
        em.op("scalar", lambda e, t8=t8: e.activation(
            out=betas[:, t8 * 512:(t8 + 1) * 512], in_=pbeta[:], func=AF.Sigmoid),
            waits=[(pe, em.n(pe))], incs=[(sca, 1)])
    em.op("sync", lambda e: e.dma_start(out=beta_out, in_=betas[:]),
          waits=[(sca, em.n(sca))], incs=[(stx, 16)])

    em.op("sync", lambda e: e.wait_ge(stq, em.n(stq)))
    em.op("sync", lambda e: e.wait_ge(stx, em.n(stx)))
    return p.finish(sim_order=sim_order)



def build_k2(sim_order=True):
    p = Prog()
    em = p.em
    nc = p.nc
    q_in = p.din("q_fm", [NCH, 128, L], F32)
    k_in = p.din("k_fm", [NCH, 128, L], F32)
    v_in = p.din("v_fm", [NCH, 128, L], F32)
    beta_in = p.din("beta", [4, L], F32)
    ident_in = p.din("ident", [128, 128], BF16)
    mlow_in = p.din("mlow", [128, 128], F32)
    mupp_in = p.din("mupp", [128, 128], F32)
    muppeq_in = p.din("muppeq", [128, 128], F32)
    onesc_in = p.din("onesc", [128, 1], F32)
    onesr_in = p.din("onesr", [1, 128], F32)

    delta_out = p.dout("delta_tm", [4, TT, 128, DH], F32)

    qn_d = p.nc.dram_tensor("qn_d", [128, NCH * L], BF16, kind="Internal").ap()
    kn_d = p.nc.dram_tensor("kn_d", [128, NCH * L], BF16, kind="Internal").ap()
    kb_d = p.nc.dram_tensor("kb_d", [128, NCH * L], BF16, kind="Internal").ap()
    vb_d = p.nc.dram_tensor("vb_d", [128, NCH * L], BF16, kind="Internal").ap()

    ident = p.sb([128, 128], BF16)
    mlow = p.sb([128, 128])
    mupp = p.sb([128, 128])
    muppeq = p.sb([128, 128])
    onesc = p.sb([128, 1])
    onesr = p.sb([1, 128])
    bt = [p.sb([1, L]) for _ in range(4)]

    lc = p.sem()
    l6 = p.sem()
    sca = p.sem()
    vec = p.sem()
    pe = p.sem()

    for ap_, t in ((ident_in, ident), (mlow_in, mlow), (mupp_in, mupp),
                   (muppeq_in, muppeq), (onesc_in, onesc), (onesr_in, onesr)):
        em.op("sync", lambda e, ap_=ap_, t=t: e.dma_start(out=t[:], in_=ap_),
              incs=[(lc, 16)])
    for h in range(4):
        em.op("sync", lambda e, h=h: e.dma_start(out=bt[h][:], in_=beta_in[h:h + 1, :]),
              incs=[(lc, 16)])
    lc_all = em.n(lc)

    stg = {n: [p.sb([128, 512]) for _ in range(2)] for n in ("q", "k", "v")}
    sqt = [p.sb([128, 512]) for _ in range(2)]
    P1 = p.ps([128, 512])
    P2 = p.ps([128, 512])
    P3 = p.ps([128, 512])
    P4 = p.ps([128, 512])
    P5 = p.ps([128, 512])
    psq = P1[0:1, :]
    srow = p.sb([1, 512])
    rq = p.sb([1, 512])
    rk = p.sb([1, 512])
    rkb = p.sb([1, 512])
    pbc = [P2[:, :], P3[:, :]]
    outt = {(s, c): p.sb([128, 512], BF16) for s in range(4) for c in range(2)}
    st1s = {k: p.sem() for k in outt}

    Sf = [[p.sb([128, 256]) for _ in range(2)] for _ in range(4)]
    Sb = [[p.sb([128, 256], BF16) for _ in range(2)] for _ in range(4)]
    for h in range(4):
        for c in range(2):
            em.op("vector", lambda e, h=h, c=c: e.memset(Sf[h][c][:], 0.0))
            em.op("vector", lambda e, h=h, c=c: e.memset(Sb[h][c][:], 0.0))

    stg_free = {}
    out_free = {}
    for h in range(4):
        g0, g1 = 2 * h, 2 * h + 1
        for t8 in range(8):
            sl = slice(t8 * 512, (t8 + 1) * 512)
            for n, src_ap in (("q", q_in), ("k", k_in), ("v", v_in)):
                for ci, g in enumerate((g0, g1)):
                    w = [stg_free.get((n, ci), (vec, 0))]
                    em.op("sync", lambda e, n=n, ci=ci, g=g, sl=sl, src_ap=src_ap:
                          e.dma_start(out=stg[n][ci][:], in_=src_ap[g][:, sl]),
                          waits=[x for x in w if x[1] > 0], incs=[(l6, 16)])
            l_now = em.n(l6)
            for tgt, n in ((rq, "q"), (rk, "k")):
                for ci in range(2):
                    em.op("scalar", lambda e, n=n, ci=ci: e.activation(
                        out=sqt[ci][:], in_=stg[n][ci][:], func=AF.Square),
                        waits=[(l6, l_now)], incs=[(sca, 1)])
                sq_done = em.n(sca)
                for ci in range(2):
                    em.op("tensor", lambda e, ci=ci: e.matmul(
                        psq, onesc[:], sqt[ci][:],
                        start=(ci == 0), stop=(ci == 1)),
                        waits=([(sca, sq_done), (lc, lc_all), (vec, em.n(vec))]
                               if ci == 0 else None),
                        incs=([(pe, 1)] if ci == 1 else None))
                em.op("scalar", lambda e: e.activation(
                    out=srow[:], in_=psq, func=AF.Sqrt),
                    waits=[(pe, em.n(pe))], incs=[(sca, 1)])
                em.op("vector", lambda e: e.nop(), waits=[(sca, em.n(sca))])
                em.op("vector", lambda e, tgt=tgt: e.reciprocal(
                    out=tgt[:], in_=srow[:]), incs=[(vec, 1)])
            em.op("vector", lambda e, h=h, sl=sl: e.tensor_mul(
                rkb[:], rk[:], bt[h][:, sl]), incs=[(vec, 1)])
            rows_done = em.n(vec)
            plan = [(rq, "q", 0), (rk, "k", 1), (rkb, "k", 2), (None, "v", 3)]
            for bi, (row, n, slot) in enumerate(plan):
                bb = bi % 2
                row_ap = (row[:] if row is not None else bt[h][0:1, sl])
                em.op("tensor", lambda e, row_ap=row_ap, bb=bb: e.matmul(
                    pbc[bb], onesr[:], row_ap, start=True, stop=True),
                    waits=[(vec, rows_done), (vec, em.n(vec))], incs=[(pe, 1)])
                pe_now = em.n(pe)
                for ci in range(2):
                    ofr = out_free.get((slot, ci))
                    em.op("vector", lambda e, n=n, ci=ci, slot=slot, bb=bb:
                          e.tensor_mul(outt[(slot, ci)][:], stg[n][ci][:], pbc[bb]),
                          waits=[(pe, pe_now)] + ([ofr] if ofr else []),
                          incs=[(vec, 1)])
                    dst = {0: qn_d, 1: kn_d, 2: kb_d, 3: vb_d}[slot]
                    g = (g0, g1)[ci]
                    em.op("sync", lambda e, dst=dst, g=g, sl=sl, slot=slot, ci=ci:
                          e.dma_start(out=dst[:, g * L + sl.start:g * L + sl.stop],
                                      in_=outt[(slot, ci)][:]),
                          waits=[(vec, em.n(vec))], incs=[(st1s[(slot, ci)], 16)])
                    out_free[(slot, ci)] = (st1s[(slot, ci)], em.n(st1s[(slot, ci)]))
            for n in ("q", "k", "v"):
                for ci in range(2):
                    stg_free[(n, ci)] = (vec, em.n(vec))
    st1_done = [(s, em.n(s)) for s in st1s.values()]
    pe1_all = em.n(pe)
    vec1_all = em.n(vec)
    sca1_all = em.n(sca)

    lem = LoopEmitter(TT)
    qnw = p.sb([128, 256], BF16)
    knw = p.sb([128, 256], BF16)
    kbw = p.sb([128, 256], BF16)
    knt = p.sb([128, 256], BF16)
    xw = p.sb([128, 512], BF16)
    S1t = p.sb([128, 128], BF16)
    ST1t = p.sb([128, 128], BF16)
    attnT = p.sb([128, 128], BF16)
    Ppow = [p.sb([128, 128], BF16) for _ in range(6)]
    PTpow = [p.sb([128, 128], BF16) for _ in range(7)]
    wt = [p.sb([128, 128], BF16) for _ in range(2)]
    ua = p.sb([128, 256], BF16)
    ot = p.sb([128, 256])
    pG = P3[:, 0:128]
    pPa = P1[:, 0:128]
    pPb = P2[:, 0:128]
    pA = P4[:, :]
    pS0 = P5[:, 0:256]
    pS1 = P1[:, 256:512]
    ptt = [p.ps([128, 128], BF16) for _ in range(2)]

    dof = delta_out

    prev_tail = []
    prev_so = None
    for h in range(4):
        g0, g1 = 2 * h, 2 * h + 1
        lds = []
        lds.append(lem.op("sync", lambda e, i, g0=g0: e.dma_start(
            out=qnw[:, 0:128], in_=qn_d[:, g0 * L:(g0 + 1) * L][:, bass.ts(i, 128)]),
            waits=prev_tail))
        lds.append(lem.op("sync", lambda e, i, g1=g1: e.dma_start(
            out=qnw[:, 128:256], in_=qn_d[:, g1 * L:(g1 + 1) * L][:, bass.ts(i, 128)])))
        lds.append(lem.op("sync", lambda e, i, g0=g0: e.dma_start(
            out=knw[:, 0:128], in_=kn_d[:, g0 * L:(g0 + 1) * L][:, bass.ts(i, 128)])))
        lds.append(lem.op("sync", lambda e, i, g1=g1: e.dma_start(
            out=knw[:, 128:256], in_=kn_d[:, g1 * L:(g1 + 1) * L][:, bass.ts(i, 128)])))
        lds.append(lem.op("sync", lambda e, i, g0=g0: e.dma_start(
            out=kbw[:, 0:128], in_=kb_d[:, g0 * L:(g0 + 1) * L][:, bass.ts(i, 128)])))
        lds.append(lem.op("sync", lambda e, i, g1=g1: e.dma_start(
            out=kbw[:, 128:256], in_=kb_d[:, g1 * L:(g1 + 1) * L][:, bass.ts(i, 128)])))
        lds.append(lem.op("sync", lambda e, i, g0=g0: e.dma_start_transpose(
            out=xw[:, 0:128], in_=vb_d[:, g0 * L:(g0 + 1) * L][:, bass.ts(i, 128)])))
        lds.append(lem.op("sync", lambda e, i, g1=g1: e.dma_start_transpose(
            out=xw[:, 128:256], in_=vb_d[:, g1 * L:(g1 + 1) * L][:, bass.ts(i, 128)])))
        lds.append(lem.op("sync", lambda e, i, g0=g0: e.dma_start_transpose(
            out=xw[:, 256:384], in_=kb_d[:, g0 * L:(g0 + 1) * L][:, bass.ts(i, 128)])))
        lds.append(lem.op("sync", lambda e, i, g1=g1: e.dma_start_transpose(
            out=xw[:, 384:512], in_=kb_d[:, g1 * L:(g1 + 1) * L][:, bass.ts(i, 128)])))
        lds.append(lem.op("sync", lambda e, i, g0=g0: e.dma_start_transpose(
            out=knt[:, 0:128], in_=kn_d[:, g0 * L:(g0 + 1) * L][:, bass.ts(i, 128)])))
        lds.append(lem.op("sync", lambda e, i, g1=g1: e.dma_start_transpose(
            out=knt[:, 128:256], in_=kn_d[:, g1 * L:(g1 + 1) * L][:, bass.ts(i, 128)])))
        ld_all = lds[-1]

        h1 = lem.op("tensor", lambda e, i: e.matmul(
            pG, kbw[:, 0:128], knw[:, 0:128], start=True, stop=False),
            waits=[lds[2], lds[3], lds[4], lds[5]])
        h1 = lem.op("tensor", lambda e, i: e.matmul(
            pG, kbw[:, 128:256], knw[:, 128:256], start=False, stop=True))
        m1 = lem.op("vector", lambda e, i: e.tensor_mul(S1t[:], pG, mlow[:]),
                    waits=[h1])
        h2 = lem.op("tensor", lambda e, i: e.matmul(
            pG, knw[:, 0:128], kbw[:, 0:128], start=True, stop=False),
            waits=[m1])
        h2 = lem.op("tensor", lambda e, i: e.matmul(
            pG, knw[:, 128:256], kbw[:, 128:256], start=False, stop=True))
        m2 = lem.op("vector", lambda e, i: e.tensor_mul(ST1t[:], pG, mupp[:]),
                    waits=[h2])
        h3 = lem.op("tensor", lambda e, i: e.matmul(
            pG, knw[:, 0:128], qnw[:, 0:128], start=True, stop=False),
            waits=[m2, lds[0], lds[1]])
        h3 = lem.op("tensor", lambda e, i: e.matmul(
            pG, knw[:, 128:256], qnw[:, 128:256], start=False, stop=True))
        m3 = lem.op("vector", lambda e, i: e.tensor_mul(attnT[:], pG, muppeq[:]),
                    waits=[h3])

        prevP, prevPT = S1t, ST1t
        ev_prev = m3
        for lev in range(1, 6):
            ha = lem.op("tensor", lambda e, i, prevP=prevP, prevPT=prevPT: e.matmul(
                pPa, prevPT[:], prevP[:], start=True, stop=True),
                waits=[ev_prev] if ev_prev else None)
            hb = lem.op("tensor", lambda e, i, prevP=prevP, prevPT=prevPT: e.matmul(
                pPb, prevP[:], prevPT[:], start=True, stop=True))
            ea = lem.op("vector", lambda e, i, lev=lev: e.tensor_copy(
                Ppow[lev][:], pPa), waits=[ha])
            eb = lem.op("vector", lambda e, i, lev=lev: e.tensor_copy(
                PTpow[lev][:], pPb), waits=[hb])
            prevP, prevPT = Ppow[lev], PTpow[lev]
            ev_prev = eb
        h64 = lem.op("tensor", lambda e, i: e.matmul(
            pPa, Ppow[5][:], PTpow[5][:], start=True, stop=True),
            waits=[ev_prev])
        e64 = lem.op("vector", lambda e, i: e.tensor_copy(PTpow[6][:], pPa),
                     waits=[h64])

        PT_of = {1: ST1t, 2: PTpow[1], 4: PTpow[2], 8: PTpow[3], 16: PTpow[4],
                 32: PTpow[5], 64: PTpow[6]}
        last_stt = e64
        for pw in (64, 32, 16, 8, 4, 2, 1):
            hm = lem.op("tensor", lambda e, i, pw=pw: e.matmul(
                pA, PT_of[pw][:], xw[:], start=True, stop=True),
                waits=[last_stt] + ([ld_all] if pw == 64 else []))
            sign = 1.0 if pw > 1 else -1.0
            last_stt = lem.op("vector", lambda e, i, sign=sign: e.scalar_tensor_tensor(
                out=xw[:], in0=pA, scalar=sign, in1=xw[:],
                op0=ALU.mult, op1=ALU.add), waits=[hm])

        t0 = lem.op("tensor", lambda e, i: e.transpose(
            ptt[0][:], xw[:, 256:384], identity=ident[:]), waits=[last_stt])
        t1 = lem.op("tensor", lambda e, i: e.transpose(
            ptt[1][:], xw[:, 384:512], identity=ident[:]))
        e0 = lem.op("vector", lambda e, i: e.tensor_copy(wt[0][:], ptt[0][:]),
                    waits=[t0])
        e1 = lem.op("vector", lambda e, i: e.tensor_copy(wt[1][:], ptt[1][:]),
                    waits=[t1])

        hu = lem.op("tensor", lambda e, i, h=h: e.matmul(
            pA[:, 0:256], wt[0][:], Sb[h][0][:], start=True, stop=False),
            waits=[e0, e1])
        hu = lem.op("tensor", lambda e, i, h=h: e.matmul(
            pA[:, 0:256], wt[1][:], Sb[h][1][:], start=False, stop=True))
        eu = lem.op("vector", lambda e, i: e.scalar_tensor_tensor(
            out=ua[:], in0=pA[:, 0:256], scalar=-1.0, in1=xw[:, 0:256],
            op0=ALU.mult, op1=ALU.add), waits=[hu])

        ho = lem.op("tensor", lambda e, i, h=h: e.matmul(
            pA[:, 0:256], qnw[:, 0:128], Sb[h][0][:], start=True, stop=False),
            waits=[eu])
        ho = lem.op("tensor", lambda e, i, h=h: e.matmul(
            pA[:, 0:256], qnw[:, 128:256], Sb[h][1][:], start=False, stop=False))
        ho = lem.op("tensor", lambda e, i: e.matmul(
            pA[:, 0:256], attnT[:], ua[:], start=False, stop=True))
        eo = lem.op("vector", lambda e, i: e.tensor_copy(ot[:], pA[:, 0:256]),
                    waits=[ho] + ([prev_so] if prev_so else []))
        so = lem.op("sync", lambda e, i, h=h: e.dma_start(
            out=dof[h][i], in_=ot[:]), waits=[eo])

        hs0 = lem.op("tensor", lambda e, i: e.matmul(
            pS0, knt[:, 0:128], ua[:], start=True, stop=True), waits=[eo])
        hs1 = lem.op("tensor", lambda e, i: e.matmul(
            pS1, knt[:, 128:256], ua[:], start=True, stop=True))
        us0 = lem.op("vector", lambda e, i, h=h: e.scalar_tensor_tensor(
            out=Sf[h][0][:], in0=pS0, scalar=1.0, in1=Sf[h][0][:],
            op0=ALU.mult, op1=ALU.add), waits=[hs0])
        us1 = lem.op("vector", lambda e, i, h=h: e.scalar_tensor_tensor(
            out=Sf[h][1][:], in0=pS1, scalar=1.0, in1=Sf[h][1][:],
            op0=ALU.mult, op1=ALU.add), waits=[hs1])
        lem.op("vector", lambda e, i, h=h: e.tensor_copy(Sb[h][0][:], Sf[h][0][:]))
        ve_last = lem.op("vector", lambda e, i, h=h: e.tensor_copy(
            Sb[h][1][:], Sf[h][1][:]))
        prev_tail = [hs1, ve_last]
        prev_so = so

    with p.ctx:
        osems = ({e: p.sem() for e in ("scalar", "vector", "tensor")
                  if em.ops[e]} if sim_order else None)
        with nc.Block() as blk1:
            em.emit(blk1, order_sems=osems)
        if osems:
            pre_t = [(osems["vector"], len(em.ops["vector"])),
                     (osems["scalar"], len(em.ops["scalar"])),
                     (lc, lc_all)]
        else:
            pre_t = [(lc, lc_all), (vec, vec1_all), (sca, sca1_all)]
        pre = {
            "sync": list(st1_done),
            "tensor": pre_t,
        }
        with nc.Block() as blk2:
            lem.emit(blk2, p, pre=pre, post_self=("sync",))
    return p.nc


def _host_k2_consts():
    import ml_dtypes
    ident = np.eye(128, dtype=ml_dtypes.bfloat16)
    idx = np.arange(128)
    mlow = (idx[:, None] > idx[None, :]).astype(np.float32)
    mupp = (idx[:, None] < idx[None, :]).astype(np.float32)
    muppeq = (idx[:, None] <= idx[None, :]).astype(np.float32)
    onesc = np.ones((128, 1), np.float32)
    onesr = np.ones((1, 128), np.float32)
    return dict(ident=ident, mlow=mlow, mupp=mupp, muppeq=muppeq,
                onesc=onesc, onesr=onesr)


def run_k2(k1_results):
    nc = build_k2(sim_order=True)
    consts = _host_k2_consts()
    maps = []
    for b in range(N_CORES):
        r = k1_results[b]
        maps.append(dict(q_fm=r["q_fm"], k_fm=r["k_fm"], v_fm=r["v_fm"],
                         beta=r["beta"], **consts))
    res = run_bass_kernel_spmd(nc, maps, core_ids=list(range(N_CORES)))
    return res.results



def build_k3a(sim_order=True):
    p = Prog()
    em = p.em
    nc = p.nc
    v_in = p.din("v_fm", [NCH, 128, L], F32)
    d_in = p.din("delta_tm", [4, TT, 128, DH], F32)
    fs_in = p.din("fsw", [128, NCH * 5], F32)
    fl_in = p.din("flw", [128, NCH * 64], F32)
    ident_in = p.din("ident", [128, 128], BF16)

    ls_out = p.dout("ls_fm", [NCH, 128, L], F32)
    ll_out = p.dout("ll_fm", [NCH, 128, L], F32)
    df_out = p.dout("delta_fm", [NCH, 128, L], F32)

    fsw = p.sb([128, NCH * 5])
    flw = p.sb([128, NCH * 64])
    ident = p.sb([128, 128], BF16)
    identf = p.sb([128, 128], F32)

    vt = [p.sb([128, L]) for _ in range(2)]
    accd = p.sb([128, L])
    accp = p.sb([128, L])
    accl = p.sb([128, L])
    dt_t = [p.sb([128, DH]) for _ in range(2)]
    dfm = [p.sb([128, 128]) for _ in range(2)]
    ptr = [p.ps([128, 128]) for _ in range(2)]

    lc = p.sem()
    lv = [p.sem(), p.sem()]
    ldd = [p.sem(), p.sem()]
    vec = p.sem()
    gp = p.sem()
    pe = p.sem()
    stl = p.sem()
    stll = p.sem()
    std2 = [p.sem(), p.sem()]

    em.op("sync", lambda e: e.dma_start(out=fsw[:], in_=fs_in), incs=[(lc, 16)])
    em.op("sync", lambda e: e.dma_start(out=flw[:], in_=fl_in), incs=[(lc, 16)])
    em.op("sync", lambda e: e.dma_start(out=ident[:], in_=ident_in), incs=[(lc, 16)])
    lc_all = em.n(lc)
    em.op("vector", lambda e: e.tensor_copy(identf[:], ident[:]), waits=[(lc, lc_all)])

    vfree = [(0, 0)] * 2
    for g in range(NCH):
        buf = g % 2
        vv, gg_ = vfree[buf]
        em.op("sync", lambda e, g=g, buf=buf: e.dma_start(out=vt[buf][:], in_=v_in[g]),
              waits=[(vec, vv), (gp, gg_)], incs=[(lv[buf], 16)])
        ldn = em.n(lv[buf])
        em.op("vector", lambda e, g=g, buf=buf: e.tensor_scalar_mul(
            out=accd[:], in0=vt[buf][:], scalar1=fsw[:, g * 5 + 4:g * 5 + 5]),
            waits=[(lv[buf], ldn), (stl, em.n(stl))])
        for s in range(1, 5):
            em.op("vector", lambda e, g=g, buf=buf, s=s: e.scalar_tensor_tensor(
                out=accd[:, s:L], in0=vt[buf][:, 0:L - s],
                scalar=fsw[:, g * 5 + 4 - s:g * 5 + 5 - s],
                in1=accd[:, s:L], op0=ALU.mult, op1=ALU.add))
        em.op("vector", lambda e: e.nop(), incs=[(vec, 1)])
        em.op("sync", lambda e, g=g: e.dma_start(out=ls_out[g], in_=accd[:]),
              waits=[(vec, em.n(vec))], incs=[(stl, 16)])
        em.op("vector", lambda e, g=g, buf=buf: e.tensor_scalar_mul(
            out=accl[:], in0=vt[buf][:], scalar1=flw[:, g * 64 + 63:g * 64 + 64]),
            waits=[(stll, em.n(stll))])
        em.op("vector", lambda e, g=g, buf=buf: e.tensor_scalar_mul(
            out=accp[:, 1:L], in0=vt[buf][:, 0:L - 1],
            scalar1=flw[:, g * 64 + 62:g * 64 + 63]),
            waits=[(lv[buf], ldn), (gp, gg_), (vec, em.n(vec))])
        for s in range(2, 64):
            if s % 2 == 0:
                em.op("vector", lambda e, g=g, buf=buf, s=s: e.scalar_tensor_tensor(
                    out=accl[:, s:L], in0=vt[buf][:, 0:L - s],
                    scalar=flw[:, g * 64 + 63 - s:g * 64 + 64 - s],
                    in1=accl[:, s:L], op0=ALU.mult, op1=ALU.add))
            else:
                em.op("vector", lambda e, g=g, buf=buf, s=s: e.scalar_tensor_tensor(
                    out=accp[:, s:L], in0=vt[buf][:, 0:L - s],
                    scalar=flw[:, g * 64 + 63 - s:g * 64 + 64 - s],
                    in1=accp[:, s:L], op0=ALU.mult, op1=ALU.add))
        em.op("vector", lambda e: e.nop(), incs=[(gp, 1)])
        gp_done = em.n(gp)
        em.op("vector", lambda e: e.tensor_add(accl[:, 1:L], accl[:, 1:L], accp[:, 1:L]),
              waits=[(gp, gp_done)])
        em.op("vector", lambda e: e.nop(), incs=[(vec, 1)])
        em.op("sync", lambda e, g=g: e.dma_start(out=ll_out[g], in_=accl[:]),
              waits=[(vec, em.n(vec))], incs=[(stll, 16)])
        vfree[buf] = (em.n(vec), em.n(gp))

    dfree = [0] * 2
    for h in range(4):
        for w in range(TT):
            k = h * TT + w
            buf = k % 2
            em.op("sync", lambda e, h=h, w=w, buf=buf: e.dma_start(
                out=dt_t[buf][:], in_=d_in[h][w]),
                waits=[(pe, dfree[buf])], incs=[(ldd[buf], 16)])
            ldn = em.n(ldd[buf])
            for c in range(2):
                j = k * 2 + c
                bb = j % 2
                em.op("tensor", lambda e, buf=buf, c=c, bb=bb: e.transpose(
                    ptr[bb][:], dt_t[buf][:, c * 128:(c + 1) * 128], identity=identf[:]),
                    waits=[(ldd[buf], ldn), (lc, lc_all), (vec, em.n(vec))],
                    incs=[(pe, 1)])
                em.op("vector", lambda e, bb=bb: e.tensor_copy(dfm[bb][:], ptr[bb][:]),
                      waits=[(pe, em.n(pe)), (std2[bb], 16 * (j // 2))],
                      incs=[(vec, 1)])
                g = 2 * h + c
                em.op("sync", lambda e, g=g, w=w, bb=bb: e.dma_start(
                    out=df_out[g][:, w * 128:(w + 1) * 128], in_=dfm[bb][:]),
                    waits=[(vec, em.n(vec))], incs=[(std2[bb], 16)])
            dfree[buf] = em.n(pe)

    em.op("sync", lambda e: e.wait_ge(stl, em.n(stl)))
    em.op("sync", lambda e: e.wait_ge(stll, em.n(stll)))
    em.op("sync", lambda e: e.wait_ge(std2[0], em.n(std2[0])))
    em.op("sync", lambda e: e.wait_ge(std2[1], em.n(std2[1])))
    return p.finish(sim_order=sim_order)



def _host_k3a_inputs(inputs):
    import ml_dtypes
    fs = inputs["L0_attn_fir_short_w"].reshape(D, 5)
    fl = inputs["L0_attn_fir_long_w"].reshape(D, 64)
    fsw = np.ascontiguousarray(fs.reshape(NCH, 128, 5).transpose(1, 0, 2).reshape(128, NCH * 5))
    flw = np.ascontiguousarray(fl.reshape(NCH, 128, 64).transpose(1, 0, 2).reshape(128, NCH * 64))
    ident = np.eye(128, dtype=ml_dtypes.bfloat16)
    return dict(fsw=fsw, flw=flw, ident=ident)



def build_k3b(sim_order=True):
    p = Prog()
    em = p.em
    ins = {n: p.din(n, [NCH, 128, L], F32)
           for n in ("ls_fm", "ll_fm", "delta_fm", "v_fm")}
    onesc_in = p.din("onesc", [128, 1], F32)
    st_out = p.dout("stats", [4, 16, L], F32)

    onesc = p.sb([128, 1])
    stg = [p.sb([128, 512]) for _ in range(2)]
    sqt = p.sb([128, 512])
    rowm = p.sb([1, 512])
    rownm = p.sb([1, 512])
    rowm2 = p.sb([1, 512])
    rowvar = p.sb([1, 512])
    rowl2 = p.sb([1, 512])
    rowam = p.sb([1, 512])
    pA = p.ps([1, 512])
    pB = p.ps([1, 512])
    pC = p.ps([1, 512])

    lc = p.sem()
    lv = [p.sem(), p.sem()]
    sca = p.sem()
    vec = p.sem()
    pe = p.sem()
    sts = p.sem()

    em.op("sync", lambda e: e.dma_start(out=onesc[:], in_=onesc_in), incs=[(lc, 16)])
    tnames = ("ls_fm", "ll_fm", "delta_fm", "v_fm")
    stg_free = [(0, 0, 0)] * 2
    for h in range(4):
        for ti, tn in enumerate(tnames):
            for t8 in range(8):
                sl = slice(t8 * 512, (t8 + 1) * 512)
                ldns = []
                for ci, g in enumerate((2 * h, 2 * h + 1)):
                    buf = ci
                    vv, ss, pp = stg_free[buf]
                    em.op("sync", lambda e, tn=tn, g=g, sl=sl, buf=buf: e.dma_start(
                        out=stg[buf][:], in_=ins[tn][g][:, sl]),
                        waits=[(vec, vv), (sca, ss), (pe, pp)], incs=[(lv[buf], 16)])
                    ldns.append(em.n(lv[buf]))
                for ci in range(2):
                    em.op("tensor", lambda e, ci=ci: e.matmul(
                        pA[:], onesc[:], stg[ci][:], start=(ci == 0), stop=(ci == 1)),
                        waits=([(lv[0], ldns[0]), (lv[1], ldns[1]), (lc, 16),
                                (vec, em.n(vec))] if ci == 0 else None),
                        incs=[(pe, 1)])
                for ci in range(2):
                    em.op("scalar", lambda e, ci=ci: e.activation(
                        out=sqt[:], in_=stg[ci][:], func=AF.Square),
                        waits=[(pe, em.n(pe))],
                        incs=[(sca, 1)])
                    em.op("tensor", lambda e, ci=ci: e.matmul(
                        pB[:], onesc[:], sqt[:], start=(ci == 0), stop=(ci == 1)),
                        waits=[(sca, em.n(sca))] + ([(vec, em.n(vec))] if ci == 0 else []),
                        incs=[(pe, 1)])
                for ci in range(2):
                    em.op("scalar", lambda e, ci=ci: e.activation(
                        out=sqt[:], in_=stg[ci][:], func=AF.Abs),
                        waits=[(pe, em.n(pe))], incs=[(sca, 1)])
                    em.op("tensor", lambda e, ci=ci: e.matmul(
                        pC[:], onesc[:], sqt[:], start=(ci == 0), stop=(ci == 1)),
                        waits=[(sca, em.n(sca))] + ([(vec, em.n(vec))] if ci == 0 else []),
                        incs=[(pe, 1)])
                stg_free[0] = (em.n(vec), em.n(sca), em.n(pe))
                stg_free[1] = stg_free[0]
                pe_done = em.n(pe)
                em.op("vector", lambda e: e.tensor_scalar_mul(
                    out=rowm[:], in0=pA[:], scalar1=1.0 / DH),
                    waits=[(pe, pe_done), (sts, em.n(sts))])
                em.op("vector", lambda e: e.tensor_scalar_mul(
                    out=rownm[:], in0=pA[:], scalar1=-1.0 / DH))
                em.op("vector", lambda e: e.tensor_mul(rowm2[:], rowm[:], rownm[:]))
                em.op("vector", lambda e: e.scalar_tensor_tensor(
                    out=rowvar[:], in0=pB[:], scalar=1.0 / DH, in1=rowm2[:],
                    op0=ALU.mult, op1=ALU.add))
                em.op("scalar", lambda e: e.activation(
                    out=rowl2[:], in_=pB[:], func=AF.Sqrt),
                    waits=[(pe, pe_done), (sts, em.n(sts))], incs=[(sca, 1)])
                em.op("vector", lambda e: e.tensor_scalar_mul(
                    out=rowam[:], in0=pC[:], scalar1=1.0 / DH), incs=[(vec, 1)])
                vd = em.n(vec)
                sd = em.n(sca)
                for si, row in ((0, rowm), (1, rowvar), (2, rowam), (3, rowl2)):
                    em.op("sync", lambda e, h=h, ti=ti, si=si, sl=sl, row=row:
                          e.dma_start(out=st_out[h][ti * 4 + si][sl], in_=row[:]),
                          waits=[(vec, vd), (sca, sd)], incs=[(sts, 16)])
    em.op("sync", lambda e: e.wait_ge(sts, em.n(sts)))
    return p.finish(sim_order=sim_order)



def build_k3c(sim_order=True):
    p = Prog()
    em = p.em
    xln_in = p.din("xln_fm", [NCH, 128, L], BF16)
    st_in = p.din("stats", [4, 16, L], F32)
    t_in = {n: p.din(n, [NCH, 128, L], F32)
            for n in ("ls_fm", "ll_fm", "delta_fm", "v_fm")}
    x_in = p.din("x_fm", [NCH, 128, L], F32)
    w1hs_in = p.din("w1hsT", [D, D], F32)
    w1st_in = p.din("w1stT", [16, D], F32)
    b1_in = p.din("b1r", [128, NCH], F32)
    w2_in = p.din("w2T", [128, 4 * NCH], F32)
    b2_in = p.din("b2", [1, 4], F32)
    lt_in = p.din("ltemp", [1, 1], F32)
    rs_in = p.din("ress", [1, 1], F32)
    rl_in = p.din("resl", [1, 1], F32)
    onw_in = p.din("onwr", [128, NCH], F32)
    opj_in = p.din("oprojT", [D, D], F32)
    one_in = p.din("cone", [1, 1], F32)
    onesr_in = p.din("onesr", [1, 128], F32)
    onesc_in = p.din("onesc", [128, 1], F32)
    identf_in = p.din("identf", [128, 128], F32)
    x1_out = p.dout("x1_fm", [NCH, 128, L], F32)

    w1hsb = [p.sb([128, D], BF16) for _ in range(NCH)]
    opjb = [p.sb([128, D], BF16) for _ in range(NCH)]
    w1stb = p.sb([16, D], BF16)
    w2s = p.sb([128, 4 * NCH])
    w2b = p.sb([128, 4 * NCH], BF16)
    b1r = p.sb([128, NCH])
    onwr = p.sb([128, NCH])
    onesr = p.sb([1, 128])
    onesc = p.sb([128, 1])
    identf = p.sb([128, 128])
    b2t = p.sb([1, 4])
    ltt = p.sb([1, 1])
    rst = p.sb([1, 1])
    rlt = p.sb([1, 1])
    onet = p.sb([1, 1])
    itcol = p.sb([128, 1])
    scr1 = p.sb([1, 1])
    scr2 = p.sb([1, 1])
    wstage = [p.sb([128, D]) for _ in range(2)]
    stg16 = p.sb([16, D])

    xlt = [p.sb([128, 512], BF16) for _ in range(NCH)]
    st4 = [p.sb([16, 512]) for _ in range(4)]
    st4b = [p.sb([16, 512], BF16) for _ in range(4)]
    hdn = [[p.sb([128, 512], BF16) for _ in range(4)] for _ in range(NCH)]
    base_sb = p.sb([128, 512])
    t1 = p.sb([128, 512])
    lgt = p.sb([128, 4])
    b2c = p.sb([128, 4])
    rm = p.sb([128, 1])
    es = p.sb([128, 4])
    ssum = p.sb([128, 1])
    fwr = [[p.sb([1, 512]) for _ in range(4)] for _ in range(4)]
    rr0 = p.sb([1, 512])
    rr1 = p.sb([1, 512])
    mixin = {n: p.sb([128, 512]) for n in t_in}
    ot = [p.sb([128, 512]) for _ in range(NCH)]
    obf = [p.sb([128, 512], BF16) for _ in range(NCH)]
    sqs = p.sb([128, 512])
    nrow = p.sb([1, 512])
    xr = p.sb([128, 512])
    x1t = p.sb([128, 512])

    P1 = p.ps([128, 512])
    P2 = p.ps([128, 512])
    PL = [p.ps([128, 512]) for _ in range(4)]
    Pfw = p.ps([4, 128])
    Pb2 = p.ps([128, 4])

    lc = p.sem()
    lw2 = [p.sem(), p.sem()]
    lx = p.sem()
    lsts = p.sem()
    lmx = p.sem()
    lxr = p.sem()
    sca = p.sem()
    vec = p.sem()
    pe = p.sem()
    st = p.sem()

    for ap_, t in ((b1_in, b1r), (onw_in, onwr), (b2_in, b2t),
                   (lt_in, ltt), (rs_in, rst), (rl_in, rlt), (one_in, onet),
                   (onesr_in, onesr), (onesc_in, onesc), (identf_in, identf),
                   (w1st_in, stg16), (w2_in, w2s)):
        em.op("sync", lambda e, ap_=ap_, t=t: e.dma_start(out=t[:], in_=ap_),
              incs=[(lc, 16)])
    lc_all = em.n(lc)
    em.op("scalar", lambda e: e.copy(out=w1stb[:], in_=stg16[:]),
          waits=[(lc, lc_all)], incs=[(sca, 1)])
    em.op("scalar", lambda e: e.copy(out=w2b[:], in_=w2s[:]),
          waits=[(lc, lc_all)], incs=[(sca, 1)])
    wfree = [0, 0]
    for wi, (src_ap, dst) in enumerate(((w1hs_in, w1hsb), (opj_in, opjb))):
        for g in range(NCH):
            k = wi * NCH + g
            buf = k % 2
            w = [(sca, wfree[buf])] if wfree[buf] else []
            em.op("sync", lambda e, src_ap=src_ap, g=g, buf=buf: e.dma_start(
                out=wstage[buf][:], in_=src_ap[g * 128:(g + 1) * 128, :]),
                waits=w, incs=[(lw2[buf], 16)])
            em.op("scalar", lambda e, dst=dst, g=g, buf=buf: e.copy(
                out=dst[g][:], in_=wstage[buf][:]),
                waits=[(lw2[buf], em.n(lw2[buf]))], incs=[(sca, 1)])
            wfree[buf] = em.n(sca)
    wconv_done = em.n(sca)

    em.op("scalar", lambda e: e.activation(out=scr1[:], in_=ltt[:], func=AF.Abs),
          waits=[(lc, lc_all)], incs=[(sca, 1)])
    em.op("scalar", lambda e: e.activation(out=scr1[:], in_=scr1[:], func=AF.Exp,
                                           scale=-1.0), incs=[(sca, 1)])
    em.op("vector", lambda e: e.tensor_add(scr1[:], scr1[:], onet[:]),
          waits=[(sca, em.n(sca))], incs=[(vec, 1)])
    em.op("scalar", lambda e: e.activation(out=scr1[:], in_=scr1[:], func=AF.Ln),
          waits=[(vec, em.n(vec))], incs=[(sca, 1)])
    em.op("vector", lambda e: e.tensor_relu(out=scr2[:], in_=ltt[:]),
          waits=[(sca, em.n(sca))])
    em.op("vector", lambda e: e.tensor_add(scr1[:], scr1[:], scr2[:]))
    em.op("vector", lambda e: e.tensor_scalar_add(out=scr1[:], in0=scr1[:],
                                                  scalar1=1e-4))
    em.op("vector", lambda e: e.nop())
    em.op("vector", lambda e: e.reciprocal(out=scr1[:], in_=scr1[:]),
          incs=[(vec, 1)])
    em.op("tensor", lambda e: e.matmul(Pfw[0:1, 0:1], onet[:], scr1[:],
                                       start=True, stop=True),
          waits=[(vec, em.n(vec)), (lc, lc_all)], incs=[(pe, 1)])
    em.op("tensor", lambda e: e.matmul(P1[:, 0:1], onesr[:], scr1[:],
                                       start=True, stop=True),
          incs=[(pe, 1)])
    em.op("vector", lambda e: e.tensor_copy(itcol[:], P1[:, 0:1]),
          waits=[(pe, em.n(pe))], incs=[(vec, 1)])

    xl_free = 0
    mix_free = 0
    for t8 in range(8):
        sl = slice(t8 * 512, (t8 + 1) * 512)
        for g in range(NCH):
            em.op("sync", lambda e, g=g, sl=sl: e.dma_start(
                out=xlt[g][:], in_=xln_in[g][:, sl]),
                waits=[(pe, xl_free)], incs=[(lx, 16)])
        lx_all = em.n(lx)
        for h in range(4):
            em.op("sync", lambda e, h=h, sl=sl: e.dma_start(
                out=st4[h][:], in_=st_in[h][:, sl]),
                waits=[(sca, em.n(sca))], incs=[(lsts, 16)])
        lsts_all = em.n(lsts)
        for h in range(4):
            em.op("scalar", lambda e, h=h: e.copy(out=st4b[h][:], in_=st4[h][:]),
                  waits=[(lsts, lsts_all), (pe, em.n(pe))], incs=[(sca, 1)])
        stb_done = em.n(sca)
        for gc in range(NCH):
            for gd in range(NCH):
                em.op("tensor", lambda e, gc=gc, gd=gd: e.matmul(
                    P1[:], w1hsb[gd][:, gc * 128:(gc + 1) * 128], xlt[gd][:],
                    start=(gd == 0), stop=(gd == NCH - 1)),
                    waits=([(lx, lx_all), (sca, wconv_done), (vec, em.n(vec))]
                           if gd == 0 else None),
                    incs=([(pe, 1)] if gd == NCH - 1 else None))
            em.op("vector", lambda e: e.tensor_copy(base_sb[:], P1[:]),
                  waits=[(pe, em.n(pe))], incs=[(vec, 1)])
            for h in range(4):
                em.op("tensor", lambda e, gc=gc, h=h: e.matmul(
                    P2[:], w1stb[:, gc * 128:(gc + 1) * 128], st4b[h][:],
                    start=True, stop=True),
                    waits=[(sca, stb_done), (vec, em.n(vec))], incs=[(pe, 1)])
                em.op("vector", lambda e: e.scalar_tensor_tensor(
                    out=t1[:], in0=P2[:], scalar=1.0, in1=base_sb[:],
                    op0=ALU.mult, op1=ALU.add),
                    waits=[(pe, em.n(pe)), (sca, em.n(sca))], incs=[(vec, 1)])
                em.op("scalar", lambda e, gc=gc, h=h: e.activation(
                    out=hdn[gc][h][:], in_=t1[:], func=AF.Gelu,
                    bias=b1r[:, gc:gc + 1]),
                    waits=[(vec, em.n(vec))], incs=[(sca, 1)])
        hdn_done = em.n(sca)
        xl_free = em.n(pe)
        for tq in range(4):
            for h in range(4):
                for gc in range(NCH):
                    em.op("tensor", lambda e, tq=tq, h=h, gc=gc: e.matmul(
                        PL[tq][:, h * 4:(h + 1) * 4],
                        hdn[gc][h][:, tq * 128:(tq + 1) * 128],
                        w2b[:, gc * 4:(gc + 1) * 4],
                        start=(gc == 0), stop=(gc == NCH - 1)),
                        waits=([(sca, hdn_done), (vec, em.n(vec))]
                               if gc == 0 else None),
                        incs=([(pe, 1)] if gc == NCH - 1 else None))
        em.op("tensor", lambda e: e.matmul(Pb2[:], onesr[:], b2t[:],
                                           start=True, stop=True),
              incs=[(pe, 1)])
        em.op("vector", lambda e: e.tensor_copy(b2c[:], Pb2[:]),
              waits=[(pe, em.n(pe))], incs=[(vec, 1)])
        b2bc_done = em.n(vec)
        for tq in range(4):
            for h in range(4):
                em.op("vector", lambda e, tq=tq, h=h: e.tensor_add(
                    lgt[:], PL[tq][:, h * 4:(h + 1) * 4], b2c[:]),
                    waits=[(vec, b2bc_done)])
                em.op("vector", lambda e: e.tensor_reduce(
                    out=rm[:], in_=lgt[:], axis=AX.X, op=ALU.max))
                em.op("vector", lambda e: e.tensor_scalar_mul(
                    out=rm[:], in0=rm[:], scalar1=-1.0))
                em.op("vector", lambda e: e.tensor_scalar(
                    out=lgt[:], in0=lgt[:], scalar1=rm[:, 0:1],
                    scalar2=itcol[:, 0:1], op0=ALU.add, op1=ALU.mult),
                    incs=[(vec, 1)])
                em.op("scalar", lambda e: e.activation(
                    out=es[:], in_=lgt[:], func=AF.Exp),
                    waits=[(vec, em.n(vec)), (pe, em.n(pe))], incs=[(sca, 1)])
                em.op("vector", lambda e: e.tensor_reduce(
                    out=ssum[:], in_=es[:], axis=AX.X, op=ALU.add),
                    waits=[(sca, em.n(sca))])
                em.op("vector", lambda e: e.nop())
                em.op("vector", lambda e: e.reciprocal(out=ssum[:], in_=ssum[:]))
                em.op("vector", lambda e: e.tensor_scalar_mul(
                    out=es[:], in0=es[:], scalar1=ssum[:, 0:1]), incs=[(vec, 1)])
                for j in range(4):
                    em.op("tensor", lambda e, j=j: e.transpose(
                        Pfw[0:1, :], es[:, j:j + 1], identity=identf[:]),
                        waits=[(vec, em.n(vec)), (lc, lc_all)], incs=[(pe, 1)])
                    em.op("vector", lambda e, tq=tq, h=h, j=j: e.tensor_copy(
                        fwr[h][j][:, tq * 128:(tq + 1) * 128], Pfw[0:1, :]),
                        waits=[(pe, em.n(pe))], incs=[(vec, 1)])
        fw_done = em.n(vec)
        for h in range(4):
            em.op("vector", lambda e, h=h: e.tensor_scalar_add(
                out=rr0[:], in0=fwr[h][0][:], scalar1=rst[0:1, 0:1]))
            em.op("vector", lambda e, h=h: e.tensor_scalar_add(
                out=rr1[:], in0=fwr[h][1][:], scalar1=rlt[0:1, 0:1]),
                incs=[(vec, 1)])
            rows_done = em.n(vec)
            rows = [rr0[:], rr1[:], fwr[h][2][:], fwr[h][3][:]]
            for j in range(4):
                em.op("tensor", lambda e, j=j, rap=rows[j]: e.matmul(
                    PL[j][:], onesr[:], rap, start=True, stop=True),
                    waits=[(vec, rows_done)], incs=[(pe, 1)])
            bc_done = em.n(pe)
            for ci, g in enumerate((2 * h, 2 * h + 1)):
                for ni, n in enumerate(("ls_fm", "ll_fm", "delta_fm", "v_fm")):
                    em.op("sync", lambda e, n=n, g=g, sl=sl: e.dma_start(
                        out=mixin[n][:], in_=t_in[n][g][:, sl]),
                        waits=[(vec, mix_free)], incs=[(lmx, 16)])
                    lm = em.n(lmx)
                    if ni == 0:
                        em.op("vector", lambda e, n=n, g=g: e.tensor_mul(
                            ot[g][:], mixin[n][:], PL[0][:]),
                            waits=[(lmx, lm), (pe, bc_done)], incs=[(vec, 1)])
                    else:
                        em.op("vector", lambda e, n=n, ni=ni: e.tensor_mul(
                            t1[:], mixin[n][:], PL[ni][:]),
                            waits=[(lmx, lm), (sca, em.n(sca))], incs=[(vec, 1)])
                        em.op("vector", lambda e, g=g: e.tensor_add(
                            ot[g][:], ot[g][:], t1[:]), incs=[(vec, 1)])
                    mix_free = em.n(vec)
            for ci, g in enumerate((2 * h, 2 * h + 1)):
                em.op("scalar", lambda e, g=g: e.activation(
                    out=sqs[:], in_=ot[g][:], func=AF.Square),
                    waits=[(vec, em.n(vec)), (pe, em.n(pe))], incs=[(sca, 1)])
                em.op("tensor", lambda e, ci=ci: e.matmul(
                    P2[0:1, :], onesc[:], sqs[:],
                    start=(ci == 0), stop=(ci == 1)),
                    waits=[(sca, em.n(sca))],
                    incs=[(pe, 1)])
            em.op("vector", lambda e: e.tensor_scalar(
                out=nrow[:], in0=P2[0:1, :], scalar1=1.0 / DH, scalar2=1e-5,
                op0=ALU.mult, op1=ALU.add),
                waits=[(pe, em.n(pe))], incs=[(vec, 1)])
            em.op("scalar", lambda e: e.activation(
                out=nrow[:], in_=nrow[:], func=AF.Sqrt),
                waits=[(vec, em.n(vec))], incs=[(sca, 1)])
            em.op("vector", lambda e: e.nop(), waits=[(sca, em.n(sca))])
            em.op("vector", lambda e: e.reciprocal(out=nrow[:], in_=nrow[:]),
                  incs=[(vec, 1)])
            em.op("tensor", lambda e: e.matmul(P1[:], onesr[:], nrow[:],
                                               start=True, stop=True),
                  waits=[(vec, em.n(vec))], incs=[(pe, 1)])
            nbc = em.n(pe)
            for g in (2 * h, 2 * h + 1):
                em.op("vector", lambda e, g=g: e.tensor_mul(
                    t1[:], ot[g][:], P1[:]), waits=[(pe, nbc)])
                em.op("vector", lambda e, g=g: e.tensor_scalar_mul(
                    out=obf[g][:], in0=t1[:], scalar1=onwr[:, g:g + 1]),
                    incs=[(vec, 1)])
        ob_done = em.n(vec)
        for co in range(NCH):
            for gd in range(NCH):
                em.op("tensor", lambda e, co=co, gd=gd: e.matmul(
                    P2[:], opjb[gd][:, co * 128:(co + 1) * 128], obf[gd][:],
                    start=(gd == 0), stop=(gd == NCH - 1)),
                    waits=([(vec, em.n(vec))] if gd == 0 else None),
                    incs=([(pe, 1)] if gd == NCH - 1 else None))
            em.op("sync", lambda e, co=co, sl=sl: e.dma_start(
                out=xr[:], in_=x_in[co][:, sl]),
                waits=[(vec, em.n(vec))], incs=[(lxr, 16)])
            em.op("vector", lambda e: e.scalar_tensor_tensor(
                out=x1t[:], in0=P2[:], scalar=1.0, in1=xr[:],
                op0=ALU.mult, op1=ALU.add),
                waits=[(pe, em.n(pe)), (lxr, em.n(lxr)), (st, em.n(st))],
                incs=[(vec, 1)])
            em.op("sync", lambda e, co=co, sl=sl: e.dma_start(
                out=x1_out[co][:, sl], in_=x1t[:]),
                waits=[(vec, em.n(vec))], incs=[(st, 16)])
    em.op("sync", lambda e: e.wait_ge(st, em.n(st)))
    return p.finish(sim_order=sim_order)



def _host_k3c_inputs(inputs):
    import ml_dtypes
    fw1 = inputs["L0_attn_fuse_w1"]
    w1hsT = np.ascontiguousarray(fw1[:, :D].T)
    w1stT = np.ascontiguousarray(fw1[:, D:].T)
    b1 = inputs["L0_attn_fuse_b1"]
    b1r = np.ascontiguousarray(b1.reshape(NCH, 128).T)
    w2 = inputs["L0_attn_fuse_w2"]
    w2T = np.ascontiguousarray(
        w2.T.reshape(NCH, 128, 4).transpose(1, 0, 2).reshape(128, NCH * 4))
    onw = inputs["L0_attn_o_norm_w"]
    onw_full = np.tile(onw, 4)
    onwr = np.ascontiguousarray(onw_full.reshape(NCH, 128).T)
    oprojT = np.ascontiguousarray(inputs["L0_attn_o_proj_w"].T)
    return dict(
        w1hsT=w1hsT, w1stT=w1stT, b1r=b1r, w2T=w2T,
        b2=inputs["L0_attn_fuse_b2"].reshape(1, 4).astype(np.float32),
        ltemp=inputs["L0_attn_logit_temp"].reshape(1, 1).astype(np.float32),
        ress=inputs["L0_attn_res_scale_short"].reshape(1, 1).astype(np.float32),
        resl=inputs["L0_attn_res_scale_long"].reshape(1, 1).astype(np.float32),
        onwr=onwr, oprojT=oprojT,
        cone=np.ones((1, 1), np.float32),
        onesr=np.ones((1, 128), np.float32),
        onesc=np.ones((128, 1), np.float32),
        identf=np.eye(128, dtype=np.float32),
    )



def _emit_ln_fm(p, em, src_getter, xout_tiles, sems, nchunks=NCH, dim=D):
    raise NotImplementedError


def build_k4a(sim_order=True):
    p = Prog()
    em = p.em
    x_in = p.din("x1_fm", [NCH, 128, L], F32)
    wg_in = p.din("wgr", [128, 8 * NCH], F32)
    onesc_in = p.din("onesc", [128, 1], F32)
    onesr_in = p.din("onesr", [1, 128], F32)
    xln_out = p.dout("xln1_fm", [NCH, 128, L], BF16)
    probs_out = p.dout("probs", [L, 8], F32)

    onesc = p.sb([128, 1])
    onesr = p.sb([1, 128])
    wgs = p.sb([128, 8 * NCH])
    wgb = p.sb([128, 8 * NCH], BF16)
    xt = [p.sb([128, 512]) for _ in range(NCH)]
    xlb = [p.sb([128, 512], BF16) for _ in range(NCH)]
    sqs = p.sb([128, 512])
    nmr = p.sb([1, 512])
    mr = p.sb([1, 512])
    m2r = p.sb([1, 512])
    vr = p.sb([1, 512])
    rsr = p.sb([1, 512])
    pt = p.sb([128, 8])
    rmx = p.sb([128, 1])
    esx = p.sb([128, 8])
    ssx = p.sb([128, 1])
    P1 = p.ps([128, 512])
    P2 = p.ps([128, 512])
    P3 = p.ps([128, 512])
    P4 = p.ps([128, 512])
    PP = p.ps([128, 8])

    lc = p.sem()
    lx = p.sem()
    sca = p.sem()
    vec = p.sem()
    pe = p.sem()
    st = p.sem()
    stp = p.sem()

    em.op("sync", lambda e: e.dma_start(out=onesc[:], in_=onesc_in), incs=[(lc, 16)])
    em.op("sync", lambda e: e.dma_start(out=onesr[:], in_=onesr_in), incs=[(lc, 16)])
    em.op("sync", lambda e: e.dma_start(out=wgs[:], in_=wg_in), incs=[(lc, 16)])
    lc_all = em.n(lc)
    em.op("scalar", lambda e: e.copy(out=wgb[:], in_=wgs[:]),
          waits=[(lc, lc_all)], incs=[(sca, 1)])

    x_free = 0
    for t8 in range(8):
        sl = slice(t8 * 512, (t8 + 1) * 512)
        for g in range(NCH):
            em.op("sync", lambda e, g=g, sl=sl: e.dma_start(
                out=xt[g][:], in_=x_in[g][:, sl]),
                waits=[(pe, x_free)], incs=[(lx, 16)])
        lx_all = em.n(lx)
        for g in range(NCH):
            em.op("tensor", lambda e, g=g: e.matmul(
                P1[0:1, :], onesc[:], xt[g][:], start=(g == 0), stop=(g == NCH - 1)),
                waits=([(lx, lx_all), (lc, lc_all), (vec, em.n(vec))] if g == 0 else None),
                incs=[(pe, 1)])
        for g in range(NCH):
            em.op("scalar", lambda e, g=g: e.activation(
                out=sqs[:], in_=xt[g][:], func=AF.Square),
                waits=[(pe, em.n(pe))], incs=[(sca, 1)])
            em.op("tensor", lambda e, g=g: e.matmul(
                P2[0:1, :], onesc[:], sqs[:], start=(g == 0), stop=(g == NCH - 1)),
                waits=[(sca, em.n(sca))] + ([(vec, em.n(vec))] if g == 0 else []),
                incs=[(pe, 1)])
        pe_done = em.n(pe)
        em.op("vector", lambda e: e.tensor_scalar_mul(
            out=nmr[:], in0=P1[0:1, :], scalar1=-1.0 / D), waits=[(pe, pe_done)])
        em.op("vector", lambda e: e.tensor_scalar_mul(
            out=mr[:], in0=P1[0:1, :], scalar1=1.0 / D))
        em.op("vector", lambda e: e.tensor_mul(m2r[:], mr[:], nmr[:]))
        em.op("vector", lambda e: e.scalar_tensor_tensor(
            out=vr[:], in0=P2[0:1, :], scalar=1.0 / D, in1=m2r[:],
            op0=ALU.mult, op1=ALU.add))
        em.op("vector", lambda e: e.tensor_scalar_add(out=vr[:], in0=vr[:],
                                                      scalar1=1e-5), incs=[(vec, 1)])
        em.op("scalar", lambda e: e.activation(out=rsr[:], in_=vr[:], func=AF.Sqrt),
              waits=[(vec, em.n(vec))], incs=[(sca, 1)])
        em.op("vector", lambda e: e.nop(), waits=[(sca, em.n(sca))])
        em.op("vector", lambda e: e.reciprocal(out=rsr[:], in_=rsr[:]),
              incs=[(vec, 1)])
        r_done = em.n(vec)
        em.op("tensor", lambda e: e.matmul(P3[:], onesr[:], nmr[:],
                                           start=True, stop=True),
              waits=[(vec, r_done)], incs=[(pe, 1)])
        em.op("tensor", lambda e: e.matmul(P4[:], onesr[:], rsr[:],
                                           start=True, stop=True), incs=[(pe, 1)])
        bc_done = em.n(pe)
        for g in range(NCH):
            em.op("vector", lambda e, g=g: e.tensor_add(sqs[:], xt[g][:], P3[:]),
                  waits=[(pe, bc_done)] + ([(st, em.n(st))] if em.n(st) else []))
            em.op("vector", lambda e, g=g: e.tensor_mul(xlb[g][:], sqs[:], P4[:]),
                  incs=[(vec, 1)])
            em.op("sync", lambda e, g=g, sl=sl: e.dma_start(
                out=xln_out[g][:, sl], in_=xlb[g][:]),
                waits=[(vec, em.n(vec))], incs=[(st, 16)])
        x_free = em.n(pe)
        xl_done = em.n(vec)
        for tq in range(4):
            for g in range(NCH):
                em.op("tensor", lambda e, g=g, tq=tq: e.matmul(
                    PP[:], xlb[g][:, tq * 128:(tq + 1) * 128],
                    wgb[:, g * 8:(g + 1) * 8],
                    start=(g == 0), stop=(g == NCH - 1)),
                    waits=([(vec, em.n(vec)), (sca, 1)] if g == 0 else None),
                    incs=([(pe, 1)] if g == NCH - 1 else None))
            em.op("vector", lambda e: e.tensor_copy(pt[:], PP[:]),
                  waits=[(pe, em.n(pe)), (stp, em.n(stp))])
            em.op("vector", lambda e: e.tensor_reduce(
                out=rmx[:], in_=pt[:], axis=AX.X, op=ALU.max))
            em.op("vector", lambda e: e.tensor_scalar_mul(
                out=rmx[:], in0=rmx[:], scalar1=-1.0))
            em.op("vector", lambda e: e.tensor_scalar_add(
                out=pt[:], in0=pt[:], scalar1=rmx[:, 0:1]), incs=[(vec, 1)])
            em.op("scalar", lambda e: e.activation(out=pt[:], in_=pt[:], func=AF.Exp),
                  waits=[(vec, em.n(vec))], incs=[(sca, 1)])
            em.op("vector", lambda e: e.tensor_reduce(
                out=ssx[:], in_=pt[:], axis=AX.X, op=ALU.add),
                waits=[(sca, em.n(sca))])
            em.op("vector", lambda e: e.nop())
            em.op("vector", lambda e: e.reciprocal(out=ssx[:], in_=ssx[:]))
            em.op("vector", lambda e: e.tensor_scalar_mul(
                out=pt[:], in0=pt[:], scalar1=ssx[:, 0:1]), incs=[(vec, 1)])
            em.op("sync", lambda e, t8=t8, tq=tq: e.dma_start(
                out=probs_out[t8 * 512 + tq * 128:t8 * 512 + (tq + 1) * 128, :],
                in_=pt[:]),
                waits=[(vec, em.n(vec))], incs=[(stp, 16)])
    em.op("sync", lambda e: e.wait_ge(st, em.n(st)))
    em.op("sync", lambda e: e.wait_ge(stp, em.n(stp)))
    return p.finish(sim_order=sim_order)



def build_k4b(sim_order=True, elide=True):
    p = Prog()
    em = p.em
    nc = p.nc
    x1_in = p.din("x1_fm", [NCH, 128, L], F32)
    xln_in = p.din("xln1_fm", [NCH, 128, L], BF16)
    pr_in = p.din("probs_r", [128, 2048], F32)
    own_in = p.din("own_p", [L, 8], F32)
    w1_in = p.din("w1f", [8 * D, 2048], F32)
    w2_in = p.din("w2f", [8 * 2048, D], F32)
    em_in = p.din("emask", [128, 8], F32)
    emt_in = p.din("emtT", [8, 128], F32)
    onesr_in = p.din("onesr", [1, 128], F32)
    identf_in = p.din("identf", [128, 128], F32)
    x2_out = p.dout("x2_fm", [NCH, 128, L], F32)
    gates_d = nc.dram_tensor("gates_d", [8, L], F32, kind="Internal").ap()
    th_d = nc.dram_tensor("th_d", [1, 8], F32, kind="Internal").ap()

    emask = p.sb([128, 8])
    emtT = p.sb([8, 128])
    onesr = p.sb([1, 128])
    identf = p.sb([128, 128])
    prt = p.sb([128, 2048])
    lo = p.sb([128, 1]); hi = p.sb([128, 1]); mid = p.sb([128, 1])
    ind = p.sb([128, 2048]); part = p.sb([128, 1]); cnt8 = p.sb([8, 1])
    ge = p.sb([128, 1])
    thbc = p.sb([128, 8]); ownp = p.sb([128, 8]); selt = p.sb([128, 8])
    mskd = p.sb([128, 8]); neg = p.sb([128, 8])
    mval = p.sb([128, 1]); asg = p.sb([128, 1]); pbst = p.sb([128, 1])
    wts = p.sb([128, 1]); idx = p.sb([128, 1]); eq = p.sb([128, 1])
    tsc = p.sb([128, 1])
    econ = [p.sb([128, 1]) for _ in range(8)]
    gts = p.sb([128, 8]); gtr = p.sb([8, 128])
    Pc = p.ps([8, 1]); Pb = p.ps([128, 1]); Pt = p.ps([8, 128])

    lc = p.sem(); lown = p.sem(); sca = p.sem(); vec = p.sem(); pe = p.sem()
    stg_s = p.sem(); sx2 = p.sem(); lth = p.sem()

    for ap_, t in ((em_in, emask), (emt_in, emtT), (onesr_in, onesr),
                   (identf_in, identf)):
        em.op("sync", lambda e, ap_=ap_, t=t: e.dma_start(out=t[:], in_=ap_),
              incs=[(lc, 16)])
    em.op("sync", lambda e: e.dma_start(out=prt[:], in_=pr_in), incs=[(lc, 16)])
    lc_all = em.n(lc)
    for g in range(NCH):
        em.op("sync", lambda e, g=g: e.dma_start(out=x2_out[g], in_=x1_in[g]),
              incs=[(sx2, 16)])
    x2init = em.n(sx2)

    em.op("vector", lambda e: e.memset(lo[:], 0.0), waits=[(lc, lc_all)])
    em.op("vector", lambda e: e.memset(hi[:], 1.0))
    em.op("vector", lambda e: e.memset(neg[:], -1e30))
    for e_ in range(8):
        em.op("vector", lambda e, e_=e_: e.memset(econ[e_][:], float(e_)))
    for it in range(48):
        em.op("vector", lambda e: e.tensor_add(mid[:], lo[:], hi[:]))
        em.op("vector", lambda e: e.tensor_scalar_mul(out=mid[:], in0=mid[:],
                                                      scalar1=0.5))
        em.op("vector", lambda e: e.tensor_single_scalar(
            out=ind[:], in_=prt[:], scalar=mid[:, 0:1], op=ALU.is_ge))
        em.op("vector", lambda e: e.tensor_reduce(
            out=part[:], in_=ind[:], axis=AX.X, op=ALU.add), incs=[(vec, 1)])
        em.op("tensor", lambda e: e.matmul(Pc[:], emask[:], part[:],
                                           start=True, stop=True),
              waits=[(vec, em.n(vec)), (lc, lc_all)], incs=[(pe, 1)])
        em.op("vector", lambda e: e.tensor_copy(cnt8[:], Pc[:]),
              waits=[(pe, em.n(pe))], incs=[(vec, 1)])
        em.op("tensor", lambda e: e.matmul(Pb[:], emtT[:], cnt8[:],
                                           start=True, stop=True),
              waits=[(vec, em.n(vec))], incs=[(pe, 1)])
        em.op("vector", lambda e: e.tensor_single_scalar(
            out=ge[:], in_=Pb[:], scalar=float(L), op=ALU.is_ge),
            waits=[(pe, em.n(pe))])
        em.op("vector", lambda e: e.tensor_sub(tsc[:], mid[:], lo[:]))
        em.op("vector", lambda e: e.scalar_tensor_tensor(
            out=lo[:], in0=tsc[:], scalar=ge[:, 0:1], in1=lo[:],
            op0=ALU.mult, op1=ALU.add))
        em.op("vector", lambda e: e.tensor_sub(tsc[:], hi[:], mid[:]))
        em.op("vector", lambda e: e.scalar_tensor_tensor(
            out=hi[:], in0=tsc[:], scalar=ge[:, 0:1], in1=mid[:],
            op0=ALU.mult, op1=ALU.add))
    em.op("vector", lambda e: e.nop(), incs=[(vec, 1)])
    bis_done = em.n(vec)
    em.op("sync", lambda e: e.dma_start(
        out=th_d, in_=bass.AP(lo.tensor if hasattr(lo, "tensor") else lo,
                              0, [[16, 8], [1, 1]])) if False else
        e.dma_start(out=th_d, in_=lo[0:128:16, 0:1]),
        waits=[(vec, bis_done)], incs=[(lth, 16)])
    em.op("sync", lambda e: e.dma_start(
        out=thbc[:], in_=bass.AP(th_d.tensor, 0, [[0, 128], [1, 8]])),
        waits=[(lth, em.n(lth))], incs=[(lth, 16)])
    th_ld = em.n(lth)

    gfree = 0
    for ti in range(TT):
        em.op("sync", lambda e, ti=ti: e.dma_start(
            out=ownp[:], in_=own_in[ti * 128:(ti + 1) * 128, :]),
            waits=[(vec, gfree)], incs=[(lown, 16)])
        lo_n = em.n(lown)
        em.op("vector", lambda e: e.tensor_tensor(
            out=selt[:], in0=ownp[:], in1=thbc[:], op=ALU.is_ge),
            waits=[(lown, lo_n), (lth, th_ld)])
        em.op("vector", lambda e: e.tensor_mul(mskd[:], ownp[:], selt[:]))
        em.op("vector", lambda e: e.tensor_scalar_add(out=neg[:], in0=selt[:],
                                                      scalar1=-1.0))
        em.op("vector", lambda e: e.tensor_add(mskd[:], mskd[:], neg[:]))
        em.op("vector", lambda e: e.tensor_reduce(
            out=mval[:], in_=mskd[:], axis=AX.X, op=ALU.max))
        em.op("vector", lambda e: e.tensor_reduce(
            out=asg[:], in_=selt[:], axis=AX.X, op=ALU.max))
        em.op("vector", lambda e: e.tensor_reduce(
            out=pbst[:], in_=ownp[:], axis=AX.X, op=ALU.max))
        em.op("vector", lambda e: e.tensor_mul(wts[:], mval[:], asg[:]))
        em.op("vector", lambda e: e.tensor_scalar(
            out=tsc[:], in0=asg[:], scalar1=-1.0, scalar2=1.0,
            op0=ALU.mult, op1=ALU.add))
        em.op("vector", lambda e: e.tensor_mul(tsc[:], pbst[:], tsc[:]))
        em.op("vector", lambda e: e.tensor_add(wts[:], wts[:], tsc[:]))
        em.op("vector", lambda e: e.memset(idx[:], 0.0))
        for e_ in range(7, -1, -1):
            em.op("vector", lambda e, e_=e_: e.tensor_single_scalar(
                out=eq[:], in_=ownp[:, e_:e_ + 1], scalar=wts[:, 0:1],
                op=ALU.is_ge))
            em.op("vector", lambda e, e_=e_: e.tensor_sub(
                tsc[:], econ[e_][:], idx[:]))
            em.op("vector", lambda e: e.scalar_tensor_tensor(
                out=idx[:], in0=tsc[:], scalar=eq[:, 0:1], in1=idx[:],
                op0=ALU.mult, op1=ALU.add))
        for e_ in range(8):
            em.op("vector", lambda e, e_=e_: e.tensor_single_scalar(
                out=eq[:], in_=idx[:], scalar=float(e_), op=ALU.is_equal))
            em.op("vector", lambda e, e_=e_: e.tensor_mul(
                gts[:, e_:e_ + 1], eq[:], wts[:]))
        em.op("vector", lambda e: e.nop(), incs=[(vec, 1)])
        em.op("tensor", lambda e: e.transpose(Pt[:], gts[:], identity=identf[:]),
              waits=[(vec, em.n(vec)), (stg_s, em.n(stg_s))], incs=[(pe, 1)])
        em.op("vector", lambda e: e.tensor_copy(gtr[:], Pt[:]),
              waits=[(pe, em.n(pe))], incs=[(vec, 1)])
        gfree = em.n(vec)
        em.op("sync", lambda e, ti=ti: e.dma_start(
            out=gates_d[:, ti * 128:(ti + 1) * 128], in_=gtr[:]),
            waits=[(vec, gfree)], incs=[(stg_s, 16)])
    route_done = em.n(stg_s)

    lem = LoopEmitter(8, self_wait=("sync", "gpsimd"), elide=elide)
    grow = p.sb([1, L])
    wst = p.sb([128, 2048])
    wst2b = p.sb([128, 2048])
    W1b = [p.sb([128, 2048], BF16) for _ in range(NCH)]
    W2b = [p.sb([128, 2048], BF16) for _ in range(8)]
    xlt = [p.sb([128, 512], BF16) for _ in range(NCH)]
    h1b = [p.sb([128, 512], BF16) for _ in range(16)]
    dsb2 = [p.sb([128, 512], BF16) for _ in range(2)]
    gt2 = [p.sb([128, 512]) for _ in range(2)]
    PU2 = [p.ps([128, 512]) for _ in range(2)]
    PD2 = [p.ps([128, 512]) for _ in range(2)]
    PG = p.ps([128, 512])

    hg = lem.op("sync", lambda e, i: e.dma_start(
        out=grow[:], in_=gates_d[bass.ts(i, 1), :]))
    wconv = []
    for gd in range(NCH):
        hld = lem.op("sync", lambda e, i, gd=gd: e.dma_start(
            out=wst[:, gd * 256:(gd + 1) * 256],
            in_=w1_in[bass.ts(i * 8 + gd, 128), gd * 256:(gd + 1) * 256])
            if False else e.dma_start(
                out=wst[:] if gd % 2 == 0 else wst2b[:],
                in_=w1_in[bass.ts(i * 8 + gd, 128), :]),
            waits=([wconv[-2]] if len(wconv) >= 2 else []))
        wconv.append(lem.op("scalar", lambda e, i, gd=gd: e.copy(
            out=W1b[gd][:], in_=wst[:] if gd % 2 == 0 else wst2b[:]),
            waits=[hld]))
    wst2 = p.sb([128, 2048])
    wst2c = p.sb([128, 2048])
    w2conv = []
    for fc2 in range(8):
        buf = wst2 if fc2 % 2 == 0 else wst2c
        hld = lem.op("gpsimd", lambda e, i, fc2=fc2, buf=buf: e.dma_start(
            out=buf[:].rearrange("p (two c) -> p two c", two=2),
            in_=w2_in[bass.ts(i * 8 + fc2, 256), :].rearrange(
                "(two p) c -> p two c", p=128)),
            waits=([w2conv[-2]] if len(w2conv) >= 2 else []))
        w2conv.append(lem.op("scalar", lambda e, i, fc2=fc2, buf=buf: e.copy(
            out=W2b[fc2][:], in_=buf[:]), waits=[hld]))
    wdone = w2conv[-1]
    w1done = wconv[-1]

    prev_dn = None
    prev_acc = None
    prev_hmu = [None, None]
    prev_hev = [None, None]
    prev_gel = [None, None]
    for t8 in range(8):
        sl = slice(t8 * 512, (t8 + 1) * 512)
        lds = []
        for g in range(NCH):
            lds.append(lem.op("sync", lambda e, i, g=g, sl=sl: e.dma_start(
                out=xlt[g][:], in_=xln_in[g][:, sl]),
                waits=([prev_dn] if prev_dn else [])))
        gel = []
        for fc in range(16):
            hm = None
            bank = PU2[fc % 2]
            for gd in range(NCH):
                w0 = None
                if gd == 0:
                    w0 = [w1done, wdone, lds[-1]]
                    if fc >= 2:
                        w0.append(gel[fc - 2])
                    elif prev_gel[fc % 2] is not None:
                        w0.append(prev_gel[fc % 2])
                hm = lem.op("tensor", lambda e, i, fc=fc, gd=gd, bank=bank:
                            e.matmul(bank[:],
                                     W1b[gd][:, fc * 128:(fc + 1) * 128],
                                     xlt[gd][:], start=(gd == 0),
                                     stop=(gd == NCH - 1)), waits=w0)
            gel.append(lem.op("scalar", lambda e, i, fc=fc, bank=bank:
                              e.activation(out=h1b[fc][:], in_=bank[:],
                                           func=AF.Gelu),
                              waits=[hm] + ([prev_dn] if prev_dn else [])))
        prev_gel = [gel[14], gel[15]]
        hbc = lem.op("tensor", lambda e, i, t8=t8: e.matmul(
            PG[:], onesr[:], grow[0:1, t8 * 512:(t8 + 1) * 512],
            start=True, stop=True),
            waits=[hg, gel[-1]]
                  + [h for h in prev_hmu if h is not None])
        for co in range(NCH):
            hm = None
            bank = PD2[co % 2]
            for fc in range(16):
                fc2, two = fc // 2, fc % 2
                off = two * 1024 + co * 128
                w0 = None
                if fc == 0:
                    w0 = [gel[-1]]
                    if co >= 2:
                        w0.append(prev_hev[co % 2])
                    elif prev_hev[co % 2] is not None:
                        w0.append(prev_hev[co % 2])
                hm = lem.op("tensor", lambda e, i, fc2=fc2, off=off, fc=fc,
                            bank=bank: e.matmul(
                                bank[:], W2b[fc2][:, off:off + 128],
                                h1b[fc][:], start=(fc == 0), stop=(fc == 15)),
                            waits=w0)
            hev = lem.op("vector", lambda e, i, co=co, bank=bank:
                         e.tensor_copy(dsb2[co % 2][:], bank[:]),
                         waits=[hm] + ([prev_hmu[co % 2]]
                                       if prev_hmu[co % 2] else []))
            hmu = lem.op("vector", lambda e, i, co=co: e.tensor_mul(
                gt2[co % 2][:], dsb2[co % 2][:], PG[:]),
                waits=[hev, hbc] + ([prev_acc] if prev_acc else []))
            prev_acc = lem.op("gpsimd", lambda e, i, co=co, sl=sl: e.dma_start(
                out=x2_out[co][:, sl], in_=gt2[co % 2][:], accum_op=ALU.add),
                waits=[hmu])
            prev_hmu[co % 2] = hmu
            prev_hev[co % 2] = hev
        prev_dn = hm
    with p.ctx:
        osems = ({e: p.sem() for e in ("scalar", "vector", "tensor")
                  if em.ops[e]} if sim_order else None)
        with nc.Block() as blk1:
            em.emit(blk1, order_sems=osems)
        pre = {"sync": [(sx2, x2init), (stg_s, route_done)]}
        if osems:
            pre["tensor"] = [(osems["vector"], len(em.ops["vector"]))]
            pre["sync"].append((osems["vector"], len(em.ops["vector"])))
        with nc.Block() as blk2:
            lem.emit(blk2, p, pre=pre, post_self=("sync",))
    return p.nc


def _host_k4_inputs(inputs):
    wg = inputs["L0_ffn_Wg"]
    wgr = np.ascontiguousarray(
        wg.reshape(NCH, 128, 8).transpose(1, 0, 2).reshape(128, NCH * 8))
    idx = np.arange(128)
    emask = np.zeros((128, 8), np.float32)
    emask[idx, idx // 16] = 1.0
    emtT = np.ascontiguousarray(emask.T)
    w1f = np.ascontiguousarray(inputs["L0_ffn_W1"].reshape(8 * D, 2048))
    w2f = np.ascontiguousarray(inputs["L0_ffn_W2"].reshape(8 * 2048, D))
    return dict(wgr=wgr, emask=emask, emtT=emtT, w1f=w1f, w2f=w2f,
                onesr=np.ones((1, 128), np.float32),
                onesc=np.ones((128, 1), np.float32),
                identf=np.eye(128, dtype=np.float32))



def build_k5a(sim_order=True):
    p = Prog()
    em = p.em
    x_in = p.din("x2_fm", [NCH, 128, L], F32)
    wdt_in = p.din("wdtT", [D, D], F32)
    bdt_in = p.din("bdtr", [128, NCH], F32)
    onesc_in = p.din("onesc", [128, 1], F32)
    onesr_in = p.din("onesr", [1, 128], F32)
    xlb_out = p.dout("xln2_fm", [NCH, 128, L], BF16)
    xlf_out = p.dout("xln2f_fm", [NCH, 128, L], F32)
    dt_out = p.dout("dt_fm", [NCH, 128, L], F32)
    dtx_out = p.dout("dtx_fm", [NCH, 128, L], F32)
    b_out = p.dout("b_rows", [16, L], F32)
    c_out = p.dout("c_rows", [16, L], F32)

    onesc = p.sb([128, 1])
    onesr = p.sb([1, 128])
    bdtr = p.sb([128, NCH])
    wstage = [p.sb([128, D]) for _ in range(2)]
    wdtb = [p.sb([128, D], BF16) for _ in range(NCH)]
    wbc16 = p.sb([128, 32 * NCH])
    wbc16b = p.sb([128, 32 * NCH], BF16)
    xt = [p.sb([128, 512]) for _ in range(NCH)]
    xlb = [p.sb([128, 512], BF16) for _ in range(NCH)]
    xlf = [p.sb([128, 512]) for _ in range(NCH)]
    sqs = p.sb([128, 512])
    nmr = p.sb([1, 512]); mr = p.sb([1, 512]); m2r = p.sb([1, 512])
    vr = p.sb([1, 512]); rsr = p.sb([1, 512])
    dtt = p.sb([128, 512])
    spt = p.sb([128, 512])
    dxt = p.sb([128, 512])
    bc16 = p.sb([16, 512])
    P1 = p.ps([128, 512])
    P2 = p.ps([128, 512])
    P3 = p.ps([128, 512])
    P4 = p.ps([128, 512])
    PB = p.ps([16, 512])

    lc = p.sem(); lw2 = [p.sem(), p.sem()]; lx = p.sem()
    sca = p.sem(); vec = p.sem(); pe = p.sem()
    st1 = p.sem(); st2 = p.sem(); st3 = p.sem(); st4 = p.sem()
    stb = p.sem(); stc = p.sem()

    c0, r0, s0 = 0.0505, 0.0495, 0.5
    p.reg_consts(em, [-c0 * s0 / r0, c0], lc)
    for ap_, t in ((onesc_in, onesc), (onesr_in, onesr), (bdt_in, bdtr)):
        em.op("sync", lambda e, ap_=ap_, t=t: e.dma_start(out=t[:], in_=ap_),
              incs=[(lc, 16)])
    wbc_in = p.din("wbcr", [128, 32 * NCH], F32)
    em.op("sync", lambda e: e.dma_start(out=wbc16[:], in_=wbc_in), incs=[(lc, 16)])
    lc_all = em.n(lc)
    em.op("scalar", lambda e: e.copy(out=wbc16b[:], in_=wbc16[:]),
          waits=[(lc, lc_all)], incs=[(sca, 1)])
    wfree = [0, 0]
    for g in range(NCH):
        buf = g % 2
        w = [(sca, wfree[buf])] if wfree[buf] else []
        em.op("sync", lambda e, g=g, buf=buf: e.dma_start(
            out=wstage[buf][:], in_=wdt_in[g * 128:(g + 1) * 128, :]),
            waits=w, incs=[(lw2[buf], 16)])
        em.op("scalar", lambda e, g=g, buf=buf: e.copy(
            out=wdtb[g][:], in_=wstage[buf][:]),
            waits=[(lw2[buf], em.n(lw2[buf]))], incs=[(sca, 1)])
        wfree[buf] = em.n(sca)
    wdone = em.n(sca)

    x_free = 0
    for t8 in range(8):
        sl = slice(t8 * 512, (t8 + 1) * 512)
        for g in range(NCH):
            em.op("sync", lambda e, g=g, sl=sl: e.dma_start(
                out=xt[g][:], in_=x_in[g][:, sl]),
                waits=[(pe, x_free)], incs=[(lx, 16)])
        lx_all = em.n(lx)
        for g in range(NCH):
            em.op("tensor", lambda e, g=g: e.matmul(
                P1[0:1, :], onesc[:], xt[g][:], start=(g == 0), stop=(g == NCH - 1)),
                waits=([(lx, lx_all), (lc, lc_all), (vec, em.n(vec))] if g == 0 else None),
                incs=[(pe, 1)])
        for g in range(NCH):
            em.op("scalar", lambda e, g=g: e.activation(
                out=sqs[:], in_=xt[g][:], func=AF.Square),
                waits=[(pe, em.n(pe))], incs=[(sca, 1)])
            em.op("tensor", lambda e, g=g: e.matmul(
                P2[0:1, :], onesc[:], sqs[:], start=(g == 0), stop=(g == NCH - 1)),
                waits=[(sca, em.n(sca))] + ([(vec, em.n(vec))] if g == 0 else []),
                incs=[(pe, 1)])
        pe_done = em.n(pe)
        em.op("vector", lambda e: e.tensor_scalar_mul(
            out=nmr[:], in0=P1[0:1, :], scalar1=-1.0 / D), waits=[(pe, pe_done)])
        em.op("vector", lambda e: e.tensor_scalar_mul(
            out=mr[:], in0=P1[0:1, :], scalar1=1.0 / D))
        em.op("vector", lambda e: e.tensor_mul(m2r[:], mr[:], nmr[:]))
        em.op("vector", lambda e: e.scalar_tensor_tensor(
            out=vr[:], in0=P2[0:1, :], scalar=1.0 / D, in1=m2r[:],
            op0=ALU.mult, op1=ALU.add))
        em.op("vector", lambda e: e.tensor_scalar_add(out=vr[:], in0=vr[:],
                                                      scalar1=1e-5), incs=[(vec, 1)])
        em.op("scalar", lambda e: e.activation(out=rsr[:], in_=vr[:], func=AF.Sqrt),
              waits=[(vec, em.n(vec))], incs=[(sca, 1)])
        em.op("vector", lambda e: e.nop(), waits=[(sca, em.n(sca))])
        em.op("vector", lambda e: e.reciprocal(out=rsr[:], in_=rsr[:]),
              incs=[(vec, 1)])
        em.op("tensor", lambda e: e.matmul(P3[:], onesr[:], nmr[:],
                                           start=True, stop=True),
              waits=[(vec, em.n(vec))], incs=[(pe, 1)])
        em.op("tensor", lambda e: e.matmul(P4[:], onesr[:], rsr[:],
                                           start=True, stop=True), incs=[(pe, 1)])
        bc_done = em.n(pe)
        for g in range(NCH):
            em.op("vector", lambda e, g=g: e.tensor_add(sqs[:], xt[g][:], P3[:]),
                  waits=[(pe, bc_done), (st1, em.n(st1)), (st2, em.n(st2))])
            em.op("vector", lambda e, g=g: e.tensor_mul(xlf[g][:], sqs[:], P4[:]))
            em.op("vector", lambda e, g=g: e.tensor_copy(xlb[g][:], xlf[g][:]),
                  incs=[(vec, 1)])
            em.op("sync", lambda e, g=g, sl=sl: e.dma_start(
                out=xlb_out[g][:, sl], in_=xlb[g][:]),
                waits=[(vec, em.n(vec))], incs=[(st1, 16)])
            em.op("sync", lambda e, g=g, sl=sl: e.dma_start(
                out=xlf_out[g][:, sl], in_=xlf[g][:]),
                waits=[(vec, em.n(vec))], incs=[(st2, 16)])
        x_free = em.n(pe)
        xl_done = em.n(vec)
        for gc in range(NCH):
            for gd in range(NCH):
                em.op("tensor", lambda e, gc=gc, gd=gd: e.matmul(
                    P1[:], wdtb[gd][:, gc * 128:(gc + 1) * 128], xlb[gd][:],
                    start=(gd == 0), stop=(gd == NCH - 1)),
                    waits=([(vec, em.n(vec)), (sca, wdone)] if gd == 0 else None),
                    incs=([(pe, 1)] if gd == NCH - 1 else None))
            em.op("vector", lambda e, gc=gc: e.tensor_scalar_add(
                out=dtt[:], in0=P1[:], scalar1=bdtr[:, gc:gc + 1]),
                waits=[(pe, em.n(pe)), (st3, em.n(st3)), (st4, em.n(st4))],
                incs=[(vec, 1)])
            em.op("scalar", lambda e: e.activation(out=spt[:], in_=dtt[:],
                                                   func=AF.Abs),
                  waits=[(vec, em.n(vec))], incs=[(sca, 1)])
            em.op("scalar", lambda e: e.activation(out=spt[:], in_=spt[:],
                                                   func=AF.Exp, scale=-1.0),
                  incs=[(sca, 1)])
            em.op("vector", lambda e: e.tensor_scalar_add(
                out=spt[:], in0=spt[:], scalar1=1.0), waits=[(sca, em.n(sca))],
                incs=[(vec, 1)])
            em.op("scalar", lambda e: e.activation(out=spt[:], in_=spt[:],
                                                   func=AF.Ln),
                  waits=[(vec, em.n(vec))], incs=[(sca, 1)])
            em.op("vector", lambda e: e.tensor_relu(out=dtt[:], in_=dtt[:]),
                  waits=[(sca, em.n(sca))])
            em.op("vector", lambda e: e.tensor_add(dtt[:], dtt[:], spt[:]),
                  incs=[(vec, 1)])
            em.op("scalar", lambda e: e.activation(
                out=dtt[:], in_=dtt[:], func=AF.Tanh, scale=s0 / r0,
                bias=-c0 * s0 / r0),
                waits=[(vec, em.n(vec))], incs=[(sca, 1)])
            em.op("scalar", lambda e: e.activation(
                out=dtt[:], in_=dtt[:], func=AF.Copy, scale=r0 / s0, bias=c0),
                incs=[(sca, 1)])
            em.op("vector", lambda e, gc=gc: e.tensor_mul(
                dxt[:], dtt[:], xlf[gc][:]), waits=[(sca, em.n(sca))],
                incs=[(vec, 1)])
            dx_done = em.n(vec)
            em.op("sync", lambda e, gc=gc, sl=sl: e.dma_start(
                out=dt_out[gc][:, sl], in_=dtt[:]),
                waits=[(vec, dx_done)], incs=[(st3, 16)])
            em.op("sync", lambda e, gc=gc, sl=sl: e.dma_start(
                out=dtx_out[gc][:, sl], in_=dxt[:]),
                waits=[(vec, dx_done)], incs=[(st4, 16)])
        for wi, (outap, sem_) in enumerate(((b_out, stb), (c_out, stc))):
            for gd in range(NCH):
                em.op("tensor", lambda e, wi=wi, gd=gd: e.matmul(
                    PB[:], wbc16b[:, gd * 32 + wi * 16:gd * 32 + (wi + 1) * 16],
                    xlb[gd][:], start=(gd == 0), stop=(gd == NCH - 1)),
                    waits=([(vec, em.n(vec)), (sca, 1)] if gd == 0 else None),
                    incs=([(pe, 1)] if gd == NCH - 1 else None))
            em.op("vector", lambda e: e.tensor_copy(bc16[:], PB[:]),
                  waits=[(pe, em.n(pe)), (stb, em.n(stb)), (stc, em.n(stc))],
                  incs=[(vec, 1)])
            em.op("sync", lambda e, outap=outap, sl=sl: e.dma_start(
                out=outap[:, sl], in_=bc16[:]),
                waits=[(vec, em.n(vec))], incs=[(sem_, 16)])
    for s_ in (st1, st2, st3, st4, stb, stc):
        em.op("sync", lambda e, s_=s_: e.wait_ge(s_, em.n(s_)))
    return p.finish(sim_order=sim_order)


def build_k5b(sim_order=True, elide=True):
    p = Prog()
    nc = p.nc
    NIT = 512
    dt3_in = p.din("dt3", [8, L * 128], F32)
    dtx3_in = p.din("dtx3", [8, L * 128], F32)
    be2_in = p.din("b_exp2", [128, L], F32)
    ce_in = p.din("c_exp", [128, L], F32)
    alog_in = p.din("alogr", [128, 128], F32)
    g8_in = p.din("g8", [8, 128], F32)
    mask8_in = p.din("mask8", [128, 8], F32)
    onesf_in = p.din("onesf", [128, 128], F32)
    y_out = p.dout("y_d", [NIT, 8, 1024], F32)
    s_out = p.dout("s_d", [NIT, 128, 8], F32)

    ar = p.sb([128, 128])
    g8 = p.sb([8, 128])
    mask8 = p.sb([128, 8])
    onesf = p.sb([128, 128])
    dt3 = p.sb([8, 1024])
    dtx3 = p.sb([8, 1024])
    be2 = p.sb([128, 8])
    ce = p.sb([128, 8])
    mt4 = [p.sb([128, 128]) for _ in range(4)]
    ut2 = [p.sb([128, 128]) for _ in range(2)]
    zb = p.sb([128, 1024])
    ab = p.sb([128, 1024])
    wb = p.sb([128, 1024])
    msq = p.sb([128, 128])
    prows = p.sb([128, 8])
    vmc = p.sb([128, 8])
    scol = p.sb([128, 8])
    cblk = p.sb([128, 64])
    ybuf = p.sb([8, 1024])
    spare = p.sb([128, 1])
    PA = p.ps([128, 1024])
    PX = p.ps([128, 1024])
    PN = p.ps([128, 8])
    PY = [p.ps([8, 512]) for _ in range(2)]

    lc = p.sem()

    em = p.em
    for ap_, t in ((alog_in, ar), (g8_in, g8), (mask8_in, mask8),
                   (onesf_in, onesf)):
        em.op("sync", lambda e, ap_=ap_, t=t: e.dma_start(out=t[:], in_=ap_),
              incs=[(lc, 16)])
    lc_all = em.n(lc)
    sca5 = p.sem()
    em.op("scalar", lambda e: e.activation(out=ar[:], in_=ar[:], func=AF.Tanh,
                                           scale=0.1), waits=[(lc, lc_all)])
    em.op("scalar", lambda e: e.activation(out=ar[:], in_=ar[:], func=AF.Exp,
                                           scale=10.0), incs=[(sca5, 1)])
    em.op("vector", lambda e: e.tensor_scalar_mul(out=ar[:], in0=ar[:],
                                                  scalar1=-1.0),
          waits=[(sca5, 1)])
    for k in range(4):
        em.op("vector", lambda e, k=k: e.memset(mt4[k][:], 0.0))
    em.op("vector", lambda e: e.memset(scol[:], 1.0))

    lem = LoopEmitter(NIT, self_wait=(), elide=elide)
    lem.op("sync", lambda e, i: e.dma_start(
        out=dt3[:], in_=dt3_in[:, bass.ts(i, 1024)]))
    lem.op("sync", lambda e, i: e.dma_start(
        out=dtx3[:], in_=dtx3_in[:, bass.ts(i, 1024)]))
    lem.op("sync", lambda e, i: e.dma_start(
        out=be2[:], in_=be2_in[:, bass.ts(i, 8)]))
    h_ld = lem.op("sync", lambda e, i: e.dma_start(
        out=ce[:], in_=ce_in[:, bass.ts(i, 8)]))

    hta = [lem.op("tensor", lambda e, i, h=h: e.matmul(
        PA[:, h * 512:(h + 1) * 512], g8[:], dt3[:, h * 512:(h + 1) * 512],
        start=True, stop=True), waits=[h_ld]) for h in range(2)]
    htx = [lem.op("tensor", lambda e, i, h=h: e.matmul(
        PX[:, h * 512:(h + 1) * 512], g8[:], dtx3[:, h * 512:(h + 1) * 512],
        start=True, stop=True), waits=[h_ld]) for h in range(2)]

    def mk_z(j):
        def f(e, i):
            return e.tensor_tensor(
                out=zb[:, j * 128:(j + 1) * 128], in0=ar[:],
                in1=PA[:, j * 128:(j + 1) * 128], op=ALU.mult)
        return f

    def mk_w(j):
        def f(e, i):
            return e.tensor_tensor(
                out=wb[:, j * 128:(j + 1) * 128],
                in0=PX[:, j * 128:(j + 1) * 128],
                in1=be2[:, j:j + 1].broadcast_to((128, 128)), op=ALU.mult)
        return f

    def mk_cb(j):
        def f(e, i):
            return e.tensor_scalar_mul(
                out=cblk[:, j * 8:(j + 1) * 8], in0=mask8[:],
                scalar1=ce[:, j:j + 1])
        return f

    hz = [lem.op("vector", mk_z(j), waits=[hta[j // 4]]) for j in range(8)]
    ha1 = lem.op("scalar", lambda e, i: e.activation(
        out=ab[:], in_=zb[:], func=AF.Tanh, scale=1.0 / 16.0), waits=[hz[7]])
    ha2 = lem.op("scalar", lambda e, i: e.activation(
        out=ab[:], in_=ab[:], func=AF.Exp, scale=16.0), waits=[ha1])
    hwp = [None] * 8
    hwp[0] = lem.op("vector", mk_w(0), waits=[htx[0], h_ld])
    hcb0 = lem.op("vector", mk_cb(0), waits=[h_ld])

    hm = [None] * 8
    hsq = [None] * 8
    hn = [None] * 8
    hvm = [None] * 8
    hrc = [None] * 8
    hs = [None] * 8
    hu = [None] * 8
    hy = [None] * 8
    he = [None] * 2
    for j in range(8):
        mcur = mt4[j % 4]
        mprev = mt4[(j + 3) % 4]
        uc = ut2[j % 2]
        sprev_ap = scol[:, 7:8] if j == 0 else scol[:, j - 1:j]
        hu[j] = lem.op("gpsimd", lambda e, i, j=j, mprev=mprev, uc=uc:
                       e.tensor_tensor(out=uc[:],
                                       in0=ab[:, j * 128:(j + 1) * 128],
                                       in1=mprev[:], op=ALU.mult),
                       waits=[ha2] + ([hm[j - 1]] if j > 0 else []))
        hm[j] = lem.op("vector", lambda e, i, j=j, mcur=mcur, uc=uc,
                       sprev_ap=sprev_ap: e.scalar_tensor_tensor(
                           out=mcur[:], in0=uc[:], scalar=sprev_ap,
                           in1=wb[:, j * 128:(j + 1) * 128],
                           op0=ALU.mult, op1=ALU.add),
                       waits=[hu[j], hwp[j]] + ([hs[j - 1]] if j > 0 else [])
                             + ([hy[j - 4]] if j >= 4 else []))
        if j < 7:
            hwp[j + 1] = lem.op("vector", mk_w(j + 1),
                                waits=[htx[(j + 1) // 4], h_ld])
        else:
            lem.op("vector", lambda e, i: e.tensor_scalar_mul(
                out=spare[:], in0=mask8[:, 0:1], scalar1=1.0))
        hsq[j] = lem.op("vector", lambda e, i, j=j, mcur=mcur:
                        e.scalar_tensor_tensor(
                            out=msq[:], in0=mcur[:], scalar=1.0, in1=mcur[:],
                            op0=ALU.mult, op1=ALU.mult,
                            accum_out=prows[:, j:j + 1]),
                        waits=[hm[j]])
        hn[j] = lem.op("tensor", lambda e, i, j=j: e.matmul(
            PN[:, j:j + 1], onesf[:], prows[:, j:j + 1],
            start=True, stop=True), waits=[hsq[j]])
        hvm[j] = lem.op("vector", lambda e, i, j=j: e.tensor_single_scalar(
            out=vmc[:, j:j + 1], in_=PN[:, j:j + 1], scalar=400.0,
            op=ALU.max), waits=[hn[j]])
        if j < 7:
            lem.op("vector", mk_cb(j + 1), waits=[h_ld])
        else:
            lem.op("vector", lambda e, i: e.tensor_scalar_mul(
                out=spare[:], in0=mask8[:, 0:1], scalar1=1.0))
        hrc[j] = lem.op("vector", lambda e, i, j=j: e.reciprocal(
            out=vmc[:, j:j + 1], in_=vmc[:, j:j + 1]), waits=[hvm[j]])
        hs[j] = lem.op("scalar", lambda e, i, j=j: e.activation(
            out=scol[:, j:j + 1], in_=vmc[:, j:j + 1], func=AF.Sqrt,
            scale=400.0), waits=[hrc[j]])
        hy[j] = lem.op("tensor", lambda e, i, j=j, mcur=mcur: e.matmul(
            PY[j // 4][:, (j % 4) * 128:(j % 4 + 1) * 128],
            cblk[:, j * 8:(j + 1) * 8], mcur[:], start=True, stop=True),
            waits=[hsq[j]])
        if j == 3:
            he[0] = lem.op("scalar", lambda e, i: e.copy(
                out=ybuf[:, 0:512], in_=PY[0][:]), waits=[hy[3]])
    he[1] = lem.op("scalar", lambda e, i: e.copy(
        out=ybuf[:, 512:1024], in_=PY[1][:]), waits=[hy[7]])
    lem.op("sync", lambda e, i: e.dma_start(
        out=y_out[bass.ts(i, 1)], in_=ybuf[:]), waits=[he[1]])
    lem.op("sync", lambda e, i: e.dma_start(
        out=s_out[bass.ts(i, 1)], in_=scol[:]), waits=[he[1]])

    with p.ctx:
        osems = ({e: p.sem() for e in ("scalar", "vector", "tensor")
                  if em.ops[e]} if sim_order else None)
        with nc.Block() as blk1:
            em.emit(blk1, order_sems=osems)
        pre = {}
        if osems:
            pre["tensor"] = [(osems["vector"], len(em.ops["vector"])),
                             (osems["scalar"], len(em.ops["scalar"])),
                             (lc, lc_all)]
            pre["gpsimd"] = [(osems["vector"], len(em.ops["vector"])),
                             (osems["scalar"], len(em.ops["scalar"])),
                             (lc, lc_all)]
            pre["vector"] = [(osems["scalar"], len(em.ops["scalar"])),
                             (lc, lc_all)]
        pre["sync"] = [(lc, lc_all)]
        with nc.Block() as blk2:
            lem.emit(blk2, p, pre=pre, post_self=("sync",))
    return p.nc


def build_k6(sim_order=True):
    p = Prog()
    em = p.em
    nc = p.nc
    x2_in = p.din("x2_fm", [NCH, 128, L], F32)
    y_in = p.din("y_fm", [NCH, 128, L], F32)
    s_in = p.din("s_row", [1, L], F32)
    xlf_in = p.din("xln2f_fm", [NCH, 128, L], F32)
    dp_in = p.din("dpr", [128, NCH], F32)
    wg_in = p.din("wgTc", [24, D, 128], I32)
    wu_in = p.din("wuTc", [24, D, 128], I32)
    wd_in = p.din("wdT", [3072, D], I32)
    onesc_in = p.din("onesc", [128, 1], F32)
    onesr_in = p.din("onesr", [1, 128], F32)
    x4_out = p.dout("x4_fm", [NCH, 128, L], F32)
    x3_d = nc.dram_tensor("x3_d", [NCH, 128, L], F32, kind="Internal").ap()

    onesc = p.sb([128, 1])
    onesr = p.sb([1, 128])
    dpr = p.sb([128, NCH])
    wstage = p.sb([128, D], I32)
    wgu_st = [p.sb([128, D], I32) for _ in range(2)]
    wgu_b = [p.sb([128, D], BF16) for _ in range(2)]
    wdb = [p.sb([128, D], BF16) for _ in range(24)]
    srow = p.sb([1, 512])
    xt = [p.sb([128, 512]) for _ in range(NCH)]
    yt = p.sb([128, 512])
    xlt = p.sb([128, 512])
    xlb = [p.sb([128, 512], BF16) for _ in range(NCH)]
    sqs = p.sb([128, 512])
    nmr = p.sb([1, 512]); mr = p.sb([1, 512]); m2r = p.sb([1, 512])
    vr = p.sb([1, 512]); rsr = p.sb([1, 512])
    sgt = [p.sb([128, 512], BF16) for _ in range(24)]
    ubt = p.sb([128, 512], BF16)
    x4t = p.sb([128, 512])
    P1 = p.ps([128, 512])
    P2 = p.ps([128, 512])
    P3 = p.ps([128, 512])
    P4 = p.ps([128, 512])

    lc = p.sem(); lw = p.sem(); lx = p.sem(); ls5 = p.sem()
    sca = p.sem(); vec = p.sem(); pe = p.sem(); st3 = p.sem(); st4 = p.sem()

    for ap_, t in ((onesc_in, onesc), (onesr_in, onesr), (dp_in, dpr)):
        em.op("sync", lambda e, ap_=ap_, t=t: e.dma_start(out=t[:], in_=ap_),
              incs=[(lc, 16)])
    lc_all = em.n(lc)
    wfree = 0
    for fc in range(24):
        em.op("sync", lambda e, fc=fc: e.dma_start(
            out=wstage[:], in_=wd_in[fc * 128:(fc + 1) * 128, :]),
            waits=([(sca, wfree)] if wfree else []), incs=[(lw, 16)])
        em.op("scalar", lambda e, fc=fc: e.copy(
            out=wdb[fc][:], in_=wstage[:]),
            waits=[(lw, em.n(lw))], incs=[(sca, 1)])
        wfree = em.n(sca)
    wdone = em.n(sca)

    x_free = 0
    for t8 in range(8):
        sl = slice(t8 * 512, (t8 + 1) * 512)
        em.op("sync", lambda e, sl=sl: e.dma_start(out=srow[:], in_=s_in[:, sl]),
              waits=[(pe, x_free)], incs=[(ls5, 16)])
        em.op("tensor", lambda e: e.matmul(P4[:], onesr[:], srow[:],
                                           start=True, stop=True),
              waits=[(ls5, em.n(ls5)), (lc, lc_all), (vec, em.n(vec))],
              incs=[(pe, 1)])
        sbc = em.n(pe)
        for g in range(NCH):
            em.op("sync", lambda e, g=g, sl=sl: e.dma_start(
                out=yt[:], in_=y_in[g][:, sl]),
                waits=[(vec, em.n(vec))], incs=[(lx, 16)])
            em.op("sync", lambda e, g=g, sl=sl: e.dma_start(
                out=xlt[:], in_=xlf_in[g][:, sl]), incs=[(lx, 16)])
            em.op("sync", lambda e, g=g, sl=sl: e.dma_start(
                out=xt[g][:], in_=x2_in[g][:, sl]), incs=[(lx, 16)])
            lxn = em.n(lx)
            em.op("vector", lambda e, g=g: e.tensor_mul(sqs[:], yt[:], P4[:]),
                  waits=[(lx, lxn), (pe, sbc), (st3, em.n(st3))])
            em.op("vector", lambda e, g=g: e.scalar_tensor_tensor(
                out=sqs[:], in0=xlt[:], scalar=dpr[:, g:g + 1], in1=sqs[:],
                op0=ALU.mult, op1=ALU.add))
            em.op("vector", lambda e, g=g: e.tensor_add(sqs[:], sqs[:], xt[g][:]),
                  incs=[(vec, 1)])
            em.op("sync", lambda e, g=g, sl=sl: e.dma_start(
                out=x3_d[g][:, sl], in_=sqs[:]),
                waits=[(vec, em.n(vec))], incs=[(st3, 16)])
        x_free = em.n(pe)
    x3_done = em.n(st3)

    x_free = 0
    for t8 in range(8):
        sl = slice(t8 * 512, (t8 + 1) * 512)
        for g in range(NCH):
            em.op("sync", lambda e, g=g, sl=sl: e.dma_start(
                out=xt[g][:], in_=x3_d[g][:, sl]),
                waits=[(pe, x_free), (st3, x3_done)], incs=[(lx, 16)])
        lx_all = em.n(lx)
        for g in range(NCH):
            em.op("tensor", lambda e, g=g: e.matmul(
                P1[0:1, :], onesc[:], xt[g][:], start=(g == 0), stop=(g == NCH - 1)),
                waits=([(lx, lx_all), (vec, em.n(vec))] if g == 0 else None),
                incs=[(pe, 1)])
        for g in range(NCH):
            em.op("scalar", lambda e, g=g: e.activation(
                out=sqs[:], in_=xt[g][:], func=AF.Square),
                waits=[(pe, em.n(pe))], incs=[(sca, 1)])
            em.op("tensor", lambda e, g=g: e.matmul(
                P2[0:1, :], onesc[:], sqs[:], start=(g == 0), stop=(g == NCH - 1)),
                waits=[(sca, em.n(sca))] + ([(vec, em.n(vec))] if g == 0 else []),
                incs=[(pe, 1)])
        pe_done = em.n(pe)
        em.op("vector", lambda e: e.tensor_scalar_mul(
            out=nmr[:], in0=P1[0:1, :], scalar1=-1.0 / D), waits=[(pe, pe_done)])
        em.op("vector", lambda e: e.tensor_scalar_mul(
            out=mr[:], in0=P1[0:1, :], scalar1=1.0 / D))
        em.op("vector", lambda e: e.tensor_mul(m2r[:], mr[:], nmr[:]))
        em.op("vector", lambda e: e.scalar_tensor_tensor(
            out=vr[:], in0=P2[0:1, :], scalar=1.0 / D, in1=m2r[:],
            op0=ALU.mult, op1=ALU.add))
        em.op("vector", lambda e: e.tensor_scalar_add(out=vr[:], in0=vr[:],
                                                      scalar1=1e-5), incs=[(vec, 1)])
        em.op("scalar", lambda e: e.activation(out=rsr[:], in_=vr[:], func=AF.Sqrt),
              waits=[(vec, em.n(vec))], incs=[(sca, 1)])
        em.op("vector", lambda e: e.nop(), waits=[(sca, em.n(sca))])
        em.op("vector", lambda e: e.reciprocal(out=rsr[:], in_=rsr[:]),
              incs=[(vec, 1)])
        em.op("tensor", lambda e: e.matmul(P3[:], onesr[:], nmr[:],
                                           start=True, stop=True),
              waits=[(vec, em.n(vec))], incs=[(pe, 1)])
        em.op("tensor", lambda e: e.matmul(P4[:], onesr[:], rsr[:],
                                           start=True, stop=True), incs=[(pe, 1)])
        bc_done = em.n(pe)
        for g in range(NCH):
            em.op("vector", lambda e, g=g: e.tensor_add(sqs[:], xt[g][:], P3[:]),
                  waits=[(pe, bc_done)])
            em.op("vector", lambda e, g=g: e.tensor_mul(xlb[g][:], sqs[:], P4[:]),
                  incs=[(vec, 1)])
        xl_done = em.n(vec)
        x_free = em.n(pe)
        for fc in range(24):
            for wi, win in enumerate((wg_in, wu_in)):
                em.op("sync", lambda e, fc=fc, wi=wi, win=win: e.dma_start(
                    out=wgu_st[wi][:].rearrange("p (g c) -> p g c", g=NCH),
                    in_=win[fc].rearrange("(g p) c -> p g c", p=128)),
                    waits=[(sca, em.n(sca))], incs=[(lw, 16)])
                em.op("scalar", lambda e, wi=wi: e.copy(
                    out=wgu_b[wi][:], in_=wgu_st[wi][:]),
                    waits=[(lw, em.n(lw)), (pe, em.n(pe))], incs=[(sca, 1)])
            wld = em.n(sca)
            for gd in range(NCH):
                em.op("tensor", lambda e, fc=fc, gd=gd: e.matmul(
                    P1[:], wgu_b[0][:, gd * 128:(gd + 1) * 128], xlb[gd][:],
                    start=(gd == 0), stop=(gd == NCH - 1)),
                    waits=([(vec, xl_done), (sca, wld)] if gd == 0 else None),
                    incs=([(pe, 1)] if gd == NCH - 1 else None))
            em.op("scalar", lambda e, fc=fc: e.activation(
                out=sgt[fc][:], in_=P1[:], func=AF.Silu, scale=1.0 / 127.0),
                waits=[(pe, em.n(pe))], incs=[(sca, 1)])
            for gd in range(NCH):
                em.op("tensor", lambda e, fc=fc, gd=gd: e.matmul(
                    P2[:], wgu_b[1][:, gd * 128:(gd + 1) * 128], xlb[gd][:],
                    start=(gd == 0), stop=(gd == NCH - 1)),
                    waits=([(sca, em.n(sca)), (vec, em.n(vec))] if gd == 0 else None),
                    incs=([(pe, 1)] if gd == NCH - 1 else None))
            em.op("vector", lambda e, fc=fc: e.tensor_scalar_mul(
                out=ubt[:], in0=P2[:], scalar1=1.0 / 127.0),
                waits=[(pe, em.n(pe))])
            em.op("vector", lambda e, fc=fc: e.tensor_mul(
                sgt[fc][:], sgt[fc][:], ubt[:]), incs=[(vec, 1)])
        h_done = em.n(vec)
        for co in range(NCH):
            for fc in range(24):
                em.op("tensor", lambda e, co=co, fc=fc: e.matmul(
                    P3[:], wdb[fc][:, co * 128:(co + 1) * 128], sgt[fc][:],
                    start=(fc == 0), stop=(fc == 23)),
                    waits=([(vec, em.n(vec))] if fc == 0 else None),
                    incs=([(pe, 1)] if fc == 23 else None))
            em.op("vector", lambda e, co=co: e.scalar_tensor_tensor(
                out=x4t[:], in0=P3[:], scalar=1.0 / 127.0, in1=xt[co][:],
                op0=ALU.mult, op1=ALU.add),
                waits=[(pe, em.n(pe)), (st4, em.n(st4))], incs=[(vec, 1)])
            em.op("sync", lambda e, co=co, sl=sl: e.dma_start(
                out=x4_out[co][:, sl], in_=x4t[:]),
                waits=[(vec, em.n(vec))], incs=[(st4, 16)])
    em.op("sync", lambda e: e.wait_ge(st4, em.n(st4)))
    return p.finish(sim_order=sim_order)


_IDENT_BF16 = None


def _host_k1_inputs(inputs):
    import ml_dtypes
    global _IDENT_BF16
    if _IDENT_BF16 is None:
        _IDENT_BF16 = np.eye(128, dtype=ml_dtypes.bfloat16)
    wqT = np.ascontiguousarray(inputs["L0_attn_q_proj_w"].T)
    wkT = np.ascontiguousarray(inputs["L0_attn_k_proj_w"].T)
    wvT = np.ascontiguousarray(inputs["L0_attn_v_proj_w"].T)
    bpw = inputs["L0_attn_b_proj_w"]
    wbT = np.ascontiguousarray(
        bpw.reshape(4, NCH, 128).transpose(2, 1, 0).reshape(128, NCH * 4))
    def conv_r(wc):
        return np.ascontiguousarray(
            wc.reshape(NCH, 128, 4).transpose(1, 0, 2).reshape(128, NCH * 4))
    qcw = conv_r(inputs["L0_attn_q_conv_w"])
    kcw = conv_r(inputs["L0_attn_k_conv_w"])
    vcw = conv_r(inputs["L0_attn_v_conv_w"])
    maps = []
    for b in range(N_CORES):
        maps.append({
            "x": np.ascontiguousarray(inputs["L0_attn_hidden_states"][b]),
            "wqT": wqT, "wkT": wkT, "wvT": wvT, "wbT": wbT,
            "qcw": qcw, "kcw": kcw, "vcw": vcw,
            "ident": _IDENT_BF16,
        })
    return maps


def run_k1(inputs):
    nc = build_k1(sim_order=True)
    maps = _host_k1_inputs(inputs)
    res = run_bass_kernel_spmd(nc, maps, core_ids=list(range(N_CORES)))
    return res.results


def _ld_to_fm(a):
    return np.ascontiguousarray(a.reshape(L, NCH, 128).transpose(1, 2, 0))


def _run(nc, maps):
    res = run_bass_kernel_spmd(nc, maps, core_ids=list(range(N_CORES)))
    return res.results


def kernel(**inputs):
    import ml_dtypes
    x0 = np.asarray(inputs["L0_attn_hidden_states"], np.float32)

    r1 = _run(build_k1(sim_order=True), _host_k1_inputs(inputs))

    consts2 = _host_k2_consts()
    m2 = [dict(q_fm=r1[b]["q_fm"], k_fm=r1[b]["k_fm"], v_fm=r1[b]["v_fm"],
               beta=r1[b]["beta"], **consts2) for b in range(N_CORES)]
    r2 = _run(build_k2(sim_order=True), m2)

    h3a = _host_k3a_inputs(inputs)
    m3a = [dict(v_fm=r1[b]["v_fm"], delta_tm=r2[b]["delta_tm"], **h3a)
           for b in range(N_CORES)]
    r3a = _run(build_k3a(sim_order=True), m3a)

    onesc = np.ones((128, 1), np.float32)
    m3b = [dict(ls_fm=r3a[b]["ls_fm"], ll_fm=r3a[b]["ll_fm"],
                delta_fm=r3a[b]["delta_fm"], v_fm=r1[b]["v_fm"], onesc=onesc)
           for b in range(N_CORES)]
    r3b = _run(build_k3b(sim_order=True), m3b)

    h3c = _host_k3c_inputs(inputs)
    m3c = [dict(xln_fm=r1[b]["xln_fm"], stats=r3b[b]["stats"],
                ls_fm=r3a[b]["ls_fm"], ll_fm=r3a[b]["ll_fm"],
                delta_fm=r3a[b]["delta_fm"], v_fm=r1[b]["v_fm"],
                x_fm=_ld_to_fm(x0[b]), **h3c) for b in range(N_CORES)]
    r3c = _run(build_k3c(sim_order=True), m3c)

    h4 = _host_k4_inputs(inputs)
    m4a = [dict(x1_fm=r3c[b]["x1_fm"], wgr=h4["wgr"], onesc=h4["onesc"],
                onesr=h4["onesr"]) for b in range(N_CORES)]
    r4a = _run(build_k4a(sim_order=True), m4a)

    probs_all = np.concatenate([r4a[b]["probs"] for b in range(N_CORES)], axis=0)
    pr = np.zeros((128, 2048), np.float32)
    for e_ in range(8):
        for th in range(16):
            pr[e_ * 16 + th, :] = probs_all[th * 2048:(th + 1) * 2048, e_]
    m4b = [dict(x1_fm=r3c[b]["x1_fm"], xln1_fm=r4a[b]["xln1_fm"], probs_r=pr,
                own_p=np.ascontiguousarray(probs_all[b * L:(b + 1) * L, :]),
                w1f=h4["w1f"], w2f=h4["w2f"], emask=h4["emask"], emtT=h4["emtT"],
                onesr=h4["onesr"], identf=h4["identf"]) for b in range(N_CORES)]
    r4b = _run(build_k4b(sim_order=True), m4b)

    wdtT = np.ascontiguousarray(inputs["L1_attn_Wdt"].T)
    bdtr = np.ascontiguousarray(inputs["L1_attn_bdt"].reshape(NCH, 128).T)
    WB, WC = inputs["L1_attn_WB"], inputs["L1_attn_WC"]
    wbc = np.zeros((128, 32 * NCH), np.float32)
    for g in range(NCH):
        wbc[:, g * 32:g * 32 + 16] = WB[:, g * 128:(g + 1) * 128].T
        wbc[:, g * 32 + 16:g * 32 + 32] = WC[:, g * 128:(g + 1) * 128].T
    c0, r0, s0 = 0.0505, 0.0495, 0.5
    cvals = np.tile(np.array([[-c0 * s0 / r0, c0]], np.float32), (128, 1))
    m5a = [dict(x2_fm=r4b[b]["x2_fm"], wdtT=wdtT, bdtr=bdtr, wbcr=wbc,
                onesc=onesc, onesr=h4["onesr"], cvals=cvals)
           for b in range(N_CORES)]
    r5a = _run(build_k5a(sim_order=True), m5a)

    al = inputs["L1_attn_A_log"]
    alogr = np.zeros((128, 128), np.float32)
    for g in range(NCH):
        for n in range(16):
            alogr[g * 16 + n, :] = al[g * 128:(g + 1) * 128, n]
    g8m = np.zeros((8, 128), np.float32)
    mask8 = np.zeros((128, 8), np.float32)
    for g in range(NCH):
        g8m[g, g * 16:(g + 1) * 16] = 1.0
        mask8[g * 16:(g + 1) * 16, g] = 1.0
    m5b = []
    for b in range(N_CORES):
        dt3 = np.ascontiguousarray(
            r5a[b]["dt_fm"].transpose(0, 2, 1)).reshape(NCH, L * 128)
        dtx3 = np.ascontiguousarray(
            r5a[b]["dtx_fm"].transpose(0, 2, 1)).reshape(NCH, L * 128)
        brows = r5a[b]["b_rows"]
        crows = r5a[b]["c_rows"]
        bex2 = np.zeros((128, L), np.float32)
        cex = np.zeros((128, L), np.float32)
        for g in range(NCH):
            bex2[g * 16:(g + 1) * 16, :] = brows
            cex[g * 16:(g + 1) * 16, :] = crows
        m5b.append(dict(dt3=dt3, dtx3=dtx3, b_exp2=bex2, c_exp=cex,
                        alogr=alogr, g8=g8m, mask8=mask8,
                        onesf=np.ones((128, 128), np.float32)))
    r5b = _run(build_k5b(sim_order=True, elide=True), m5b)

    m6 = []
    wgTc = np.ascontiguousarray(
        inputs["L1_ffn_w_gate"].T.reshape(D, 24, 128).transpose(1, 0, 2))
    wuTc = np.ascontiguousarray(
        inputs["L1_ffn_w_up"].T.reshape(D, 24, 128).transpose(1, 0, 2))
    wdT = np.ascontiguousarray(inputs["L1_ffn_w_down"].T)
    dpr = np.ascontiguousarray(np.asarray(inputs["L1_attn_Dp"]).reshape(NCH, 128).T)
    for b in range(N_CORES):
        y_d = r5b[b]["y_d"]
        s_d = r5b[b]["s_d"]
        yr = y_d.reshape(512, 8, 8, 128)
        y_fm = np.ascontiguousarray(yr.transpose(1, 3, 0, 2).reshape(NCH, 128, L))
        s_row = np.ascontiguousarray(s_d[:, 0, :].reshape(1, L))
        m6.append(dict(x2_fm=r4b[b]["x2_fm"], y_fm=y_fm, s_row=s_row,
                       xln2f_fm=r5a[b]["xln2f_fm"], dpr=dpr,
                       wgTc=wgTc, wuTc=wuTc, wdT=wdT,
                       onesc=onesc, onesr=h4["onesr"]))
    r6 = _run(build_k6(sim_order=True), m6)

    out = np.stack([r6[b]["x4_fm"].transpose(2, 0, 1).reshape(L, D)
                    for b in range(N_CORES)])
    return out.astype(np.float32)


# version 09
# speedup vs baseline: 1.7590x; 1.0770x over previous
import numpy as np
from contextlib import ExitStack

import concourse.bass as bass
import concourse.mybir as mybir
from concourse.bass_utils import run_bass_kernel_spmd

F32 = mybir.dt.float32
BF16 = mybir.dt.bfloat16
I32 = mybir.dt.int32
AF = mybir.ActivationFunctionType
ALU = mybir.AluOpType
AX = mybir.AxisListType

B, L, D = 8, 4096, 1024
H, DH = 4, 256
NCH = D // 128
TT = L // 128
N_CORES = 8
ENGINES = ("sync", "scalar", "vector", "tensor", "gpsimd")


class Emitter:

    def __init__(self):
        self.ops = {e: [] for e in ENGINES}
        self.counts = {}

    def op(self, eng, fn, waits=None, incs=None):
        incs = list(incs or [])
        for s, k in incs:
            self.counts[id(s)] = self.counts.get(id(s), 0) + k
        self.ops[eng].append((fn, list(waits or []), incs))

    def n(self, sem):
        return self.counts.get(id(sem), 0)

    def emit(self, blk, order_sems=None):
        ordered = set(order_sems) if order_sems else set()
        trans = {}
        if order_sems:
            for eng in ENGINES:
                for idx, (fn, waits, incs) in enumerate(self.ops[eng]):
                    for s, k in incs:
                        if eng in ordered:
                            hist = trans.setdefault(id(s), [])
                            prev = hist[-1][0] if hist else 0
                            hist.append((prev + k, eng, idx))

        def xlate(s, v):
            if not order_sems or id(s) not in trans:
                return (s, v)
            for cum, eng, idx in trans[id(s)]:
                if cum >= v:
                    return (order_sems[eng], idx + 1)
            raise AssertionError("wait beyond produced count")

        def mk(eng_name):
            def body(e):
                osem = order_sems.get(eng_name) if order_sems else None
                for idx, (fn, waits, incs) in enumerate(self.ops[eng_name]):
                    for s, v in waits:
                        if v > 0:
                            s2, v2 = xlate(s, v)
                            e.wait_ge(s2, v2)
                    if osem is not None and idx > 0:
                        e.wait_ge(osem, idx)
                    ins = fn(e)
                    if osem is not None:
                        if ins is None:
                            ins = e.nop()
                        ins.then_inc(osem, 1)
                        continue
                    for s, k in incs:
                        if ins is None:
                            ins = e.nop()
                        ins.then_inc(s, k)
            return body
        for eng_name, reg in (("sync", blk.sync), ("scalar", blk.scalar),
                              ("vector", blk.vector), ("tensor", blk.tensor),
                              ("gpsimd", blk.gpsimd)):
            if self.ops[eng_name]:
                reg(mk(eng_name))



class LoopEmitter:

    def __init__(self, n_iter, self_wait=("sync", "gpsimd", "scalar",
                                           "vector", "tensor"), elide=False):
        self.ops = {e: [] for e in ENGINES}
        self.n_iter = n_iter
        self.self_wait = set(self_wait)
        self.elide = elide

    def op(self, eng, fn, waits=None):
        self.ops[eng].append((fn, list(waits or [])))
        return (eng, len(self.ops[eng]) - 1)

    def emit(self, blk, prog, pre=None, post_self=()):
        active = [e for e in ENGINES if self.ops[e]]
        osem = {e: prog.sem() for e in active}
        q = {e: (16 if e in ("sync", "gpsimd") else 1) for e in active}
        total = {e: len(self.ops[e]) for e in active}

        def mk(eng_name):
            def body(e):
                for s, v in (pre or {}).get(eng_name, []):
                    if v > 0:
                        e.wait_ge(s, v)
                base = {oe: e.alloc_register(f"b_{oe}") for oe in active}
                scr = e.alloc_register("scr")
                with e.Fori(0, self.n_iter) as i:
                    for oe in active:
                        e.reg_alu(base[oe], i, q[oe] * total[oe], ALU.mult)
                        if oe != eng_name:
                            e.wait_ge(osem[oe], base[oe])
                    for idx, (fn, waits) in enumerate(self.ops[eng_name]):
                        for weng, widx in waits:
                            if (self.elide and weng == eng_name
                                    and weng != "sync" and idx - widx >= 2):
                                continue
                            e.reg_alu(scr, base[weng], q[weng] * (widx + 1),
                                      ALU.add)
                            e.wait_ge(osem[weng], scr)
                        if eng_name in self.self_wait or idx == 0:
                            if idx > 0:
                                e.reg_alu(scr, base[eng_name],
                                          q[eng_name] * idx, ALU.add)
                                e.wait_ge(osem[eng_name], scr)
                            else:
                                e.wait_ge(osem[eng_name], base[eng_name])
                        ins = fn(e, i)
                        if ins is None:
                            ins = e.nop()
                        ins.then_inc(osem[eng_name], q[eng_name])
                if eng_name in post_self:
                    e.wait_ge(osem[eng_name],
                              q[eng_name] * total[eng_name] * self.n_iter)
            return body
        for eng_name, reg in (("sync", blk.sync), ("scalar", blk.scalar),
                              ("vector", blk.vector), ("tensor", blk.tensor),
                              ("gpsimd", blk.gpsimd)):
            if self.ops.get(eng_name):
                reg(mk(eng_name))
        return osem, q, total


class Prog:
    def __init__(self):
        self.nc = bass.Bass("TRN2", target_bir_lowering=False, debug=False)
        self.ctx = ExitStack()
        self._n = 0
        self.em = Emitter()

    def name(self, pfx):
        self._n += 1
        return f"{pfx}_{self._n}"

    def sb(self, shape, dt=F32):
        return self.ctx.enter_context(self.nc.sbuf_tensor(self.name("sb"), shape, dt))

    def ps(self, shape, dt=F32):
        return self.ctx.enter_context(self.nc.psum_tensor(self.name("ps"), shape, dt))

    def sem(self):
        return self.ctx.enter_context(self.nc.semaphore(self.name("s")))

    def din(self, name, shape, dt=F32):
        return self.nc.dram_tensor(name, shape, dt, kind="ExternalInput").ap()

    def dout(self, name, shape, dt=F32):
        return self.nc.dram_tensor(name, shape, dt, kind="ExternalOutput").ap()

    def reg_consts(self, em, values, lc_sem):
        cin = self.din("cvals", [128, len(values)], F32)
        ct = self.sb([128, len(values)])
        em.op("sync", lambda e: e.dma_start(out=ct[:], in_=cin),
              incs=[(lc_sem, 16)])
        for k, v in enumerate(values):
            self.nc.const_aps.register(mybir.dt.float32, float(v), ct[:, k:k + 1])
        return values

    def finish(self, sim_order=False):
        with self.ctx:
            order_sems = None
            if sim_order:
                order_sems = {e: self.sem()
                              for e in ("scalar", "vector", "tensor", "gpsimd")
                              if self.em.ops[e] and not any(
                                  True for fn, w, incs in self.em.ops[e]
                                  for s, k in incs if k >= 16)}
            blk = self.ctx.enter_context(self.nc.Block())
            self.em.emit(blk, order_sems)
        return self.nc


def build_k1(sim_order=False):
    p = Prog()
    em = p.em
    x_in = p.din("x", [L, D], F32)
    wT_in = {n: p.din(n, [D, D], F32) for n in ("wqT", "wkT", "wvT")}
    wbT_in = p.din("wbT", [128, 4 * NCH], F32)
    cw_in = {n: p.din(n, [128, NCH * 4], F32) for n in ("qcw", "kcw", "vcw")}
    ident_in = p.din("ident", [128, 128], BF16)

    xln_out = p.dout("xln_fm", [NCH, 128, L], BF16)
    qkv_out = {"wqT": p.dout("q_fm", [NCH, 128, L], F32),
               "wkT": p.dout("k_fm", [NCH, 128, L], F32),
               "wvT": p.dout("v_fm", [NCH, 128, L], F32)}
    cw_of = {"wqT": "qcw", "wkT": "kcw", "wvT": "vcw"}
    beta_out = p.dout("beta", [4, L], F32)

    ident = p.sb([128, 128], BF16)
    cw = {n: p.sb([128, NCH * 4]) for n in cw_in}
    wbTb = p.sb([128, 4 * NCH], BF16)
    wbTs = p.sb([128, 4 * NCH], F32)
    xln_fm = [p.sb([128, L], BF16) for _ in range(NCH)]

    xt = [p.sb([128, D]) for _ in range(2)]
    sqj = p.sb([128, D])
    stat = [{k: p.sb([128, 1]) for k in ("sum", "sq", "nm", "m", "var", "rstd")}
            for _ in range(2)]
    xln_tm = [p.sb([128, D], BF16) for _ in range(2)]
    pt = [p.ps([128, 128], BF16) for _ in range(2)]

    wstage = [p.sb([128, D]) for _ in range(2)]
    wbf = [p.sb([128, D], BF16) for _ in range(NCH)]
    pmm = [p.ps([128, 512]) for _ in range(2)]
    qpre = p.sb([128, L])
    accv = p.sb([128, L])
    pbeta = p.ps([4, 512])
    betas = p.sb([4, L])

    lc = p.sem()
    ldx = [p.sem(), p.sem()]
    lw2 = [p.sem(), p.sem()]
    sca = p.sem()
    vec = p.sem()
    pe = p.sem()
    tr = p.sem()
    qe = p.sem()
    stq = p.sem()
    stx = p.sem()

    em.op("sync", lambda e: e.dma_start(out=ident[:], in_=ident_in), incs=[(lc, 16)])
    for n in cw_in:
        em.op("sync", lambda e, n=n: e.dma_start(out=cw[n][:], in_=cw_in[n]), incs=[(lc, 16)])
    em.op("sync", lambda e: e.dma_start(out=wbTs[:], in_=wbT_in), incs=[(lc, 16)])
    consts_ld = em.n(lc)
    em.op("scalar", lambda e: e.copy(out=wbTb[:], in_=wbTs[:]),
          waits=[(lc, consts_ld)], incs=[(sca, 1)])

    xt_free = [(0, 0)] * 2
    tm_free = [0] * 2
    for i in range(TT):
        buf = i % 2
        vfree, sfree = xt_free[buf]
        em.op("sync",
              lambda e, i=i, buf=buf: e.dma_start(
                  out=xt[buf][:], in_=x_in[i * 128:(i + 1) * 128, :]),
              waits=[(vec, vfree), (sca, sfree)], incs=[(ldx[buf], 16)])
        ld_i = em.n(ldx[buf])
        em.op("vector", lambda e, buf=buf: e.tensor_reduce(
            out=stat[buf]["sum"][:], in_=xt[buf][:], axis=AX.X, op=ALU.add),
            waits=[(ldx[buf], ld_i)])
        em.op("scalar", lambda e, buf=buf: e.activation(
            out=sqj[:], in_=xt[buf][:], func=AF.Square, accum_out=stat[buf]["sq"][:]),
            waits=[(ldx[buf], ld_i)], incs=[(sca, 1)])
        sq_done = em.n(sca)
        em.op("vector", lambda e, buf=buf: e.tensor_scalar_mul(
            out=stat[buf]["nm"][:], in0=stat[buf]["sum"][:], scalar1=-1.0 / D))
        em.op("vector", lambda e, buf=buf: e.tensor_scalar_mul(
            out=stat[buf]["m"][:], in0=stat[buf]["sum"][:], scalar1=1.0 / D))
        em.op("vector", lambda e, buf=buf: e.scalar_tensor_tensor(
            out=stat[buf]["var"][:], in0=stat[buf]["nm"][:], scalar=stat[buf]["m"][:, 0:1],
            in1=stat[buf]["sq"][:], op0=ALU.mult, op1=ALU.add),
            waits=[(sca, sq_done)])
        em.op("vector", lambda e, buf=buf: e.tensor_scalar(
            out=stat[buf]["var"][:], in0=stat[buf]["var"][:], scalar1=1.0 / D,
            scalar2=1e-5, op0=ALU.mult, op1=ALU.add), incs=[(vec, 1)])
        var_done = em.n(vec)
        em.op("scalar", lambda e, buf=buf: e.activation(
            out=stat[buf]["rstd"][:], in_=stat[buf]["var"][:], func=AF.Sqrt),
            waits=[(vec, var_done)], incs=[(sca, 1)])
        sd_done = em.n(sca)
        em.op("vector", lambda e: e.nop(), waits=[(sca, sd_done)])
        em.op("vector", lambda e, buf=buf: e.reciprocal(
            out=stat[buf]["rstd"][:], in_=stat[buf]["rstd"][:]))
        em.op("vector", lambda e, buf=buf: e.tensor_scalar(
            out=xln_tm[buf][:], in0=xt[buf][:], scalar1=stat[buf]["nm"][:, 0:1],
            scalar2=stat[buf]["rstd"][:, 0:1], op0=ALU.add, op1=ALU.mult),
            waits=[(pe, tm_free[buf])], incs=[(vec, 1)])
        xt_free[buf] = (em.n(vec), sq_done)
        tm_ready = em.n(vec)
        for g in range(NCH):
            k = i * NCH + g
            bb = k % 2
            w = [(vec, tm_ready), (lc, consts_ld)]
            if k >= 2:
                w.append((tr, k - 1))
            em.op("tensor", lambda e, buf=buf, g=g, bb=bb: e.transpose(
                out=pt[bb][:], in_=xln_tm[buf][:, g * 128:(g + 1) * 128],
                identity=ident[:]),
                waits=w, incs=[(pe, 1)])
            em.op("vector", lambda e, i=i, g=g, bb=bb: e.tensor_copy(
                xln_fm[g][:, i * 128:(i + 1) * 128], pt[bb][:]),
                waits=[(pe, em.n(pe))], incs=[(tr, 1)])
        tm_free[buf] = em.n(pe)
    ln_tr = em.n(tr)

    for g in range(NCH):
        em.op("sync", lambda e, g=g: e.dma_start(out=xln_out[g], in_=xln_fm[g][:]),
              waits=[(tr, ln_tr)], incs=[(stx, 16)])

    wstage_free = [0, 0]
    for wname in ("wqT", "wkT", "wvT"):
        conv_name = cw_of[wname]
        for g in range(NCH):
            w = []
            if wstage_free[g % 2] > 0:
                w.append((sca, wstage_free[g % 2]))
            cw2 = [(pe, em.n(pe))] if g == 0 else []
            em.op("gpsimd", lambda e, wname=wname, g=g: e.dma_start(
                out=wstage[g % 2][:], in_=wT_in[wname][g * 128:(g + 1) * 128, :]),
                waits=w, incs=[(lw2[g % 2], 16)])
            em.op("scalar", lambda e, g=g: e.copy(out=wbf[g][:], in_=wstage[g % 2][:]),
                  waits=[(lw2[g % 2], em.n(lw2[g % 2]))] + cw2, incs=[(sca, 1)])
            wstage_free[g % 2] = em.n(sca)
        conv_done = em.n(sca)
        for gc in range(NCH):
            for t8 in range(8):
                grp = gc * 8 + t8
                w = [(sca, conv_done)]
                if gc == 0 and t8 == 0:
                    w.append((tr, ln_tr))
                if grp >= 2:
                    w.append((qe, em.n(qe) - 1))
                for gd in range(NCH):
                    em.op("tensor", lambda e, gd=gd, gc=gc, t8=t8, grp=grp: e.matmul(
                        pmm[grp % 2][:],
                        wbf[gd][:, gc * 128:(gc + 1) * 128],
                        xln_fm[gd][:, t8 * 512:(t8 + 1) * 512],
                        start=(gd == 0), stop=(gd == NCH - 1)),
                        waits=(w if gd == 0 else None),
                        incs=([(pe, 1)] if gd == NCH - 1 else None))
                em.op("vector", lambda e, t8=t8, grp=grp: e.tensor_copy(
                    qpre[:, t8 * 512:(t8 + 1) * 512], pmm[grp % 2][:]),
                    waits=[(pe, em.n(pe))], incs=[(qe, 1)])
            evac_all = em.n(qe)
            w0 = [(qe, evac_all)]
            if em.n(stq) > 0:
                w0.append((stq, em.n(stq)))
            em.op("vector", lambda e, conv_name=conv_name, gc=gc: e.tensor_scalar_mul(
                out=accv[:], in0=qpre[:], scalar1=cw[conv_name][:, gc * 4 + 3:gc * 4 + 4]),
                waits=w0)
            for s in (1, 2, 3):
                em.op("vector", lambda e, conv_name=conv_name, gc=gc, s=s:
                      e.scalar_tensor_tensor(
                          out=accv[:, s:L], in0=qpre[:, 0:L - s],
                          scalar=cw[conv_name][:, gc * 4 + 3 - s:gc * 4 + 4 - s],
                          in1=accv[:, s:L], op0=ALU.mult, op1=ALU.add))
            em.op("vector", lambda e: e.nop(), incs=[(vec, 1)])
            em.op("scalar", lambda e: e.activation(out=accv[:], in_=accv[:], func=AF.Silu),
                  waits=[(vec, em.n(vec))], incs=[(sca, 1)])
            em.op("sync", lambda e, wname=wname, gc=gc: e.dma_start(
                out=qkv_out[wname][gc], in_=accv[:]),
                waits=[(sca, em.n(sca))], incs=[(stq, 16)])

    for t8 in range(8):
        w = [(sca, 1)]
        if t8 >= 1:
            w.append((sca, em.n(sca)))
        for gd in range(NCH):
            em.op("tensor", lambda e, gd=gd, t8=t8: e.matmul(
                pbeta[:], wbTb[:, gd * 4:(gd + 1) * 4],
                xln_fm[gd][:, t8 * 512:(t8 + 1) * 512],
                start=(gd == 0), stop=(gd == NCH - 1)),
                waits=(w if gd == 0 else None),
                incs=([(pe, 1)] if gd == NCH - 1 else None))
        em.op("scalar", lambda e, t8=t8: e.activation(
            out=betas[:, t8 * 512:(t8 + 1) * 512], in_=pbeta[:], func=AF.Sigmoid),
            waits=[(pe, em.n(pe))], incs=[(sca, 1)])
    em.op("sync", lambda e: e.dma_start(out=beta_out, in_=betas[:]),
          waits=[(sca, em.n(sca))], incs=[(stx, 16)])

    em.op("sync", lambda e: e.wait_ge(stq, em.n(stq)))
    em.op("sync", lambda e: e.wait_ge(stx, em.n(stx)))
    return p.finish(sim_order=sim_order)



def build_k2(sim_order=True):
    p = Prog()
    em = p.em
    nc = p.nc
    q_in = p.din("q_fm", [NCH, 128, L], F32)
    k_in = p.din("k_fm", [NCH, 128, L], F32)
    v_in = p.din("v_fm", [NCH, 128, L], F32)
    beta_in = p.din("beta", [4, L], F32)
    ident_in = p.din("ident", [128, 128], BF16)
    mlow_in = p.din("mlow", [128, 128], F32)
    mupp_in = p.din("mupp", [128, 128], F32)
    muppeq_in = p.din("muppeq", [128, 128], F32)
    onesc_in = p.din("onesc", [128, 1], F32)
    onesr_in = p.din("onesr", [1, 128], F32)

    delta_out = p.dout("delta_tm", [4, TT, 128, DH], F32)

    qn_d = p.nc.dram_tensor("qn_d", [128, NCH * L], BF16, kind="Internal").ap()
    kn_d = p.nc.dram_tensor("kn_d", [128, NCH * L], BF16, kind="Internal").ap()
    kb_d = p.nc.dram_tensor("kb_d", [128, NCH * L], BF16, kind="Internal").ap()
    vb_d = p.nc.dram_tensor("vb_d", [128, NCH * L], BF16, kind="Internal").ap()

    ident = p.sb([128, 128], BF16)
    mlow = p.sb([128, 128])
    mupp = p.sb([128, 128])
    muppeq = p.sb([128, 128])
    onesc = p.sb([128, 1])
    onesr = p.sb([1, 128])
    bt = [p.sb([1, L]) for _ in range(4)]

    lc = p.sem()
    l6 = p.sem()
    sca = p.sem()
    vec = p.sem()
    pe = p.sem()

    for ap_, t in ((ident_in, ident), (mlow_in, mlow), (mupp_in, mupp),
                   (muppeq_in, muppeq), (onesc_in, onesc), (onesr_in, onesr)):
        em.op("sync", lambda e, ap_=ap_, t=t: e.dma_start(out=t[:], in_=ap_),
              incs=[(lc, 16)])
    for h in range(4):
        em.op("sync", lambda e, h=h: e.dma_start(out=bt[h][:], in_=beta_in[h:h + 1, :]),
              incs=[(lc, 16)])
    lc_all = em.n(lc)

    stg = {n: [p.sb([128, 512]) for _ in range(2)] for n in ("q", "k", "v")}
    sqt = [p.sb([128, 512]) for _ in range(2)]
    P1 = p.ps([128, 512])
    P2 = p.ps([128, 512])
    P3 = p.ps([128, 512])
    P4 = p.ps([128, 512])
    P5 = p.ps([128, 512])
    psq = P1[0:1, :]
    srow = p.sb([1, 512])
    rq = p.sb([1, 512])
    rk = p.sb([1, 512])
    rkb = p.sb([1, 512])
    pbc = [P2[:, :], P3[:, :]]
    outt = {(s, c): p.sb([128, 512], BF16) for s in range(4) for c in range(2)}
    st1s = {k: p.sem() for k in outt}

    Sf = [[p.sb([128, 256]) for _ in range(2)] for _ in range(4)]
    Sb = [[p.sb([128, 256], BF16) for _ in range(2)] for _ in range(4)]
    for h in range(4):
        for c in range(2):
            em.op("vector", lambda e, h=h, c=c: e.memset(Sf[h][c][:], 0.0))
            em.op("vector", lambda e, h=h, c=c: e.memset(Sb[h][c][:], 0.0))

    stg_free = {}
    out_free = {}
    for h in range(4):
        g0, g1 = 2 * h, 2 * h + 1
        for t8 in range(8):
            sl = slice(t8 * 512, (t8 + 1) * 512)
            for n, src_ap in (("q", q_in), ("k", k_in), ("v", v_in)):
                for ci, g in enumerate((g0, g1)):
                    w = [stg_free.get((n, ci), (vec, 0))]
                    em.op("sync", lambda e, n=n, ci=ci, g=g, sl=sl, src_ap=src_ap:
                          e.dma_start(out=stg[n][ci][:], in_=src_ap[g][:, sl]),
                          waits=[x for x in w if x[1] > 0], incs=[(l6, 16)])
            l_now = em.n(l6)
            for tgt, n in ((rq, "q"), (rk, "k")):
                for ci in range(2):
                    em.op("scalar", lambda e, n=n, ci=ci: e.activation(
                        out=sqt[ci][:], in_=stg[n][ci][:], func=AF.Square),
                        waits=[(l6, l_now)], incs=[(sca, 1)])
                sq_done = em.n(sca)
                for ci in range(2):
                    em.op("tensor", lambda e, ci=ci: e.matmul(
                        psq, onesc[:], sqt[ci][:],
                        start=(ci == 0), stop=(ci == 1)),
                        waits=([(sca, sq_done), (lc, lc_all), (vec, em.n(vec))]
                               if ci == 0 else None),
                        incs=([(pe, 1)] if ci == 1 else None))
                em.op("scalar", lambda e: e.activation(
                    out=srow[:], in_=psq, func=AF.Sqrt),
                    waits=[(pe, em.n(pe))], incs=[(sca, 1)])
                em.op("vector", lambda e: e.nop(), waits=[(sca, em.n(sca))])
                em.op("vector", lambda e, tgt=tgt: e.reciprocal(
                    out=tgt[:], in_=srow[:]), incs=[(vec, 1)])
            em.op("vector", lambda e, h=h, sl=sl: e.tensor_mul(
                rkb[:], rk[:], bt[h][:, sl]), incs=[(vec, 1)])
            rows_done = em.n(vec)
            plan = [(rq, "q", 0), (rk, "k", 1), (rkb, "k", 2), (None, "v", 3)]
            for bi, (row, n, slot) in enumerate(plan):
                bb = bi % 2
                row_ap = (row[:] if row is not None else bt[h][0:1, sl])
                em.op("tensor", lambda e, row_ap=row_ap, bb=bb: e.matmul(
                    pbc[bb], onesr[:], row_ap, start=True, stop=True),
                    waits=[(vec, rows_done), (vec, em.n(vec))], incs=[(pe, 1)])
                pe_now = em.n(pe)
                for ci in range(2):
                    ofr = out_free.get((slot, ci))
                    em.op("vector", lambda e, n=n, ci=ci, slot=slot, bb=bb:
                          e.tensor_mul(outt[(slot, ci)][:], stg[n][ci][:], pbc[bb]),
                          waits=[(pe, pe_now)] + ([ofr] if ofr else []),
                          incs=[(vec, 1)])
                    dst = {0: qn_d, 1: kn_d, 2: kb_d, 3: vb_d}[slot]
                    g = (g0, g1)[ci]
                    em.op("sync", lambda e, dst=dst, g=g, sl=sl, slot=slot, ci=ci:
                          e.dma_start(out=dst[:, g * L + sl.start:g * L + sl.stop],
                                      in_=outt[(slot, ci)][:]),
                          waits=[(vec, em.n(vec))], incs=[(st1s[(slot, ci)], 16)])
                    out_free[(slot, ci)] = (st1s[(slot, ci)], em.n(st1s[(slot, ci)]))
            for n in ("q", "k", "v"):
                for ci in range(2):
                    stg_free[(n, ci)] = (vec, em.n(vec))
    st1_done = [(s, em.n(s)) for s in st1s.values()]
    pe1_all = em.n(pe)
    vec1_all = em.n(vec)
    sca1_all = em.n(sca)

    lem = LoopEmitter(TT)
    qnw = p.sb([128, 256], BF16)
    knw = p.sb([128, 256], BF16)
    kbw = p.sb([128, 256], BF16)
    knt = p.sb([128, 256], BF16)
    xw = p.sb([128, 512], BF16)
    S1t = p.sb([128, 128], BF16)
    ST1t = p.sb([128, 128], BF16)
    attnT = p.sb([128, 128], BF16)
    Ppow = [p.sb([128, 128], BF16) for _ in range(6)]
    PTpow = [p.sb([128, 128], BF16) for _ in range(7)]
    wt = [p.sb([128, 128], BF16) for _ in range(2)]
    ua = p.sb([128, 256], BF16)
    ot = p.sb([128, 256])
    pG = P3[:, 0:128]
    pPa = P1[:, 0:128]
    pPb = P2[:, 0:128]
    pA = P4[:, :]
    pS0 = P5[:, 0:256]
    pS1 = P1[:, 256:512]
    ptt = [p.ps([128, 128], BF16) for _ in range(2)]

    dof = delta_out

    prev_tail = []
    prev_so = None
    for h in range(4):
        g0, g1 = 2 * h, 2 * h + 1
        lds = []
        lds.append(lem.op("sync", lambda e, i, g0=g0: e.dma_start(
            out=qnw[:, 0:128], in_=qn_d[:, g0 * L:(g0 + 1) * L][:, bass.ts(i, 128)]),
            waits=prev_tail))
        lds.append(lem.op("sync", lambda e, i, g1=g1: e.dma_start(
            out=qnw[:, 128:256], in_=qn_d[:, g1 * L:(g1 + 1) * L][:, bass.ts(i, 128)])))
        lds.append(lem.op("sync", lambda e, i, g0=g0: e.dma_start(
            out=knw[:, 0:128], in_=kn_d[:, g0 * L:(g0 + 1) * L][:, bass.ts(i, 128)])))
        lds.append(lem.op("sync", lambda e, i, g1=g1: e.dma_start(
            out=knw[:, 128:256], in_=kn_d[:, g1 * L:(g1 + 1) * L][:, bass.ts(i, 128)])))
        lds.append(lem.op("sync", lambda e, i, g0=g0: e.dma_start(
            out=kbw[:, 0:128], in_=kb_d[:, g0 * L:(g0 + 1) * L][:, bass.ts(i, 128)])))
        lds.append(lem.op("sync", lambda e, i, g1=g1: e.dma_start(
            out=kbw[:, 128:256], in_=kb_d[:, g1 * L:(g1 + 1) * L][:, bass.ts(i, 128)])))
        lds.append(lem.op("sync", lambda e, i, g0=g0: e.dma_start_transpose(
            out=xw[:, 0:128], in_=vb_d[:, g0 * L:(g0 + 1) * L][:, bass.ts(i, 128)])))
        lds.append(lem.op("sync", lambda e, i, g1=g1: e.dma_start_transpose(
            out=xw[:, 128:256], in_=vb_d[:, g1 * L:(g1 + 1) * L][:, bass.ts(i, 128)])))
        lds.append(lem.op("sync", lambda e, i, g0=g0: e.dma_start_transpose(
            out=xw[:, 256:384], in_=kb_d[:, g0 * L:(g0 + 1) * L][:, bass.ts(i, 128)])))
        lds.append(lem.op("sync", lambda e, i, g1=g1: e.dma_start_transpose(
            out=xw[:, 384:512], in_=kb_d[:, g1 * L:(g1 + 1) * L][:, bass.ts(i, 128)])))
        lds.append(lem.op("sync", lambda e, i, g0=g0: e.dma_start_transpose(
            out=knt[:, 0:128], in_=kn_d[:, g0 * L:(g0 + 1) * L][:, bass.ts(i, 128)])))
        lds.append(lem.op("sync", lambda e, i, g1=g1: e.dma_start_transpose(
            out=knt[:, 128:256], in_=kn_d[:, g1 * L:(g1 + 1) * L][:, bass.ts(i, 128)])))
        ld_all = lds[-1]

        h1 = lem.op("tensor", lambda e, i: e.matmul(
            pG, kbw[:, 0:128], knw[:, 0:128], start=True, stop=False),
            waits=[lds[2], lds[3], lds[4], lds[5]])
        h1 = lem.op("tensor", lambda e, i: e.matmul(
            pG, kbw[:, 128:256], knw[:, 128:256], start=False, stop=True))
        m1 = lem.op("vector", lambda e, i: e.tensor_mul(S1t[:], pG, mlow[:]),
                    waits=[h1])
        h2 = lem.op("tensor", lambda e, i: e.matmul(
            pG, knw[:, 0:128], kbw[:, 0:128], start=True, stop=False),
            waits=[m1])
        h2 = lem.op("tensor", lambda e, i: e.matmul(
            pG, knw[:, 128:256], kbw[:, 128:256], start=False, stop=True))
        m2 = lem.op("vector", lambda e, i: e.tensor_mul(ST1t[:], pG, mupp[:]),
                    waits=[h2])
        h3 = lem.op("tensor", lambda e, i: e.matmul(
            pG, knw[:, 0:128], qnw[:, 0:128], start=True, stop=False),
            waits=[m2, lds[0], lds[1]])
        h3 = lem.op("tensor", lambda e, i: e.matmul(
            pG, knw[:, 128:256], qnw[:, 128:256], start=False, stop=True))
        m3 = lem.op("vector", lambda e, i: e.tensor_mul(attnT[:], pG, muppeq[:]),
                    waits=[h3])

        prevP, prevPT = S1t, ST1t
        ev_prev = m3
        for lev in range(1, 6):
            ha = lem.op("tensor", lambda e, i, prevP=prevP, prevPT=prevPT: e.matmul(
                pPa, prevPT[:], prevP[:], start=True, stop=True),
                waits=[ev_prev] if ev_prev else None)
            hb = lem.op("tensor", lambda e, i, prevP=prevP, prevPT=prevPT: e.matmul(
                pPb, prevP[:], prevPT[:], start=True, stop=True))
            ea = lem.op("vector", lambda e, i, lev=lev: e.tensor_copy(
                Ppow[lev][:], pPa), waits=[ha])
            eb = lem.op("vector", lambda e, i, lev=lev: e.tensor_copy(
                PTpow[lev][:], pPb), waits=[hb])
            prevP, prevPT = Ppow[lev], PTpow[lev]
            ev_prev = eb
        h64 = lem.op("tensor", lambda e, i: e.matmul(
            pPa, Ppow[5][:], PTpow[5][:], start=True, stop=True),
            waits=[ev_prev])
        e64 = lem.op("vector", lambda e, i: e.tensor_copy(PTpow[6][:], pPa),
                     waits=[h64])

        PT_of = {1: ST1t, 2: PTpow[1], 4: PTpow[2], 8: PTpow[3], 16: PTpow[4],
                 32: PTpow[5], 64: PTpow[6]}
        last_stt = e64
        for pw in (64, 32, 16, 8, 4, 2, 1):
            hm = lem.op("tensor", lambda e, i, pw=pw: e.matmul(
                pA, PT_of[pw][:], xw[:], start=True, stop=True),
                waits=[last_stt] + ([ld_all] if pw == 64 else []))
            sign = 1.0 if pw > 1 else -1.0
            last_stt = lem.op("vector", lambda e, i, sign=sign: e.scalar_tensor_tensor(
                out=xw[:], in0=pA, scalar=sign, in1=xw[:],
                op0=ALU.mult, op1=ALU.add), waits=[hm])

        t0 = lem.op("tensor", lambda e, i: e.transpose(
            ptt[0][:], xw[:, 256:384], identity=ident[:]), waits=[last_stt])
        t1 = lem.op("tensor", lambda e, i: e.transpose(
            ptt[1][:], xw[:, 384:512], identity=ident[:]))
        e0 = lem.op("vector", lambda e, i: e.tensor_copy(wt[0][:], ptt[0][:]),
                    waits=[t0])
        e1 = lem.op("vector", lambda e, i: e.tensor_copy(wt[1][:], ptt[1][:]),
                    waits=[t1])

        hu = lem.op("tensor", lambda e, i, h=h: e.matmul(
            pA[:, 0:256], wt[0][:], Sb[h][0][:], start=True, stop=False),
            waits=[e0, e1])
        hu = lem.op("tensor", lambda e, i, h=h: e.matmul(
            pA[:, 0:256], wt[1][:], Sb[h][1][:], start=False, stop=True))
        eu = lem.op("vector", lambda e, i: e.scalar_tensor_tensor(
            out=ua[:], in0=pA[:, 0:256], scalar=-1.0, in1=xw[:, 0:256],
            op0=ALU.mult, op1=ALU.add), waits=[hu])

        ho = lem.op("tensor", lambda e, i, h=h: e.matmul(
            pA[:, 0:256], qnw[:, 0:128], Sb[h][0][:], start=True, stop=False),
            waits=[eu])
        ho = lem.op("tensor", lambda e, i, h=h: e.matmul(
            pA[:, 0:256], qnw[:, 128:256], Sb[h][1][:], start=False, stop=False))
        ho = lem.op("tensor", lambda e, i: e.matmul(
            pA[:, 0:256], attnT[:], ua[:], start=False, stop=True))
        eo = lem.op("vector", lambda e, i: e.tensor_copy(ot[:], pA[:, 0:256]),
                    waits=[ho] + ([prev_so] if prev_so else []))
        so = lem.op("sync", lambda e, i, h=h: e.dma_start(
            out=dof[h][i], in_=ot[:]), waits=[eo])

        hs0 = lem.op("tensor", lambda e, i: e.matmul(
            pS0, knt[:, 0:128], ua[:], start=True, stop=True), waits=[eo])
        hs1 = lem.op("tensor", lambda e, i: e.matmul(
            pS1, knt[:, 128:256], ua[:], start=True, stop=True))
        us0 = lem.op("vector", lambda e, i, h=h: e.scalar_tensor_tensor(
            out=Sf[h][0][:], in0=pS0, scalar=1.0, in1=Sf[h][0][:],
            op0=ALU.mult, op1=ALU.add), waits=[hs0])
        us1 = lem.op("vector", lambda e, i, h=h: e.scalar_tensor_tensor(
            out=Sf[h][1][:], in0=pS1, scalar=1.0, in1=Sf[h][1][:],
            op0=ALU.mult, op1=ALU.add), waits=[hs1])
        lem.op("vector", lambda e, i, h=h: e.tensor_copy(Sb[h][0][:], Sf[h][0][:]))
        ve_last = lem.op("vector", lambda e, i, h=h: e.tensor_copy(
            Sb[h][1][:], Sf[h][1][:]))
        prev_tail = [hs1, ve_last]
        prev_so = so

    with p.ctx:
        osems = ({e: p.sem() for e in ("scalar", "vector", "tensor")
                  if em.ops[e]} if sim_order else None)
        with nc.Block() as blk1:
            em.emit(blk1, order_sems=osems)
        if osems:
            pre_t = [(osems["vector"], len(em.ops["vector"])),
                     (osems["scalar"], len(em.ops["scalar"])),
                     (lc, lc_all)]
        else:
            pre_t = [(lc, lc_all), (vec, vec1_all), (sca, sca1_all)]
        pre = {
            "sync": list(st1_done),
            "tensor": pre_t,
        }
        with nc.Block() as blk2:
            lem.emit(blk2, p, pre=pre, post_self=("sync",))
    return p.nc


def _host_k2_consts():
    import ml_dtypes
    ident = np.eye(128, dtype=ml_dtypes.bfloat16)
    idx = np.arange(128)
    mlow = (idx[:, None] > idx[None, :]).astype(np.float32)
    mupp = (idx[:, None] < idx[None, :]).astype(np.float32)
    muppeq = (idx[:, None] <= idx[None, :]).astype(np.float32)
    onesc = np.ones((128, 1), np.float32)
    onesr = np.ones((1, 128), np.float32)
    return dict(ident=ident, mlow=mlow, mupp=mupp, muppeq=muppeq,
                onesc=onesc, onesr=onesr)


def run_k2(k1_results):
    nc = build_k2(sim_order=True)
    consts = _host_k2_consts()
    maps = []
    for b in range(N_CORES):
        r = k1_results[b]
        maps.append(dict(q_fm=r["q_fm"], k_fm=r["k_fm"], v_fm=r["v_fm"],
                         beta=r["beta"], **consts))
    res = run_bass_kernel_spmd(nc, maps, core_ids=list(range(N_CORES)))
    return res.results



def build_k3a(sim_order=True):
    p = Prog()
    em = p.em
    nc = p.nc
    v_in = p.din("v_fm", [NCH, 128, L], F32)
    d_in = p.din("delta_tm", [4, TT, 128, DH], F32)
    fs_in = p.din("fsw", [128, NCH * 5], F32)
    fl_in = p.din("flw", [128, NCH * 64], F32)
    ident_in = p.din("ident", [128, 128], BF16)

    ls_out = p.dout("ls_fm", [NCH, 128, L], F32)
    ll_out = p.dout("ll_fm", [NCH, 128, L], F32)
    df_out = p.dout("delta_fm", [NCH, 128, L], F32)

    fsw = p.sb([128, NCH * 5])
    flw = p.sb([128, NCH * 64])
    ident = p.sb([128, 128], BF16)
    identf = p.sb([128, 128], F32)

    vt = [p.sb([128, L]) for _ in range(2)]
    accd = p.sb([128, L])
    accp = p.sb([128, L])
    accl = p.sb([128, L])
    dt_t = [p.sb([128, DH]) for _ in range(2)]
    dfm = [p.sb([128, 128]) for _ in range(2)]
    ptr = [p.ps([128, 128]) for _ in range(2)]

    lc = p.sem()
    lv = [p.sem(), p.sem()]
    ldd = [p.sem(), p.sem()]
    vec = p.sem()
    gp = p.sem()
    pe = p.sem()
    stl = p.sem()
    stll = p.sem()
    std2 = [p.sem(), p.sem()]

    em.op("sync", lambda e: e.dma_start(out=fsw[:], in_=fs_in), incs=[(lc, 16)])
    em.op("sync", lambda e: e.dma_start(out=flw[:], in_=fl_in), incs=[(lc, 16)])
    em.op("sync", lambda e: e.dma_start(out=ident[:], in_=ident_in), incs=[(lc, 16)])
    lc_all = em.n(lc)
    em.op("vector", lambda e: e.tensor_copy(identf[:], ident[:]), waits=[(lc, lc_all)])

    vfree = [(0, 0)] * 2
    for g in range(NCH):
        buf = g % 2
        vv, gg_ = vfree[buf]
        em.op("sync", lambda e, g=g, buf=buf: e.dma_start(out=vt[buf][:], in_=v_in[g]),
              waits=[(vec, vv), (gp, gg_)], incs=[(lv[buf], 16)])
        ldn = em.n(lv[buf])
        em.op("vector", lambda e, g=g, buf=buf: e.tensor_scalar_mul(
            out=accd[:], in0=vt[buf][:], scalar1=fsw[:, g * 5 + 4:g * 5 + 5]),
            waits=[(lv[buf], ldn), (stl, em.n(stl))])
        for s in range(1, 5):
            em.op("vector", lambda e, g=g, buf=buf, s=s: e.scalar_tensor_tensor(
                out=accd[:, s:L], in0=vt[buf][:, 0:L - s],
                scalar=fsw[:, g * 5 + 4 - s:g * 5 + 5 - s],
                in1=accd[:, s:L], op0=ALU.mult, op1=ALU.add))
        em.op("vector", lambda e: e.nop(), incs=[(vec, 1)])
        em.op("sync", lambda e, g=g: e.dma_start(out=ls_out[g], in_=accd[:]),
              waits=[(vec, em.n(vec))], incs=[(stl, 16)])
        em.op("vector", lambda e, g=g, buf=buf: e.tensor_scalar_mul(
            out=accl[:], in0=vt[buf][:], scalar1=flw[:, g * 64 + 63:g * 64 + 64]),
            waits=[(stll, em.n(stll))])
        em.op("vector", lambda e, g=g, buf=buf: e.tensor_scalar_mul(
            out=accp[:, 1:L], in0=vt[buf][:, 0:L - 1],
            scalar1=flw[:, g * 64 + 62:g * 64 + 63]),
            waits=[(lv[buf], ldn), (gp, gg_), (vec, em.n(vec))])
        for s in range(2, 64):
            if s % 2 == 0:
                em.op("vector", lambda e, g=g, buf=buf, s=s: e.scalar_tensor_tensor(
                    out=accl[:, s:L], in0=vt[buf][:, 0:L - s],
                    scalar=flw[:, g * 64 + 63 - s:g * 64 + 64 - s],
                    in1=accl[:, s:L], op0=ALU.mult, op1=ALU.add))
            else:
                em.op("vector", lambda e, g=g, buf=buf, s=s: e.scalar_tensor_tensor(
                    out=accp[:, s:L], in0=vt[buf][:, 0:L - s],
                    scalar=flw[:, g * 64 + 63 - s:g * 64 + 64 - s],
                    in1=accp[:, s:L], op0=ALU.mult, op1=ALU.add))
        em.op("vector", lambda e: e.nop(), incs=[(gp, 1)])
        gp_done = em.n(gp)
        em.op("vector", lambda e: e.tensor_add(accl[:, 1:L], accl[:, 1:L], accp[:, 1:L]),
              waits=[(gp, gp_done)])
        em.op("vector", lambda e: e.nop(), incs=[(vec, 1)])
        em.op("sync", lambda e, g=g: e.dma_start(out=ll_out[g], in_=accl[:]),
              waits=[(vec, em.n(vec))], incs=[(stll, 16)])
        vfree[buf] = (em.n(vec), em.n(gp))

    dfree = [0] * 2
    for h in range(4):
        for w in range(TT):
            k = h * TT + w
            buf = k % 2
            em.op("sync", lambda e, h=h, w=w, buf=buf: e.dma_start(
                out=dt_t[buf][:], in_=d_in[h][w]),
                waits=[(pe, dfree[buf])], incs=[(ldd[buf], 16)])
            ldn = em.n(ldd[buf])
            for c in range(2):
                j = k * 2 + c
                bb = j % 2
                em.op("tensor", lambda e, buf=buf, c=c, bb=bb: e.transpose(
                    ptr[bb][:], dt_t[buf][:, c * 128:(c + 1) * 128], identity=identf[:]),
                    waits=[(ldd[buf], ldn), (lc, lc_all), (vec, em.n(vec))],
                    incs=[(pe, 1)])
                em.op("vector", lambda e, bb=bb: e.tensor_copy(dfm[bb][:], ptr[bb][:]),
                      waits=[(pe, em.n(pe)), (std2[bb], 16 * (j // 2))],
                      incs=[(vec, 1)])
                g = 2 * h + c
                em.op("sync", lambda e, g=g, w=w, bb=bb: e.dma_start(
                    out=df_out[g][:, w * 128:(w + 1) * 128], in_=dfm[bb][:]),
                    waits=[(vec, em.n(vec))], incs=[(std2[bb], 16)])
            dfree[buf] = em.n(pe)

    em.op("sync", lambda e: e.wait_ge(stl, em.n(stl)))
    em.op("sync", lambda e: e.wait_ge(stll, em.n(stll)))
    em.op("sync", lambda e: e.wait_ge(std2[0], em.n(std2[0])))
    em.op("sync", lambda e: e.wait_ge(std2[1], em.n(std2[1])))
    return p.finish(sim_order=sim_order)



def _host_k3a_inputs(inputs):
    import ml_dtypes
    fs = inputs["L0_attn_fir_short_w"].reshape(D, 5)
    fl = inputs["L0_attn_fir_long_w"].reshape(D, 64)
    fsw = np.ascontiguousarray(fs.reshape(NCH, 128, 5).transpose(1, 0, 2).reshape(128, NCH * 5))
    flw = np.ascontiguousarray(fl.reshape(NCH, 128, 64).transpose(1, 0, 2).reshape(128, NCH * 64))
    ident = np.eye(128, dtype=ml_dtypes.bfloat16)
    return dict(fsw=fsw, flw=flw, ident=ident)



def build_k3b(sim_order=True):
    p = Prog()
    em = p.em
    ins = {n: p.din(n, [NCH, 128, L], F32)
           for n in ("ls_fm", "ll_fm", "delta_fm", "v_fm")}
    onesc_in = p.din("onesc", [128, 1], F32)
    st_out = p.dout("stats", [4, 16, L], F32)

    onesc = p.sb([128, 1])
    stg = [p.sb([128, 512]) for _ in range(2)]
    sqt = p.sb([128, 512])
    rowm = p.sb([1, 512])
    rownm = p.sb([1, 512])
    rowm2 = p.sb([1, 512])
    rowvar = p.sb([1, 512])
    rowl2 = p.sb([1, 512])
    rowam = p.sb([1, 512])
    pA = p.ps([1, 512])
    pB = p.ps([1, 512])
    pC = p.ps([1, 512])

    lc = p.sem()
    lv = [p.sem(), p.sem()]
    sca = p.sem()
    vec = p.sem()
    pe = p.sem()
    sts = p.sem()

    em.op("sync", lambda e: e.dma_start(out=onesc[:], in_=onesc_in), incs=[(lc, 16)])
    tnames = ("ls_fm", "ll_fm", "delta_fm", "v_fm")
    stg_free = [(0, 0, 0)] * 2
    for h in range(4):
        for ti, tn in enumerate(tnames):
            for t8 in range(8):
                sl = slice(t8 * 512, (t8 + 1) * 512)
                ldns = []
                for ci, g in enumerate((2 * h, 2 * h + 1)):
                    buf = ci
                    vv, ss, pp = stg_free[buf]
                    em.op("sync", lambda e, tn=tn, g=g, sl=sl, buf=buf: e.dma_start(
                        out=stg[buf][:], in_=ins[tn][g][:, sl]),
                        waits=[(vec, vv), (sca, ss), (pe, pp)], incs=[(lv[buf], 16)])
                    ldns.append(em.n(lv[buf]))
                for ci in range(2):
                    em.op("tensor", lambda e, ci=ci: e.matmul(
                        pA[:], onesc[:], stg[ci][:], start=(ci == 0), stop=(ci == 1)),
                        waits=([(lv[0], ldns[0]), (lv[1], ldns[1]), (lc, 16),
                                (vec, em.n(vec))] if ci == 0 else None),
                        incs=[(pe, 1)])
                for ci in range(2):
                    em.op("scalar", lambda e, ci=ci: e.activation(
                        out=sqt[:], in_=stg[ci][:], func=AF.Square),
                        waits=[(pe, em.n(pe))],
                        incs=[(sca, 1)])
                    em.op("tensor", lambda e, ci=ci: e.matmul(
                        pB[:], onesc[:], sqt[:], start=(ci == 0), stop=(ci == 1)),
                        waits=[(sca, em.n(sca))] + ([(vec, em.n(vec))] if ci == 0 else []),
                        incs=[(pe, 1)])
                for ci in range(2):
                    em.op("scalar", lambda e, ci=ci: e.activation(
                        out=sqt[:], in_=stg[ci][:], func=AF.Abs),
                        waits=[(pe, em.n(pe))], incs=[(sca, 1)])
                    em.op("tensor", lambda e, ci=ci: e.matmul(
                        pC[:], onesc[:], sqt[:], start=(ci == 0), stop=(ci == 1)),
                        waits=[(sca, em.n(sca))] + ([(vec, em.n(vec))] if ci == 0 else []),
                        incs=[(pe, 1)])
                stg_free[0] = (em.n(vec), em.n(sca), em.n(pe))
                stg_free[1] = stg_free[0]
                pe_done = em.n(pe)
                em.op("vector", lambda e: e.tensor_scalar_mul(
                    out=rowm[:], in0=pA[:], scalar1=1.0 / DH),
                    waits=[(pe, pe_done), (sts, em.n(sts))])
                em.op("vector", lambda e: e.tensor_scalar_mul(
                    out=rownm[:], in0=pA[:], scalar1=-1.0 / DH))
                em.op("vector", lambda e: e.tensor_mul(rowm2[:], rowm[:], rownm[:]))
                em.op("vector", lambda e: e.scalar_tensor_tensor(
                    out=rowvar[:], in0=pB[:], scalar=1.0 / DH, in1=rowm2[:],
                    op0=ALU.mult, op1=ALU.add))
                em.op("scalar", lambda e: e.activation(
                    out=rowl2[:], in_=pB[:], func=AF.Sqrt),
                    waits=[(pe, pe_done), (sts, em.n(sts))], incs=[(sca, 1)])
                em.op("vector", lambda e: e.tensor_scalar_mul(
                    out=rowam[:], in0=pC[:], scalar1=1.0 / DH), incs=[(vec, 1)])
                vd = em.n(vec)
                sd = em.n(sca)
                for si, row in ((0, rowm), (1, rowvar), (2, rowam), (3, rowl2)):
                    em.op("sync", lambda e, h=h, ti=ti, si=si, sl=sl, row=row:
                          e.dma_start(out=st_out[h][ti * 4 + si][sl], in_=row[:]),
                          waits=[(vec, vd), (sca, sd)], incs=[(sts, 16)])
    em.op("sync", lambda e: e.wait_ge(sts, em.n(sts)))
    return p.finish(sim_order=sim_order)



def build_k3c(sim_order=True):
    p = Prog()
    em = p.em
    xln_in = p.din("xln_fm", [NCH, 128, L], BF16)
    st_in = p.din("stats", [4, 16, L], F32)
    t_in = {n: p.din(n, [NCH, 128, L], F32)
            for n in ("ls_fm", "ll_fm", "delta_fm", "v_fm")}
    x_in = p.din("x_fm", [NCH, 128, L], F32)
    w1hs_in = p.din("w1hsT", [D, D], F32)
    w1st_in = p.din("w1stT", [16, D], F32)
    b1_in = p.din("b1r", [128, NCH], F32)
    w2_in = p.din("w2T", [128, 4 * NCH], F32)
    b2_in = p.din("b2", [1, 4], F32)
    lt_in = p.din("ltemp", [1, 1], F32)
    rs_in = p.din("ress", [1, 1], F32)
    rl_in = p.din("resl", [1, 1], F32)
    onw_in = p.din("onwr", [128, NCH], F32)
    opj_in = p.din("oprojT", [D, D], F32)
    one_in = p.din("cone", [1, 1], F32)
    onesr_in = p.din("onesr", [1, 128], F32)
    onesc_in = p.din("onesc", [128, 1], F32)
    identf_in = p.din("identf", [128, 128], F32)
    x1_out = p.dout("x1_fm", [NCH, 128, L], F32)

    w1hsb = [p.sb([128, D], BF16) for _ in range(NCH)]
    opjb = [p.sb([128, D], BF16) for _ in range(NCH)]
    w1stb = p.sb([16, D], BF16)
    w2s = p.sb([128, 4 * NCH])
    w2b = p.sb([128, 4 * NCH], BF16)
    b1r = p.sb([128, NCH])
    onwr = p.sb([128, NCH])
    onesr = p.sb([1, 128])
    onesc = p.sb([128, 1])
    identf = p.sb([128, 128])
    b2t = p.sb([1, 4])
    ltt = p.sb([1, 1])
    rst = p.sb([1, 1])
    rlt = p.sb([1, 1])
    onet = p.sb([1, 1])
    itcol = p.sb([128, 1])
    scr1 = p.sb([1, 1])
    scr2 = p.sb([1, 1])
    wstage = [p.sb([128, D]) for _ in range(2)]
    stg16 = p.sb([16, D])

    xlt = [p.sb([128, 512], BF16) for _ in range(NCH)]
    st4 = [p.sb([16, 512]) for _ in range(4)]
    st4b = [p.sb([16, 512], BF16) for _ in range(4)]
    hdn = [[p.sb([128, 512], BF16) for _ in range(4)] for _ in range(NCH)]
    base_sb = p.sb([128, 512])
    t1 = p.sb([128, 512])
    lgt = p.sb([128, 4])
    b2c = p.sb([128, 4])
    rm = p.sb([128, 1])
    es = p.sb([128, 4])
    ssum = p.sb([128, 1])
    fwr = [[p.sb([1, 512]) for _ in range(4)] for _ in range(4)]
    rr0 = p.sb([1, 512])
    rr1 = p.sb([1, 512])
    mixin = {n: p.sb([128, 512]) for n in t_in}
    ot = [p.sb([128, 512]) for _ in range(NCH)]
    obf = [p.sb([128, 512], BF16) for _ in range(NCH)]
    sqs = p.sb([128, 512])
    nrow = p.sb([1, 512])
    xr = p.sb([128, 512])
    x1t = p.sb([128, 512])

    P1 = p.ps([128, 512])
    P2 = p.ps([128, 512])
    PL = [p.ps([128, 512]) for _ in range(4)]
    Pfw = p.ps([4, 128])
    Pb2 = p.ps([128, 4])

    lc = p.sem()
    lw2 = [p.sem(), p.sem()]
    lx = p.sem()
    lsts = p.sem()
    lmx = p.sem()
    lxr = p.sem()
    sca = p.sem()
    vec = p.sem()
    pe = p.sem()
    st = p.sem()

    for ap_, t in ((b1_in, b1r), (onw_in, onwr), (b2_in, b2t),
                   (lt_in, ltt), (rs_in, rst), (rl_in, rlt), (one_in, onet),
                   (onesr_in, onesr), (onesc_in, onesc), (identf_in, identf),
                   (w1st_in, stg16), (w2_in, w2s)):
        em.op("sync", lambda e, ap_=ap_, t=t: e.dma_start(out=t[:], in_=ap_),
              incs=[(lc, 16)])
    lc_all = em.n(lc)
    em.op("scalar", lambda e: e.copy(out=w1stb[:], in_=stg16[:]),
          waits=[(lc, lc_all)], incs=[(sca, 1)])
    em.op("scalar", lambda e: e.copy(out=w2b[:], in_=w2s[:]),
          waits=[(lc, lc_all)], incs=[(sca, 1)])
    wfree = [0, 0]
    for wi, (src_ap, dst) in enumerate(((w1hs_in, w1hsb), (opj_in, opjb))):
        for g in range(NCH):
            k = wi * NCH + g
            buf = k % 2
            w = [(sca, wfree[buf])] if wfree[buf] else []
            em.op("sync", lambda e, src_ap=src_ap, g=g, buf=buf: e.dma_start(
                out=wstage[buf][:], in_=src_ap[g * 128:(g + 1) * 128, :]),
                waits=w, incs=[(lw2[buf], 16)])
            em.op("scalar", lambda e, dst=dst, g=g, buf=buf: e.copy(
                out=dst[g][:], in_=wstage[buf][:]),
                waits=[(lw2[buf], em.n(lw2[buf]))], incs=[(sca, 1)])
            wfree[buf] = em.n(sca)
    wconv_done = em.n(sca)

    em.op("scalar", lambda e: e.activation(out=scr1[:], in_=ltt[:], func=AF.Abs),
          waits=[(lc, lc_all)], incs=[(sca, 1)])
    em.op("scalar", lambda e: e.activation(out=scr1[:], in_=scr1[:], func=AF.Exp,
                                           scale=-1.0), incs=[(sca, 1)])
    em.op("vector", lambda e: e.tensor_add(scr1[:], scr1[:], onet[:]),
          waits=[(sca, em.n(sca))], incs=[(vec, 1)])
    em.op("scalar", lambda e: e.activation(out=scr1[:], in_=scr1[:], func=AF.Ln),
          waits=[(vec, em.n(vec))], incs=[(sca, 1)])
    em.op("vector", lambda e: e.tensor_relu(out=scr2[:], in_=ltt[:]),
          waits=[(sca, em.n(sca))])
    em.op("vector", lambda e: e.tensor_add(scr1[:], scr1[:], scr2[:]))
    em.op("vector", lambda e: e.tensor_scalar_add(out=scr1[:], in0=scr1[:],
                                                  scalar1=1e-4))
    em.op("vector", lambda e: e.nop())
    em.op("vector", lambda e: e.reciprocal(out=scr1[:], in_=scr1[:]),
          incs=[(vec, 1)])
    em.op("tensor", lambda e: e.matmul(Pfw[0:1, 0:1], onet[:], scr1[:],
                                       start=True, stop=True),
          waits=[(vec, em.n(vec)), (lc, lc_all)], incs=[(pe, 1)])
    em.op("tensor", lambda e: e.matmul(P1[:, 0:1], onesr[:], scr1[:],
                                       start=True, stop=True),
          incs=[(pe, 1)])
    em.op("vector", lambda e: e.tensor_copy(itcol[:], P1[:, 0:1]),
          waits=[(pe, em.n(pe))], incs=[(vec, 1)])

    xl_free = 0
    mix_free = 0
    for t8 in range(8):
        sl = slice(t8 * 512, (t8 + 1) * 512)
        for g in range(NCH):
            em.op("sync", lambda e, g=g, sl=sl: e.dma_start(
                out=xlt[g][:], in_=xln_in[g][:, sl]),
                waits=[(pe, xl_free)], incs=[(lx, 16)])
        lx_all = em.n(lx)
        for h in range(4):
            em.op("sync", lambda e, h=h, sl=sl: e.dma_start(
                out=st4[h][:], in_=st_in[h][:, sl]),
                waits=[(sca, em.n(sca))], incs=[(lsts, 16)])
        lsts_all = em.n(lsts)
        for h in range(4):
            em.op("scalar", lambda e, h=h: e.copy(out=st4b[h][:], in_=st4[h][:]),
                  waits=[(lsts, lsts_all), (pe, em.n(pe))], incs=[(sca, 1)])
        stb_done = em.n(sca)
        for gc in range(NCH):
            for gd in range(NCH):
                em.op("tensor", lambda e, gc=gc, gd=gd: e.matmul(
                    P1[:], w1hsb[gd][:, gc * 128:(gc + 1) * 128], xlt[gd][:],
                    start=(gd == 0), stop=(gd == NCH - 1)),
                    waits=([(lx, lx_all), (sca, wconv_done), (vec, em.n(vec))]
                           if gd == 0 else None),
                    incs=([(pe, 1)] if gd == NCH - 1 else None))
            em.op("vector", lambda e: e.tensor_copy(base_sb[:], P1[:]),
                  waits=[(pe, em.n(pe))], incs=[(vec, 1)])
            for h in range(4):
                em.op("tensor", lambda e, gc=gc, h=h: e.matmul(
                    P2[:], w1stb[:, gc * 128:(gc + 1) * 128], st4b[h][:],
                    start=True, stop=True),
                    waits=[(sca, stb_done), (vec, em.n(vec))], incs=[(pe, 1)])
                em.op("vector", lambda e: e.scalar_tensor_tensor(
                    out=t1[:], in0=P2[:], scalar=1.0, in1=base_sb[:],
                    op0=ALU.mult, op1=ALU.add),
                    waits=[(pe, em.n(pe)), (sca, em.n(sca))], incs=[(vec, 1)])
                em.op("scalar", lambda e, gc=gc, h=h: e.activation(
                    out=hdn[gc][h][:], in_=t1[:], func=AF.Gelu,
                    bias=b1r[:, gc:gc + 1]),
                    waits=[(vec, em.n(vec))], incs=[(sca, 1)])
        hdn_done = em.n(sca)
        xl_free = em.n(pe)
        for tq in range(4):
            for h in range(4):
                for gc in range(NCH):
                    em.op("tensor", lambda e, tq=tq, h=h, gc=gc: e.matmul(
                        PL[tq][:, h * 4:(h + 1) * 4],
                        hdn[gc][h][:, tq * 128:(tq + 1) * 128],
                        w2b[:, gc * 4:(gc + 1) * 4],
                        start=(gc == 0), stop=(gc == NCH - 1)),
                        waits=([(sca, hdn_done), (vec, em.n(vec))]
                               if gc == 0 else None),
                        incs=([(pe, 1)] if gc == NCH - 1 else None))
        em.op("tensor", lambda e: e.matmul(Pb2[:], onesr[:], b2t[:],
                                           start=True, stop=True),
              incs=[(pe, 1)])
        em.op("vector", lambda e: e.tensor_copy(b2c[:], Pb2[:]),
              waits=[(pe, em.n(pe))], incs=[(vec, 1)])
        b2bc_done = em.n(vec)
        for tq in range(4):
            for h in range(4):
                em.op("vector", lambda e, tq=tq, h=h: e.tensor_add(
                    lgt[:], PL[tq][:, h * 4:(h + 1) * 4], b2c[:]),
                    waits=[(vec, b2bc_done)])
                em.op("vector", lambda e: e.tensor_reduce(
                    out=rm[:], in_=lgt[:], axis=AX.X, op=ALU.max))
                em.op("vector", lambda e: e.tensor_scalar_mul(
                    out=rm[:], in0=rm[:], scalar1=-1.0))
                em.op("vector", lambda e: e.tensor_scalar(
                    out=lgt[:], in0=lgt[:], scalar1=rm[:, 0:1],
                    scalar2=itcol[:, 0:1], op0=ALU.add, op1=ALU.mult),
                    incs=[(vec, 1)])
                em.op("scalar", lambda e: e.activation(
                    out=es[:], in_=lgt[:], func=AF.Exp),
                    waits=[(vec, em.n(vec)), (pe, em.n(pe))], incs=[(sca, 1)])
                em.op("vector", lambda e: e.tensor_reduce(
                    out=ssum[:], in_=es[:], axis=AX.X, op=ALU.add),
                    waits=[(sca, em.n(sca))])
                em.op("vector", lambda e: e.nop())
                em.op("vector", lambda e: e.reciprocal(out=ssum[:], in_=ssum[:]))
                em.op("vector", lambda e: e.tensor_scalar_mul(
                    out=es[:], in0=es[:], scalar1=ssum[:, 0:1]), incs=[(vec, 1)])
                for j in range(4):
                    em.op("tensor", lambda e, j=j: e.transpose(
                        Pfw[0:1, :], es[:, j:j + 1], identity=identf[:]),
                        waits=[(vec, em.n(vec)), (lc, lc_all)], incs=[(pe, 1)])
                    em.op("vector", lambda e, tq=tq, h=h, j=j: e.tensor_copy(
                        fwr[h][j][:, tq * 128:(tq + 1) * 128], Pfw[0:1, :]),
                        waits=[(pe, em.n(pe))], incs=[(vec, 1)])
        fw_done = em.n(vec)
        for h in range(4):
            em.op("vector", lambda e, h=h: e.tensor_scalar_add(
                out=rr0[:], in0=fwr[h][0][:], scalar1=rst[0:1, 0:1]))
            em.op("vector", lambda e, h=h: e.tensor_scalar_add(
                out=rr1[:], in0=fwr[h][1][:], scalar1=rlt[0:1, 0:1]),
                incs=[(vec, 1)])
            rows_done = em.n(vec)
            rows = [rr0[:], rr1[:], fwr[h][2][:], fwr[h][3][:]]
            for j in range(4):
                em.op("tensor", lambda e, j=j, rap=rows[j]: e.matmul(
                    PL[j][:], onesr[:], rap, start=True, stop=True),
                    waits=[(vec, rows_done)], incs=[(pe, 1)])
            bc_done = em.n(pe)
            for ci, g in enumerate((2 * h, 2 * h + 1)):
                for ni, n in enumerate(("ls_fm", "ll_fm", "delta_fm", "v_fm")):
                    em.op("sync", lambda e, n=n, g=g, sl=sl: e.dma_start(
                        out=mixin[n][:], in_=t_in[n][g][:, sl]),
                        waits=[(vec, mix_free)], incs=[(lmx, 16)])
                    lm = em.n(lmx)
                    if ni == 0:
                        em.op("vector", lambda e, n=n, g=g: e.tensor_mul(
                            ot[g][:], mixin[n][:], PL[0][:]),
                            waits=[(lmx, lm), (pe, bc_done)], incs=[(vec, 1)])
                    else:
                        em.op("vector", lambda e, n=n, ni=ni: e.tensor_mul(
                            t1[:], mixin[n][:], PL[ni][:]),
                            waits=[(lmx, lm), (sca, em.n(sca))], incs=[(vec, 1)])
                        em.op("vector", lambda e, g=g: e.tensor_add(
                            ot[g][:], ot[g][:], t1[:]), incs=[(vec, 1)])
                    mix_free = em.n(vec)
            for ci, g in enumerate((2 * h, 2 * h + 1)):
                em.op("scalar", lambda e, g=g: e.activation(
                    out=sqs[:], in_=ot[g][:], func=AF.Square),
                    waits=[(vec, em.n(vec)), (pe, em.n(pe))], incs=[(sca, 1)])
                em.op("tensor", lambda e, ci=ci: e.matmul(
                    P2[0:1, :], onesc[:], sqs[:],
                    start=(ci == 0), stop=(ci == 1)),
                    waits=[(sca, em.n(sca))],
                    incs=[(pe, 1)])
            em.op("vector", lambda e: e.tensor_scalar(
                out=nrow[:], in0=P2[0:1, :], scalar1=1.0 / DH, scalar2=1e-5,
                op0=ALU.mult, op1=ALU.add),
                waits=[(pe, em.n(pe))], incs=[(vec, 1)])
            em.op("scalar", lambda e: e.activation(
                out=nrow[:], in_=nrow[:], func=AF.Sqrt),
                waits=[(vec, em.n(vec))], incs=[(sca, 1)])
            em.op("vector", lambda e: e.nop(), waits=[(sca, em.n(sca))])
            em.op("vector", lambda e: e.reciprocal(out=nrow[:], in_=nrow[:]),
                  incs=[(vec, 1)])
            em.op("tensor", lambda e: e.matmul(P1[:], onesr[:], nrow[:],
                                               start=True, stop=True),
                  waits=[(vec, em.n(vec))], incs=[(pe, 1)])
            nbc = em.n(pe)
            for g in (2 * h, 2 * h + 1):
                em.op("vector", lambda e, g=g: e.tensor_mul(
                    t1[:], ot[g][:], P1[:]), waits=[(pe, nbc)])
                em.op("vector", lambda e, g=g: e.tensor_scalar_mul(
                    out=obf[g][:], in0=t1[:], scalar1=onwr[:, g:g + 1]),
                    incs=[(vec, 1)])
        ob_done = em.n(vec)
        for co in range(NCH):
            for gd in range(NCH):
                em.op("tensor", lambda e, co=co, gd=gd: e.matmul(
                    P2[:], opjb[gd][:, co * 128:(co + 1) * 128], obf[gd][:],
                    start=(gd == 0), stop=(gd == NCH - 1)),
                    waits=([(vec, em.n(vec))] if gd == 0 else None),
                    incs=([(pe, 1)] if gd == NCH - 1 else None))
            em.op("sync", lambda e, co=co, sl=sl: e.dma_start(
                out=xr[:], in_=x_in[co][:, sl]),
                waits=[(vec, em.n(vec))], incs=[(lxr, 16)])
            em.op("vector", lambda e: e.scalar_tensor_tensor(
                out=x1t[:], in0=P2[:], scalar=1.0, in1=xr[:],
                op0=ALU.mult, op1=ALU.add),
                waits=[(pe, em.n(pe)), (lxr, em.n(lxr)), (st, em.n(st))],
                incs=[(vec, 1)])
            em.op("sync", lambda e, co=co, sl=sl: e.dma_start(
                out=x1_out[co][:, sl], in_=x1t[:]),
                waits=[(vec, em.n(vec))], incs=[(st, 16)])
    em.op("sync", lambda e: e.wait_ge(st, em.n(st)))
    return p.finish(sim_order=sim_order)



def _host_k3c_inputs(inputs):
    import ml_dtypes
    fw1 = inputs["L0_attn_fuse_w1"]
    w1hsT = np.ascontiguousarray(fw1[:, :D].T)
    w1stT = np.ascontiguousarray(fw1[:, D:].T)
    b1 = inputs["L0_attn_fuse_b1"]
    b1r = np.ascontiguousarray(b1.reshape(NCH, 128).T)
    w2 = inputs["L0_attn_fuse_w2"]
    w2T = np.ascontiguousarray(
        w2.T.reshape(NCH, 128, 4).transpose(1, 0, 2).reshape(128, NCH * 4))
    onw = inputs["L0_attn_o_norm_w"]
    onw_full = np.tile(onw, 4)
    onwr = np.ascontiguousarray(onw_full.reshape(NCH, 128).T)
    oprojT = np.ascontiguousarray(inputs["L0_attn_o_proj_w"].T)
    return dict(
        w1hsT=w1hsT, w1stT=w1stT, b1r=b1r, w2T=w2T,
        b2=inputs["L0_attn_fuse_b2"].reshape(1, 4).astype(np.float32),
        ltemp=inputs["L0_attn_logit_temp"].reshape(1, 1).astype(np.float32),
        ress=inputs["L0_attn_res_scale_short"].reshape(1, 1).astype(np.float32),
        resl=inputs["L0_attn_res_scale_long"].reshape(1, 1).astype(np.float32),
        onwr=onwr, oprojT=oprojT,
        cone=np.ones((1, 1), np.float32),
        onesr=np.ones((1, 128), np.float32),
        onesc=np.ones((128, 1), np.float32),
        identf=np.eye(128, dtype=np.float32),
    )



def _emit_ln_fm(p, em, src_getter, xout_tiles, sems, nchunks=NCH, dim=D):
    raise NotImplementedError


def build_k4a(sim_order=True):
    p = Prog()
    em = p.em
    x_in = p.din("x1_fm", [NCH, 128, L], F32)
    wg_in = p.din("wgr", [128, 8 * NCH], F32)
    onesc_in = p.din("onesc", [128, 1], F32)
    onesr_in = p.din("onesr", [1, 128], F32)
    xln_out = p.dout("xln1_fm", [NCH, 128, L], BF16)
    probs_out = p.dout("probs", [L, 8], F32)

    onesc = p.sb([128, 1])
    onesr = p.sb([1, 128])
    wgs = p.sb([128, 8 * NCH])
    wgb = p.sb([128, 8 * NCH], BF16)
    xt = [p.sb([128, 512]) for _ in range(NCH)]
    xlb = [p.sb([128, 512], BF16) for _ in range(NCH)]
    sqs = p.sb([128, 512])
    nmr = p.sb([1, 512])
    mr = p.sb([1, 512])
    m2r = p.sb([1, 512])
    vr = p.sb([1, 512])
    rsr = p.sb([1, 512])
    pt = p.sb([128, 8])
    rmx = p.sb([128, 1])
    esx = p.sb([128, 8])
    ssx = p.sb([128, 1])
    P1 = p.ps([128, 512])
    P2 = p.ps([128, 512])
    P3 = p.ps([128, 512])
    P4 = p.ps([128, 512])
    PP = p.ps([128, 8])

    lc = p.sem()
    lx = p.sem()
    sca = p.sem()
    vec = p.sem()
    pe = p.sem()
    st = p.sem()
    stp = p.sem()

    em.op("sync", lambda e: e.dma_start(out=onesc[:], in_=onesc_in), incs=[(lc, 16)])
    em.op("sync", lambda e: e.dma_start(out=onesr[:], in_=onesr_in), incs=[(lc, 16)])
    em.op("sync", lambda e: e.dma_start(out=wgs[:], in_=wg_in), incs=[(lc, 16)])
    lc_all = em.n(lc)
    em.op("scalar", lambda e: e.copy(out=wgb[:], in_=wgs[:]),
          waits=[(lc, lc_all)], incs=[(sca, 1)])

    x_free = 0
    for t8 in range(8):
        sl = slice(t8 * 512, (t8 + 1) * 512)
        for g in range(NCH):
            em.op("sync", lambda e, g=g, sl=sl: e.dma_start(
                out=xt[g][:], in_=x_in[g][:, sl]),
                waits=[(pe, x_free)], incs=[(lx, 16)])
        lx_all = em.n(lx)
        for g in range(NCH):
            em.op("tensor", lambda e, g=g: e.matmul(
                P1[0:1, :], onesc[:], xt[g][:], start=(g == 0), stop=(g == NCH - 1)),
                waits=([(lx, lx_all), (lc, lc_all), (vec, em.n(vec))] if g == 0 else None),
                incs=[(pe, 1)])
        for g in range(NCH):
            em.op("scalar", lambda e, g=g: e.activation(
                out=sqs[:], in_=xt[g][:], func=AF.Square),
                waits=[(pe, em.n(pe))], incs=[(sca, 1)])
            em.op("tensor", lambda e, g=g: e.matmul(
                P2[0:1, :], onesc[:], sqs[:], start=(g == 0), stop=(g == NCH - 1)),
                waits=[(sca, em.n(sca))] + ([(vec, em.n(vec))] if g == 0 else []),
                incs=[(pe, 1)])
        pe_done = em.n(pe)
        em.op("vector", lambda e: e.tensor_scalar_mul(
            out=nmr[:], in0=P1[0:1, :], scalar1=-1.0 / D), waits=[(pe, pe_done)])
        em.op("vector", lambda e: e.tensor_scalar_mul(
            out=mr[:], in0=P1[0:1, :], scalar1=1.0 / D))
        em.op("vector", lambda e: e.tensor_mul(m2r[:], mr[:], nmr[:]))
        em.op("vector", lambda e: e.scalar_tensor_tensor(
            out=vr[:], in0=P2[0:1, :], scalar=1.0 / D, in1=m2r[:],
            op0=ALU.mult, op1=ALU.add))
        em.op("vector", lambda e: e.tensor_scalar_add(out=vr[:], in0=vr[:],
                                                      scalar1=1e-5), incs=[(vec, 1)])
        em.op("scalar", lambda e: e.activation(out=rsr[:], in_=vr[:], func=AF.Sqrt),
              waits=[(vec, em.n(vec))], incs=[(sca, 1)])
        em.op("vector", lambda e: e.nop(), waits=[(sca, em.n(sca))])
        em.op("vector", lambda e: e.reciprocal(out=rsr[:], in_=rsr[:]),
              incs=[(vec, 1)])
        r_done = em.n(vec)
        em.op("tensor", lambda e: e.matmul(P3[:], onesr[:], nmr[:],
                                           start=True, stop=True),
              waits=[(vec, r_done)], incs=[(pe, 1)])
        em.op("tensor", lambda e: e.matmul(P4[:], onesr[:], rsr[:],
                                           start=True, stop=True), incs=[(pe, 1)])
        bc_done = em.n(pe)
        for g in range(NCH):
            em.op("vector", lambda e, g=g: e.tensor_add(sqs[:], xt[g][:], P3[:]),
                  waits=[(pe, bc_done)] + ([(st, em.n(st))] if em.n(st) else []))
            em.op("vector", lambda e, g=g: e.tensor_mul(xlb[g][:], sqs[:], P4[:]),
                  incs=[(vec, 1)])
            em.op("sync", lambda e, g=g, sl=sl: e.dma_start(
                out=xln_out[g][:, sl], in_=xlb[g][:]),
                waits=[(vec, em.n(vec))], incs=[(st, 16)])
        x_free = em.n(pe)
        xl_done = em.n(vec)
        for tq in range(4):
            for g in range(NCH):
                em.op("tensor", lambda e, g=g, tq=tq: e.matmul(
                    PP[:], xlb[g][:, tq * 128:(tq + 1) * 128],
                    wgb[:, g * 8:(g + 1) * 8],
                    start=(g == 0), stop=(g == NCH - 1)),
                    waits=([(vec, em.n(vec)), (sca, 1)] if g == 0 else None),
                    incs=([(pe, 1)] if g == NCH - 1 else None))
            em.op("vector", lambda e: e.tensor_copy(pt[:], PP[:]),
                  waits=[(pe, em.n(pe)), (stp, em.n(stp))])
            em.op("vector", lambda e: e.tensor_reduce(
                out=rmx[:], in_=pt[:], axis=AX.X, op=ALU.max))
            em.op("vector", lambda e: e.tensor_scalar_mul(
                out=rmx[:], in0=rmx[:], scalar1=-1.0))
            em.op("vector", lambda e: e.tensor_scalar_add(
                out=pt[:], in0=pt[:], scalar1=rmx[:, 0:1]), incs=[(vec, 1)])
            em.op("scalar", lambda e: e.activation(out=pt[:], in_=pt[:], func=AF.Exp),
                  waits=[(vec, em.n(vec))], incs=[(sca, 1)])
            em.op("vector", lambda e: e.tensor_reduce(
                out=ssx[:], in_=pt[:], axis=AX.X, op=ALU.add),
                waits=[(sca, em.n(sca))])
            em.op("vector", lambda e: e.nop())
            em.op("vector", lambda e: e.reciprocal(out=ssx[:], in_=ssx[:]))
            em.op("vector", lambda e: e.tensor_scalar_mul(
                out=pt[:], in0=pt[:], scalar1=ssx[:, 0:1]), incs=[(vec, 1)])
            em.op("sync", lambda e, t8=t8, tq=tq: e.dma_start(
                out=probs_out[t8 * 512 + tq * 128:t8 * 512 + (tq + 1) * 128, :],
                in_=pt[:]),
                waits=[(vec, em.n(vec))], incs=[(stp, 16)])
    em.op("sync", lambda e: e.wait_ge(st, em.n(st)))
    em.op("sync", lambda e: e.wait_ge(stp, em.n(stp)))
    return p.finish(sim_order=sim_order)



def build_k4b(sim_order=True, elide=True):
    p = Prog()
    em = p.em
    nc = p.nc
    x1_in = p.din("x1_fm", [NCH, 128, L], F32)
    xln_in = p.din("xln1_fm", [NCH, 128, L], BF16)
    pr_in = p.din("probs_r", [128, 2048], F32)
    own_in = p.din("own_p", [L, 8], F32)
    w1_in = p.din("w1f", [8 * D, 2048], F32)
    w2_in = p.din("w2f", [8 * 2048, D], F32)
    em_in = p.din("emask", [128, 8], F32)
    emt_in = p.din("emtT", [8, 128], F32)
    onesr_in = p.din("onesr", [1, 128], F32)
    identf_in = p.din("identf", [128, 128], F32)
    x2_out = p.dout("x2_fm", [NCH, 128, L], F32)
    gates_d = nc.dram_tensor("gates_d", [8, L], F32, kind="Internal").ap()
    th_d = nc.dram_tensor("th_d", [1, 8], F32, kind="Internal").ap()

    emask = p.sb([128, 8])
    emtT = p.sb([8, 128])
    onesr = p.sb([1, 128])
    identf = p.sb([128, 128])
    prt = p.sb([128, 2048])
    lo = p.sb([128, 1]); hi = p.sb([128, 1]); mid = p.sb([128, 1])
    ind = p.sb([128, 2048]); part = p.sb([128, 1]); cnt8 = p.sb([8, 1])
    ge = p.sb([128, 1])
    thbc = p.sb([128, 8]); ownp = p.sb([128, 8]); selt = p.sb([128, 8])
    mskd = p.sb([128, 8]); neg = p.sb([128, 8])
    mval = p.sb([128, 1]); asg = p.sb([128, 1]); pbst = p.sb([128, 1])
    wts = p.sb([128, 1]); idx = p.sb([128, 1]); eq = p.sb([128, 1])
    tsc = p.sb([128, 1])
    econ = [p.sb([128, 1]) for _ in range(8)]
    gts = p.sb([128, 8]); gtr = p.sb([8, 128])
    Pc = p.ps([8, 1]); Pb = p.ps([128, 1]); Pt = p.ps([8, 128])

    lc = p.sem(); lown = p.sem(); sca = p.sem(); vec = p.sem(); pe = p.sem()
    stg_s = p.sem(); sx2 = p.sem(); lth = p.sem()

    for ap_, t in ((em_in, emask), (emt_in, emtT), (onesr_in, onesr),
                   (identf_in, identf)):
        em.op("sync", lambda e, ap_=ap_, t=t: e.dma_start(out=t[:], in_=ap_),
              incs=[(lc, 16)])
    em.op("sync", lambda e: e.dma_start(out=prt[:], in_=pr_in), incs=[(lc, 16)])
    lc_all = em.n(lc)
    for g in range(NCH):
        em.op("sync", lambda e, g=g: e.dma_start(out=x2_out[g], in_=x1_in[g]),
              incs=[(sx2, 16)])
    x2init = em.n(sx2)

    em.op("vector", lambda e: e.memset(lo[:], 0.0), waits=[(lc, lc_all)])
    em.op("vector", lambda e: e.memset(hi[:], 1.0))
    em.op("vector", lambda e: e.memset(neg[:], -1e30))
    for e_ in range(8):
        em.op("vector", lambda e, e_=e_: e.memset(econ[e_][:], float(e_)))
    for it in range(48):
        em.op("vector", lambda e: e.tensor_add(mid[:], lo[:], hi[:]))
        em.op("vector", lambda e: e.tensor_scalar_mul(out=mid[:], in0=mid[:],
                                                      scalar1=0.5))
        em.op("vector", lambda e: e.tensor_single_scalar(
            out=ind[:], in_=prt[:], scalar=mid[:, 0:1], op=ALU.is_ge))
        em.op("vector", lambda e: e.tensor_reduce(
            out=part[:], in_=ind[:], axis=AX.X, op=ALU.add), incs=[(vec, 1)])
        em.op("tensor", lambda e: e.matmul(Pc[:], emask[:], part[:],
                                           start=True, stop=True),
              waits=[(vec, em.n(vec)), (lc, lc_all)], incs=[(pe, 1)])
        em.op("vector", lambda e: e.tensor_copy(cnt8[:], Pc[:]),
              waits=[(pe, em.n(pe))], incs=[(vec, 1)])
        em.op("tensor", lambda e: e.matmul(Pb[:], emtT[:], cnt8[:],
                                           start=True, stop=True),
              waits=[(vec, em.n(vec))], incs=[(pe, 1)])
        em.op("vector", lambda e: e.tensor_single_scalar(
            out=ge[:], in_=Pb[:], scalar=float(L), op=ALU.is_ge),
            waits=[(pe, em.n(pe))])
        em.op("vector", lambda e: e.tensor_sub(tsc[:], mid[:], lo[:]))
        em.op("vector", lambda e: e.scalar_tensor_tensor(
            out=lo[:], in0=tsc[:], scalar=ge[:, 0:1], in1=lo[:],
            op0=ALU.mult, op1=ALU.add))
        em.op("vector", lambda e: e.tensor_sub(tsc[:], hi[:], mid[:]))
        em.op("vector", lambda e: e.scalar_tensor_tensor(
            out=hi[:], in0=tsc[:], scalar=ge[:, 0:1], in1=mid[:],
            op0=ALU.mult, op1=ALU.add))
    em.op("vector", lambda e: e.nop(), incs=[(vec, 1)])
    bis_done = em.n(vec)
    em.op("sync", lambda e: e.dma_start(
        out=th_d, in_=bass.AP(lo.tensor if hasattr(lo, "tensor") else lo,
                              0, [[16, 8], [1, 1]])) if False else
        e.dma_start(out=th_d, in_=lo[0:128:16, 0:1]),
        waits=[(vec, bis_done)], incs=[(lth, 16)])
    em.op("sync", lambda e: e.dma_start(
        out=thbc[:], in_=bass.AP(th_d.tensor, 0, [[0, 128], [1, 8]])),
        waits=[(lth, em.n(lth))], incs=[(lth, 16)])
    th_ld = em.n(lth)

    gfree = 0
    for ti in range(TT):
        em.op("sync", lambda e, ti=ti: e.dma_start(
            out=ownp[:], in_=own_in[ti * 128:(ti + 1) * 128, :]),
            waits=[(vec, gfree)], incs=[(lown, 16)])
        lo_n = em.n(lown)
        em.op("vector", lambda e: e.tensor_tensor(
            out=selt[:], in0=ownp[:], in1=thbc[:], op=ALU.is_ge),
            waits=[(lown, lo_n), (lth, th_ld)])
        em.op("vector", lambda e: e.tensor_mul(mskd[:], ownp[:], selt[:]))
        em.op("vector", lambda e: e.tensor_scalar_add(out=neg[:], in0=selt[:],
                                                      scalar1=-1.0))
        em.op("vector", lambda e: e.tensor_add(mskd[:], mskd[:], neg[:]))
        em.op("vector", lambda e: e.tensor_reduce(
            out=mval[:], in_=mskd[:], axis=AX.X, op=ALU.max))
        em.op("vector", lambda e: e.tensor_reduce(
            out=asg[:], in_=selt[:], axis=AX.X, op=ALU.max))
        em.op("vector", lambda e: e.tensor_reduce(
            out=pbst[:], in_=ownp[:], axis=AX.X, op=ALU.max))
        em.op("vector", lambda e: e.tensor_mul(wts[:], mval[:], asg[:]))
        em.op("vector", lambda e: e.tensor_scalar(
            out=tsc[:], in0=asg[:], scalar1=-1.0, scalar2=1.0,
            op0=ALU.mult, op1=ALU.add))
        em.op("vector", lambda e: e.tensor_mul(tsc[:], pbst[:], tsc[:]))
        em.op("vector", lambda e: e.tensor_add(wts[:], wts[:], tsc[:]))
        em.op("vector", lambda e: e.memset(idx[:], 0.0))
        for e_ in range(7, -1, -1):
            em.op("vector", lambda e, e_=e_: e.tensor_single_scalar(
                out=eq[:], in_=ownp[:, e_:e_ + 1], scalar=wts[:, 0:1],
                op=ALU.is_ge))
            em.op("vector", lambda e, e_=e_: e.tensor_sub(
                tsc[:], econ[e_][:], idx[:]))
            em.op("vector", lambda e: e.scalar_tensor_tensor(
                out=idx[:], in0=tsc[:], scalar=eq[:, 0:1], in1=idx[:],
                op0=ALU.mult, op1=ALU.add))
        for e_ in range(8):
            em.op("vector", lambda e, e_=e_: e.tensor_single_scalar(
                out=eq[:], in_=idx[:], scalar=float(e_), op=ALU.is_equal))
            em.op("vector", lambda e, e_=e_: e.tensor_mul(
                gts[:, e_:e_ + 1], eq[:], wts[:]))
        em.op("vector", lambda e: e.nop(), incs=[(vec, 1)])
        em.op("tensor", lambda e: e.transpose(Pt[:], gts[:], identity=identf[:]),
              waits=[(vec, em.n(vec)), (stg_s, em.n(stg_s))], incs=[(pe, 1)])
        em.op("vector", lambda e: e.tensor_copy(gtr[:], Pt[:]),
              waits=[(pe, em.n(pe))], incs=[(vec, 1)])
        gfree = em.n(vec)
        em.op("sync", lambda e, ti=ti: e.dma_start(
            out=gates_d[:, ti * 128:(ti + 1) * 128], in_=gtr[:]),
            waits=[(vec, gfree)], incs=[(stg_s, 16)])
    route_done = em.n(stg_s)

    lem = LoopEmitter(8, self_wait=("sync", "gpsimd"), elide=elide)
    grow = p.sb([1, L])
    wst = p.sb([128, 2048])
    wst2b = p.sb([128, 2048])
    W1b = [p.sb([128, 2048], BF16) for _ in range(NCH)]
    W2b = [p.sb([128, 2048], BF16) for _ in range(8)]
    xlt = [p.sb([128, 512], BF16) for _ in range(NCH)]
    h1b = [p.sb([128, 512], BF16) for _ in range(16)]
    dsb2 = [p.sb([128, 512], BF16) for _ in range(2)]
    gt2 = [p.sb([128, 512]) for _ in range(2)]
    PU2 = [p.ps([128, 512]) for _ in range(2)]
    PD2 = [p.ps([128, 512]) for _ in range(2)]
    PG = p.ps([128, 512])

    hg = lem.op("sync", lambda e, i: e.dma_start(
        out=grow[:], in_=gates_d[bass.ts(i, 1), :]))
    wconv = []
    for gd in range(NCH):
        hld = lem.op("sync", lambda e, i, gd=gd: e.dma_start(
            out=wst[:, gd * 256:(gd + 1) * 256],
            in_=w1_in[bass.ts(i * 8 + gd, 128), gd * 256:(gd + 1) * 256])
            if False else e.dma_start(
                out=wst[:] if gd % 2 == 0 else wst2b[:],
                in_=w1_in[bass.ts(i * 8 + gd, 128), :]),
            waits=([wconv[-2]] if len(wconv) >= 2 else []))
        wconv.append(lem.op("scalar", lambda e, i, gd=gd: e.copy(
            out=W1b[gd][:], in_=wst[:] if gd % 2 == 0 else wst2b[:]),
            waits=[hld]))
    wst2 = p.sb([128, 2048])
    wst2c = p.sb([128, 2048])
    w2conv = []
    for fc2 in range(8):
        buf = wst2 if fc2 % 2 == 0 else wst2c
        hld = lem.op("gpsimd", lambda e, i, fc2=fc2, buf=buf: e.dma_start(
            out=buf[:].rearrange("p (two c) -> p two c", two=2),
            in_=w2_in[bass.ts(i * 8 + fc2, 256), :].rearrange(
                "(two p) c -> p two c", p=128)),
            waits=([w2conv[-2]] if len(w2conv) >= 2 else []))
        w2conv.append(lem.op("scalar", lambda e, i, fc2=fc2, buf=buf: e.copy(
            out=W2b[fc2][:], in_=buf[:]), waits=[hld]))
    wdone = w2conv[-1]
    w1done = wconv[-1]

    prev_dn = None
    prev_acc = None
    prev_hmu = [None, None]
    prev_hev = [None, None]
    prev_gel = [None, None]
    for t8 in range(8):
        sl = slice(t8 * 512, (t8 + 1) * 512)
        lds = []
        for g in range(NCH):
            lds.append(lem.op("sync", lambda e, i, g=g, sl=sl: e.dma_start(
                out=xlt[g][:], in_=xln_in[g][:, sl]),
                waits=([prev_dn] if prev_dn else [])))
        gel = []
        for fc in range(16):
            hm = None
            bank = PU2[fc % 2]
            for gd in range(NCH):
                w0 = None
                if gd == 0:
                    w0 = [w1done, wdone, lds[-1]]
                    if fc >= 2:
                        w0.append(gel[fc - 2])
                    elif prev_gel[fc % 2] is not None:
                        w0.append(prev_gel[fc % 2])
                hm = lem.op("tensor", lambda e, i, fc=fc, gd=gd, bank=bank:
                            e.matmul(bank[:],
                                     W1b[gd][:, fc * 128:(fc + 1) * 128],
                                     xlt[gd][:], start=(gd == 0),
                                     stop=(gd == NCH - 1)), waits=w0)
            gel.append(lem.op("scalar", lambda e, i, fc=fc, bank=bank:
                              e.activation(out=h1b[fc][:], in_=bank[:],
                                           func=AF.Gelu),
                              waits=[hm] + ([prev_dn] if prev_dn else [])))
        prev_gel = [gel[14], gel[15]]
        hbc = lem.op("tensor", lambda e, i, t8=t8: e.matmul(
            PG[:], onesr[:], grow[0:1, t8 * 512:(t8 + 1) * 512],
            start=True, stop=True),
            waits=[hg, gel[-1]]
                  + [h for h in prev_hmu if h is not None])
        for co in range(NCH):
            hm = None
            bank = PD2[co % 2]
            for fc in range(16):
                fc2, two = fc // 2, fc % 2
                off = two * 1024 + co * 128
                w0 = None
                if fc == 0:
                    w0 = [gel[-1]]
                    if co >= 2:
                        w0.append(prev_hev[co % 2])
                    elif prev_hev[co % 2] is not None:
                        w0.append(prev_hev[co % 2])
                hm = lem.op("tensor", lambda e, i, fc2=fc2, off=off, fc=fc,
                            bank=bank: e.matmul(
                                bank[:], W2b[fc2][:, off:off + 128],
                                h1b[fc][:], start=(fc == 0), stop=(fc == 15)),
                            waits=w0)
            hev = lem.op("vector", lambda e, i, co=co, bank=bank:
                         e.tensor_copy(dsb2[co % 2][:], bank[:]),
                         waits=[hm] + ([prev_hmu[co % 2]]
                                       if prev_hmu[co % 2] else []))
            hmu = lem.op("vector", lambda e, i, co=co: e.tensor_mul(
                gt2[co % 2][:], dsb2[co % 2][:], PG[:]),
                waits=[hev, hbc] + ([prev_acc] if prev_acc else []))
            prev_acc = lem.op("gpsimd", lambda e, i, co=co, sl=sl: e.dma_start(
                out=x2_out[co][:, sl], in_=gt2[co % 2][:], accum_op=ALU.add),
                waits=[hmu])
            prev_hmu[co % 2] = hmu
            prev_hev[co % 2] = hev
        prev_dn = hm
    with p.ctx:
        osems = ({e: p.sem() for e in ("scalar", "vector", "tensor")
                  if em.ops[e]} if sim_order else None)
        with nc.Block() as blk1:
            em.emit(blk1, order_sems=osems)
        pre = {"sync": [(sx2, x2init), (stg_s, route_done)]}
        if osems:
            pre["tensor"] = [(osems["vector"], len(em.ops["vector"]))]
            pre["sync"].append((osems["vector"], len(em.ops["vector"])))
        with nc.Block() as blk2:
            lem.emit(blk2, p, pre=pre, post_self=("sync",))
    return p.nc


def _host_k4_inputs(inputs):
    wg = inputs["L0_ffn_Wg"]
    wgr = np.ascontiguousarray(
        wg.reshape(NCH, 128, 8).transpose(1, 0, 2).reshape(128, NCH * 8))
    idx = np.arange(128)
    emask = np.zeros((128, 8), np.float32)
    emask[idx, idx // 16] = 1.0
    emtT = np.ascontiguousarray(emask.T)
    w1f = np.ascontiguousarray(inputs["L0_ffn_W1"].reshape(8 * D, 2048))
    w2f = np.ascontiguousarray(inputs["L0_ffn_W2"].reshape(8 * 2048, D))
    return dict(wgr=wgr, emask=emask, emtT=emtT, w1f=w1f, w2f=w2f,
                onesr=np.ones((1, 128), np.float32),
                onesc=np.ones((128, 1), np.float32),
                identf=np.eye(128, dtype=np.float32))



def build_k5a(sim_order=True):
    p = Prog()
    em = p.em
    x_in = p.din("x2_fm", [NCH, 128, L], F32)
    wdt_in = p.din("wdtT", [D, D], F32)
    bdt_in = p.din("bdtr", [128, NCH], F32)
    onesc_in = p.din("onesc", [128, 1], F32)
    onesr_in = p.din("onesr", [1, 128], F32)
    xlb_out = p.dout("xln2_fm", [NCH, 128, L], BF16)
    xlf_out = p.dout("xln2f_fm", [NCH, 128, L], F32)
    dt_out = p.dout("dt_fm", [NCH, 128, L], F32)
    dtx_out = p.dout("dtx_fm", [NCH, 128, L], F32)
    b_out = p.dout("b_rows", [16, L], F32)
    c_out = p.dout("c_rows", [16, L], F32)

    onesc = p.sb([128, 1])
    onesr = p.sb([1, 128])
    bdtr = p.sb([128, NCH])
    wstage = [p.sb([128, D]) for _ in range(2)]
    wdtb = [p.sb([128, D], BF16) for _ in range(NCH)]
    wbc16 = p.sb([128, 32 * NCH])
    wbc16b = p.sb([128, 32 * NCH], BF16)
    xt = [p.sb([128, 512]) for _ in range(NCH)]
    xlb = [p.sb([128, 512], BF16) for _ in range(NCH)]
    xlf = [p.sb([128, 512]) for _ in range(NCH)]
    sqs = p.sb([128, 512])
    nmr = p.sb([1, 512]); mr = p.sb([1, 512]); m2r = p.sb([1, 512])
    vr = p.sb([1, 512]); rsr = p.sb([1, 512])
    dtt = p.sb([128, 512])
    spt = p.sb([128, 512])
    dxt = p.sb([128, 512])
    bc16 = p.sb([16, 512])
    P1 = p.ps([128, 512])
    P2 = p.ps([128, 512])
    P3 = p.ps([128, 512])
    P4 = p.ps([128, 512])
    PB = p.ps([16, 512])

    lc = p.sem(); lw2 = [p.sem(), p.sem()]; lx = p.sem()
    sca = p.sem(); vec = p.sem(); pe = p.sem()
    st1 = p.sem(); st2 = p.sem(); st3 = p.sem(); st4 = p.sem()
    stb = p.sem(); stc = p.sem()

    c0, r0, s0 = 0.0505, 0.0495, 0.5
    p.reg_consts(em, [-c0 * s0 / r0, c0], lc)
    for ap_, t in ((onesc_in, onesc), (onesr_in, onesr), (bdt_in, bdtr)):
        em.op("sync", lambda e, ap_=ap_, t=t: e.dma_start(out=t[:], in_=ap_),
              incs=[(lc, 16)])
    wbc_in = p.din("wbcr", [128, 32 * NCH], F32)
    em.op("sync", lambda e: e.dma_start(out=wbc16[:], in_=wbc_in), incs=[(lc, 16)])
    lc_all = em.n(lc)
    em.op("scalar", lambda e: e.copy(out=wbc16b[:], in_=wbc16[:]),
          waits=[(lc, lc_all)], incs=[(sca, 1)])
    wfree = [0, 0]
    for g in range(NCH):
        buf = g % 2
        w = [(sca, wfree[buf])] if wfree[buf] else []
        em.op("sync", lambda e, g=g, buf=buf: e.dma_start(
            out=wstage[buf][:], in_=wdt_in[g * 128:(g + 1) * 128, :]),
            waits=w, incs=[(lw2[buf], 16)])
        em.op("scalar", lambda e, g=g, buf=buf: e.copy(
            out=wdtb[g][:], in_=wstage[buf][:]),
            waits=[(lw2[buf], em.n(lw2[buf]))], incs=[(sca, 1)])
        wfree[buf] = em.n(sca)
    wdone = em.n(sca)

    x_free = 0
    for t8 in range(8):
        sl = slice(t8 * 512, (t8 + 1) * 512)
        for g in range(NCH):
            em.op("sync", lambda e, g=g, sl=sl: e.dma_start(
                out=xt[g][:], in_=x_in[g][:, sl]),
                waits=[(pe, x_free)], incs=[(lx, 16)])
        lx_all = em.n(lx)
        for g in range(NCH):
            em.op("tensor", lambda e, g=g: e.matmul(
                P1[0:1, :], onesc[:], xt[g][:], start=(g == 0), stop=(g == NCH - 1)),
                waits=([(lx, lx_all), (lc, lc_all), (vec, em.n(vec))] if g == 0 else None),
                incs=[(pe, 1)])
        for g in range(NCH):
            em.op("scalar", lambda e, g=g: e.activation(
                out=sqs[:], in_=xt[g][:], func=AF.Square),
                waits=[(pe, em.n(pe))], incs=[(sca, 1)])
            em.op("tensor", lambda e, g=g: e.matmul(
                P2[0:1, :], onesc[:], sqs[:], start=(g == 0), stop=(g == NCH - 1)),
                waits=[(sca, em.n(sca))] + ([(vec, em.n(vec))] if g == 0 else []),
                incs=[(pe, 1)])
        pe_done = em.n(pe)
        em.op("vector", lambda e: e.tensor_scalar_mul(
            out=nmr[:], in0=P1[0:1, :], scalar1=-1.0 / D), waits=[(pe, pe_done)])
        em.op("vector", lambda e: e.tensor_scalar_mul(
            out=mr[:], in0=P1[0:1, :], scalar1=1.0 / D))
        em.op("vector", lambda e: e.tensor_mul(m2r[:], mr[:], nmr[:]))
        em.op("vector", lambda e: e.scalar_tensor_tensor(
            out=vr[:], in0=P2[0:1, :], scalar=1.0 / D, in1=m2r[:],
            op0=ALU.mult, op1=ALU.add))
        em.op("vector", lambda e: e.tensor_scalar_add(out=vr[:], in0=vr[:],
                                                      scalar1=1e-5), incs=[(vec, 1)])
        em.op("scalar", lambda e: e.activation(out=rsr[:], in_=vr[:], func=AF.Sqrt),
              waits=[(vec, em.n(vec))], incs=[(sca, 1)])
        em.op("vector", lambda e: e.nop(), waits=[(sca, em.n(sca))])
        em.op("vector", lambda e: e.reciprocal(out=rsr[:], in_=rsr[:]),
              incs=[(vec, 1)])
        em.op("tensor", lambda e: e.matmul(P3[:], onesr[:], nmr[:],
                                           start=True, stop=True),
              waits=[(vec, em.n(vec))], incs=[(pe, 1)])
        em.op("tensor", lambda e: e.matmul(P4[:], onesr[:], rsr[:],
                                           start=True, stop=True), incs=[(pe, 1)])
        bc_done = em.n(pe)
        for g in range(NCH):
            em.op("vector", lambda e, g=g: e.tensor_add(sqs[:], xt[g][:], P3[:]),
                  waits=[(pe, bc_done), (st1, em.n(st1)), (st2, em.n(st2))])
            em.op("vector", lambda e, g=g: e.tensor_mul(xlf[g][:], sqs[:], P4[:]))
            em.op("vector", lambda e, g=g: e.tensor_copy(xlb[g][:], xlf[g][:]),
                  incs=[(vec, 1)])
            em.op("sync", lambda e, g=g, sl=sl: e.dma_start(
                out=xlb_out[g][:, sl], in_=xlb[g][:]),
                waits=[(vec, em.n(vec))], incs=[(st1, 16)])
            em.op("sync", lambda e, g=g, sl=sl: e.dma_start(
                out=xlf_out[g][:, sl], in_=xlf[g][:]),
                waits=[(vec, em.n(vec))], incs=[(st2, 16)])
        x_free = em.n(pe)
        xl_done = em.n(vec)
        for gc in range(NCH):
            for gd in range(NCH):
                em.op("tensor", lambda e, gc=gc, gd=gd: e.matmul(
                    P1[:], wdtb[gd][:, gc * 128:(gc + 1) * 128], xlb[gd][:],
                    start=(gd == 0), stop=(gd == NCH - 1)),
                    waits=([(vec, em.n(vec)), (sca, wdone)] if gd == 0 else None),
                    incs=([(pe, 1)] if gd == NCH - 1 else None))
            em.op("vector", lambda e, gc=gc: e.tensor_scalar_add(
                out=dtt[:], in0=P1[:], scalar1=bdtr[:, gc:gc + 1]),
                waits=[(pe, em.n(pe)), (st3, em.n(st3)), (st4, em.n(st4))],
                incs=[(vec, 1)])
            em.op("scalar", lambda e: e.activation(out=spt[:], in_=dtt[:],
                                                   func=AF.Abs),
                  waits=[(vec, em.n(vec))], incs=[(sca, 1)])
            em.op("scalar", lambda e: e.activation(out=spt[:], in_=spt[:],
                                                   func=AF.Exp, scale=-1.0),
                  incs=[(sca, 1)])
            em.op("vector", lambda e: e.tensor_scalar_add(
                out=spt[:], in0=spt[:], scalar1=1.0), waits=[(sca, em.n(sca))],
                incs=[(vec, 1)])
            em.op("scalar", lambda e: e.activation(out=spt[:], in_=spt[:],
                                                   func=AF.Ln),
                  waits=[(vec, em.n(vec))], incs=[(sca, 1)])
            em.op("vector", lambda e: e.tensor_relu(out=dtt[:], in_=dtt[:]),
                  waits=[(sca, em.n(sca))])
            em.op("vector", lambda e: e.tensor_add(dtt[:], dtt[:], spt[:]),
                  incs=[(vec, 1)])
            em.op("scalar", lambda e: e.activation(
                out=dtt[:], in_=dtt[:], func=AF.Tanh, scale=s0 / r0,
                bias=-c0 * s0 / r0),
                waits=[(vec, em.n(vec))], incs=[(sca, 1)])
            em.op("scalar", lambda e: e.activation(
                out=dtt[:], in_=dtt[:], func=AF.Copy, scale=r0 / s0, bias=c0),
                incs=[(sca, 1)])
            em.op("vector", lambda e, gc=gc: e.tensor_mul(
                dxt[:], dtt[:], xlf[gc][:]), waits=[(sca, em.n(sca))],
                incs=[(vec, 1)])
            dx_done = em.n(vec)
            em.op("sync", lambda e, gc=gc, sl=sl: e.dma_start(
                out=dt_out[gc][:, sl], in_=dtt[:]),
                waits=[(vec, dx_done)], incs=[(st3, 16)])
            em.op("sync", lambda e, gc=gc, sl=sl: e.dma_start(
                out=dtx_out[gc][:, sl], in_=dxt[:]),
                waits=[(vec, dx_done)], incs=[(st4, 16)])
        for wi, (outap, sem_) in enumerate(((b_out, stb), (c_out, stc))):
            for gd in range(NCH):
                em.op("tensor", lambda e, wi=wi, gd=gd: e.matmul(
                    PB[:], wbc16b[:, gd * 32 + wi * 16:gd * 32 + (wi + 1) * 16],
                    xlb[gd][:], start=(gd == 0), stop=(gd == NCH - 1)),
                    waits=([(vec, em.n(vec)), (sca, 1)] if gd == 0 else None),
                    incs=([(pe, 1)] if gd == NCH - 1 else None))
            em.op("vector", lambda e: e.tensor_copy(bc16[:], PB[:]),
                  waits=[(pe, em.n(pe)), (stb, em.n(stb)), (stc, em.n(stc))],
                  incs=[(vec, 1)])
            em.op("sync", lambda e, outap=outap, sl=sl: e.dma_start(
                out=outap[:, sl], in_=bc16[:]),
                waits=[(vec, em.n(vec))], incs=[(sem_, 16)])
    for s_ in (st1, st2, st3, st4, stb, stc):
        em.op("sync", lambda e, s_=s_: e.wait_ge(s_, em.n(s_)))
    return p.finish(sim_order=sim_order)


def build_k5b(sim_order=True, elide=True):
    p = Prog()
    nc = p.nc
    NIT = 256
    dt3_in = p.din("dt3", [8, L * 128], F32)
    dtx3_in = p.din("dtx3", [8, L * 128], F32)
    be2_in = p.din("b_exp2", [128, L], F32)
    ce_in = p.din("c_exp", [128, L], F32)
    alog_in = p.din("alogr", [128, 128], F32)
    g8_in = p.din("g8", [8, 128], F32)
    mask8_in = p.din("mask8", [128, 8], F32)
    onesb_in = p.din("onesb", [128, 128], BF16)
    y_out = p.dout("y_d", [2 * NIT, 8, 1024], F32)
    s_out = p.dout("s_d", [2 * NIT, 128, 8], F32)

    ar = p.sb([128, 128])
    g8 = p.sb([8, 128])
    mask8 = p.sb([128, 8])
    onesb = p.sb([128, 128], BF16)
    dt3 = [p.sb([8, 1024]) for _ in range(2)]
    dtx3 = [p.sb([8, 1024]) for _ in range(2)]
    be2 = [p.sb([128, 8]) for _ in range(2)]
    ce = [p.sb([128, 8]) for _ in range(2)]
    mt4 = [p.sb([128, 128]) for _ in range(4)]
    ut2 = [p.sb([128, 128]) for _ in range(2)]
    zb = p.sb([128, 1024])
    ab = [p.sb([128, 1024]) for _ in range(2)]
    wb = [p.sb([128, 1024]) for _ in range(2)]
    msq = p.sb([128, 128])
    prows = p.sb([128, 16], BF16)
    vmc = p.sb([128, 16])
    scol = p.sb([128, 16])
    cblk = [p.sb([128, 64]) for _ in range(2)]
    ybuf = [p.sb([8, 1024]) for _ in range(2)]
    spare = p.sb([128, 1])
    PA = p.ps([128, 1024])
    PX = p.ps([128, 1024])
    PN = p.ps([128, 16])
    PY = [p.ps([8, 512]) for _ in range(2)]

    lc = p.sem()

    em = p.em
    for ap_, t in ((alog_in, ar), (g8_in, g8), (mask8_in, mask8),
                   (onesb_in, onesb)):
        em.op("sync", lambda e, ap_=ap_, t=t: e.dma_start(out=t[:], in_=ap_),
              incs=[(lc, 16)])
    lc_all = em.n(lc)
    sca5 = p.sem()
    em.op("scalar", lambda e: e.activation(out=ar[:], in_=ar[:], func=AF.Tanh,
                                           scale=0.1), waits=[(lc, lc_all)])
    em.op("scalar", lambda e: e.activation(out=ar[:], in_=ar[:], func=AF.Exp,
                                           scale=10.0), incs=[(sca5, 1)])
    em.op("vector", lambda e: e.tensor_scalar_mul(out=ar[:], in0=ar[:],
                                                  scalar1=-1.0),
          waits=[(sca5, 1)])
    for k in range(4):
        em.op("vector", lambda e, k=k: e.memset(mt4[k][:], 0.0))
    em.op("vector", lambda e: e.memset(scol[:], 1.0))

    lem = LoopEmitter(NIT, self_wait=(), elide=elide)
    h_ld = None
    for s in range(2):
        for src_, dst in ((dt3_in, dt3[s]), (dtx3_in, dtx3[s])):
            h_ld = lem.op("sync", lambda e, i, s=s, src_=src_, dst=dst:
                          e.dma_start(out=dst[:],
                                      in_=src_[:, bass.ts(i * 2 + s, 1024)]))
        h_ld = lem.op("sync", lambda e, i, s=s: e.dma_start(
            out=be2[s][:], in_=be2_in[:, bass.ts(i * 2 + s, 8)]))
        h_ld = lem.op("sync", lambda e, i, s=s: e.dma_start(
            out=ce[s][:], in_=ce_in[:, bass.ts(i * 2 + s, 8)]))

    def mk_bc(s, which, h):
        def f(e, i):
            tgt = PA if which == 0 else PX
            srct = dt3[s] if which == 0 else dtx3[s]
            return e.matmul(tgt[:, h * 512:(h + 1) * 512], g8[:],
                            srct[:, h * 512:(h + 1) * 512],
                            start=True, stop=True)
        return f

    def mk_z(s, j):
        def f(e, i):
            return e.tensor_tensor(
                out=zb[:, j * 128:(j + 1) * 128], in0=ar[:],
                in1=PA[:, j * 128:(j + 1) * 128], op=ALU.mult)
        return f

    def mk_w(s, j):
        def f(e, i):
            return e.tensor_tensor(
                out=wb[s][:, j * 128:(j + 1) * 128],
                in0=PX[:, j * 128:(j + 1) * 128],
                in1=be2[s][:, j:j + 1].broadcast_to((128, 128)), op=ALU.mult)
        return f

    def mk_cb(s, j):
        def f(e, i):
            return e.tensor_scalar_mul(
                out=cblk[s][:, j * 8:(j + 1) * 8], in0=mask8[:],
                scalar1=ce[s][:, j:j + 1])
        return f

    def mk_sp():
        return lambda e, i: e.tensor_scalar_mul(out=spare[:],
                                                in0=mask8[:, 0:1], scalar1=1.0)

    hta = [lem.op("tensor", mk_bc(0, 0, h), waits=[h_ld]) for h in range(2)]
    htx = [lem.op("tensor", mk_bc(0, 1, h), waits=[h_ld]) for h in range(2)]
    hzA = [lem.op("vector", mk_z(0, j), waits=[hta[j // 4]]) for j in range(8)]
    hwp = {}
    for j in range(8):
        hwp[j] = lem.op("vector", mk_w(0, j), waits=[htx[j // 4], h_ld])
    hcbA = [lem.op("vector", mk_cb(0, j), waits=[h_ld]) for j in range(8)]
    ha2h = {}
    ha1A = lem.op("scalar", lambda e, i: e.activation(
        out=ab[0][:], in_=zb[:], func=AF.Tanh, scale=1.0 / 16.0),
        waits=[hzA[7]])
    ha2h[(0, 0)] = ha2h[(0, 1)] = lem.op("scalar", lambda e, i: e.activation(
        out=ab[0][:], in_=ab[0][:], func=AF.Exp, scale=16.0), waits=[ha1A])

    hzB = {}
    htb = {}
    slotA = {(0, 0): ("cb", 0), (0, 1): ("cb", 1),
             (1, 0): ("cb", 2), (1, 1): ("cb", 3),
             (2, 0): ("z", 0), (2, 1): ("z", 1),
             (3, 0): ("z", 2), (3, 1): ("z", 3),
             (4, 0): ("z", 4), (4, 1): ("z", 5),
             (5, 0): ("z", 6), (5, 1): ("z", 7),
             (6, 0): ("w", 0), (6, 1): ("w", 1),
             (7, 0): ("w", 2), (7, 1): ("w", 3)}
    slotB = {(0, 0): ("cb", 4), (0, 1): ("cb", 5),
             (1, 0): ("cb", 6), (1, 1): ("cb", 7)}

    hm = [None] * 16
    hsq = [None] * 16
    hn = [None] * 16
    hvm = [None] * 16
    hrc = [None] * 16
    hs = [None] * 16
    hu = [None] * 16
    hy = [None] * 16
    he = [None] * 4

    def spacer(s, step, slot):
        key = (step, slot)
        tab = slotA if s == 0 else slotB
        if key not in tab:
            return lem.op("vector", mk_sp())
        kind, idx = tab[key]
        if kind == "cb":
            return lem.op("vector", mk_cb(1, idx), waits=[h_ld])
        if kind == "z":
            h = lem.op("vector", mk_z(1, idx),
                       waits=[htb[("a", idx // 4)], ha1A])
            hzB[idx] = h
            return h
        h = lem.op("vector", mk_w(1, idx), waits=[htb[("x", idx // 4)], h_ld])
        hwp[8 + idx] = h
        return h

    for g in range(16):
        s, j = g // 8, g % 8
        mcur = mt4[g % 4]
        mprev = mt4[(g + 3) % 4]
        uc = ut2[g % 2]
        sprev_ap = scol[:, 15:16] if g == 0 else scol[:, g - 1:g]
        if g == 8:
            for idx in range(4, 8):
                hwp[8 + idx] = lem.op("vector", mk_w(1, idx),
                                      waits=[htb[("x", idx // 4)], h_ld])
        hu[g] = lem.op("gpsimd", lambda e, i, j=j, mprev=mprev, uc=uc, s=s:
                       e.tensor_tensor(out=uc[:],
                                       in0=ab[s][:, j * 128:(j + 1) * 128],
                                       in1=mprev[:], op=ALU.mult),
                       waits=[ha2h[(s, j // 4)]]
                             + ([hm[g - 1]] if g > 0 else []))
        hm[g] = lem.op("vector", lambda e, i, g=g, j=j, s=s, mcur=mcur, uc=uc,
                       sprev_ap=sprev_ap: e.scalar_tensor_tensor(
                           out=mcur[:], in0=uc[:], scalar=sprev_ap,
                           in1=wb[s][:, j * 128:(j + 1) * 128],
                           op0=ALU.mult, op1=ALU.add),
                       waits=[hu[g], hwp[g]] + ([hs[g - 1]] if g > 0 else [])
                             + ([hy[g - 4]] if g >= 4 else []))
        spacer(s, j, 0)
        hsq[g] = lem.op("vector", lambda e, i, g=g, mcur=mcur:
                        e.scalar_tensor_tensor(
                            out=msq[:], in0=mcur[:], scalar=1.0, in1=mcur[:],
                            op0=ALU.mult, op1=ALU.mult,
                            accum_out=prows[:, g:g + 1]),
                        waits=[hm[g]])
        hn[g] = lem.op("tensor", lambda e, i, g=g: e.matmul(
            PN[:, g:g + 1], onesb[:], prows[:, g:g + 1],
            start=True, stop=True), waits=[hsq[g]])
        hvm[g] = lem.op("vector", lambda e, i, g=g: e.tensor_single_scalar(
            out=vmc[:, g:g + 1], in_=PN[:, g:g + 1], scalar=400.0,
            op=ALU.max), waits=[hn[g]])
        spacer(s, j, 1)
        hrc[g] = lem.op("vector", lambda e, i, g=g: e.reciprocal(
            out=vmc[:, g:g + 1], in_=vmc[:, g:g + 1]), waits=[hvm[g]])
        hy[g] = lem.op("tensor", lambda e, i, g=g, s=s, j=j, mcur=mcur:
                       e.matmul(PY[j // 4][:, (j % 4) * 128:(j % 4 + 1) * 128],
                                cblk[s][:, j * 8:(j + 1) * 8], mcur[:],
                                start=True, stop=True),
                       waits=[hsq[g]] + ([hcbA[j]] if s == 0 else [])
                             + ([he[2 * s + j // 4 - 2]]
                                if (2 * s + j // 4 - 2) >= 0 else []))
        if s == 0 and 1 <= j <= 4:
            which = (j - 1) // 2
            half = (j - 1) % 2
            key = ("a", half) if which == 0 else ("x", half)
            htb[key] = lem.op(
                "tensor", mk_bc(1, which, half),
                waits=[hzA[7] if which == 0 else hwp[7]])
        hs[g] = lem.op("scalar", lambda e, i, g=g: e.activation(
            out=scol[:, g:g + 1], in_=vmc[:, g:g + 1], func=AF.Sqrt,
            scale=400.0), waits=[hrc[g]])
        if j == 3:
            he[2 * s] = lem.op("scalar", lambda e, i, s=s: e.copy(
                out=ybuf[s][:, 0:512], in_=PY[0][:]), waits=[hy[g]])
        if j == 7:
            he[2 * s + 1] = lem.op("scalar", lambda e, i, s=s: e.copy(
                out=ybuf[s][:, 512:1024], in_=PY[1][:]), waits=[hy[g]])
        if s == 0 and j == 4:
            ha1h1 = lem.op("scalar", lambda e, i: e.activation(
                out=ab[1][:, 0:512], in_=zb[:, 0:512], func=AF.Tanh,
                scale=1.0 / 16.0), waits=[hzB[3]])
        if s == 0 and j == 5:
            ha2h[(1, 0)] = lem.op("scalar", lambda e, i: e.activation(
                out=ab[1][:, 0:512], in_=ab[1][:, 0:512], func=AF.Exp,
                scale=16.0), waits=[ha1h1])
        if s == 1 and j == 0:
            ha1h2 = lem.op("scalar", lambda e, i: e.activation(
                out=ab[1][:, 512:1024], in_=zb[:, 512:1024], func=AF.Tanh,
                scale=1.0 / 16.0), waits=[hzB[7]])
        if s == 1 and j == 1:
            ha2h[(1, 1)] = lem.op("scalar", lambda e, i: e.activation(
                out=ab[1][:, 512:1024], in_=ab[1][:, 512:1024], func=AF.Exp,
                scale=16.0), waits=[ha1h2])
        if j == 7:
            lem.op("sync", lambda e, i, s=s: e.dma_start(
                out=y_out[bass.ts(i * 2 + s, 1)], in_=ybuf[s][:]),
                waits=[he[2 * s + 1]])
            lem.op("sync", lambda e, i, s=s: e.dma_start(
                out=s_out[bass.ts(i * 2 + s, 1)],
                in_=scol[:, s * 8:(s + 1) * 8]), waits=[he[2 * s + 1]])

    with p.ctx:
        osems = ({e: p.sem() for e in ("scalar", "vector", "tensor")
                  if em.ops[e]} if sim_order else None)
        with nc.Block() as blk1:
            em.emit(blk1, order_sems=osems)
        pre = {}
        if osems:
            pre["tensor"] = [(osems["vector"], len(em.ops["vector"])),
                             (osems["scalar"], len(em.ops["scalar"])),
                             (lc, lc_all)]
            pre["gpsimd"] = [(osems["vector"], len(em.ops["vector"])),
                             (osems["scalar"], len(em.ops["scalar"])),
                             (lc, lc_all)]
            pre["vector"] = [(osems["scalar"], len(em.ops["scalar"])),
                             (lc, lc_all)]
        pre["sync"] = [(lc, lc_all)]
        with nc.Block() as blk2:
            lem.emit(blk2, p, pre=pre, post_self=("sync",))
    return p.nc


def build_k6(sim_order=True):
    p = Prog()
    em = p.em
    nc = p.nc
    x2_in = p.din("x2_fm", [NCH, 128, L], F32)
    y_in = p.din("y_fm", [NCH, 128, L], F32)
    s_in = p.din("s_row", [1, L], F32)
    xlf_in = p.din("xln2f_fm", [NCH, 128, L], F32)
    dp_in = p.din("dpr", [128, NCH], F32)
    wg_in = p.din("wgTc", [24, D, 128], I32)
    wu_in = p.din("wuTc", [24, D, 128], I32)
    wd_in = p.din("wdT", [3072, D], I32)
    onesc_in = p.din("onesc", [128, 1], F32)
    onesr_in = p.din("onesr", [1, 128], F32)
    x4_out = p.dout("x4_fm", [NCH, 128, L], F32)
    x3_d = nc.dram_tensor("x3_d", [NCH, 128, L], F32, kind="Internal").ap()

    onesc = p.sb([128, 1])
    onesr = p.sb([1, 128])
    dpr = p.sb([128, NCH])
    wstage = p.sb([128, D], I32)
    wgu_st = [p.sb([128, D], I32) for _ in range(2)]
    wgu_b = [p.sb([128, D], BF16) for _ in range(2)]
    wdb = [p.sb([128, D], BF16) for _ in range(24)]
    srow = p.sb([1, 512])
    xt = [p.sb([128, 512]) for _ in range(NCH)]
    yt = p.sb([128, 512])
    xlt = p.sb([128, 512])
    xlb = [p.sb([128, 512], BF16) for _ in range(NCH)]
    sqs = p.sb([128, 512])
    nmr = p.sb([1, 512]); mr = p.sb([1, 512]); m2r = p.sb([1, 512])
    vr = p.sb([1, 512]); rsr = p.sb([1, 512])
    sgt = [p.sb([128, 512], BF16) for _ in range(24)]
    ubt = p.sb([128, 512], BF16)
    x4t = p.sb([128, 512])
    P1 = p.ps([128, 512])
    P2 = p.ps([128, 512])
    P3 = p.ps([128, 512])
    P4 = p.ps([128, 512])

    lc = p.sem(); lw = p.sem(); lx = p.sem(); ls5 = p.sem()
    sca = p.sem(); vec = p.sem(); pe = p.sem(); st3 = p.sem(); st4 = p.sem()

    for ap_, t in ((onesc_in, onesc), (onesr_in, onesr), (dp_in, dpr)):
        em.op("sync", lambda e, ap_=ap_, t=t: e.dma_start(out=t[:], in_=ap_),
              incs=[(lc, 16)])
    lc_all = em.n(lc)
    wfree = 0
    for fc in range(24):
        em.op("sync", lambda e, fc=fc: e.dma_start(
            out=wstage[:], in_=wd_in[fc * 128:(fc + 1) * 128, :]),
            waits=([(sca, wfree)] if wfree else []), incs=[(lw, 16)])
        em.op("scalar", lambda e, fc=fc: e.copy(
            out=wdb[fc][:], in_=wstage[:]),
            waits=[(lw, em.n(lw))], incs=[(sca, 1)])
        wfree = em.n(sca)
    wdone = em.n(sca)

    x_free = 0
    for t8 in range(8):
        sl = slice(t8 * 512, (t8 + 1) * 512)
        em.op("sync", lambda e, sl=sl: e.dma_start(out=srow[:], in_=s_in[:, sl]),
              waits=[(pe, x_free)], incs=[(ls5, 16)])
        em.op("tensor", lambda e: e.matmul(P4[:], onesr[:], srow[:],
                                           start=True, stop=True),
              waits=[(ls5, em.n(ls5)), (lc, lc_all), (vec, em.n(vec))],
              incs=[(pe, 1)])
        sbc = em.n(pe)
        for g in range(NCH):
            em.op("sync", lambda e, g=g, sl=sl: e.dma_start(
                out=yt[:], in_=y_in[g][:, sl]),
                waits=[(vec, em.n(vec))], incs=[(lx, 16)])
            em.op("sync", lambda e, g=g, sl=sl: e.dma_start(
                out=xlt[:], in_=xlf_in[g][:, sl]), incs=[(lx, 16)])
            em.op("sync", lambda e, g=g, sl=sl: e.dma_start(
                out=xt[g][:], in_=x2_in[g][:, sl]), incs=[(lx, 16)])
            lxn = em.n(lx)
            em.op("vector", lambda e, g=g: e.tensor_mul(sqs[:], yt[:], P4[:]),
                  waits=[(lx, lxn), (pe, sbc), (st3, em.n(st3))])
            em.op("vector", lambda e, g=g: e.scalar_tensor_tensor(
                out=sqs[:], in0=xlt[:], scalar=dpr[:, g:g + 1], in1=sqs[:],
                op0=ALU.mult, op1=ALU.add))
            em.op("vector", lambda e, g=g: e.tensor_add(sqs[:], sqs[:], xt[g][:]),
                  incs=[(vec, 1)])
            em.op("sync", lambda e, g=g, sl=sl: e.dma_start(
                out=x3_d[g][:, sl], in_=sqs[:]),
                waits=[(vec, em.n(vec))], incs=[(st3, 16)])
        x_free = em.n(pe)
    x3_done = em.n(st3)

    x_free = 0
    for t8 in range(8):
        sl = slice(t8 * 512, (t8 + 1) * 512)
        for g in range(NCH):
            em.op("sync", lambda e, g=g, sl=sl: e.dma_start(
                out=xt[g][:], in_=x3_d[g][:, sl]),
                waits=[(pe, x_free), (st3, x3_done)], incs=[(lx, 16)])
        lx_all = em.n(lx)
        for g in range(NCH):
            em.op("tensor", lambda e, g=g: e.matmul(
                P1[0:1, :], onesc[:], xt[g][:], start=(g == 0), stop=(g == NCH - 1)),
                waits=([(lx, lx_all), (vec, em.n(vec))] if g == 0 else None),
                incs=[(pe, 1)])
        for g in range(NCH):
            em.op("scalar", lambda e, g=g: e.activation(
                out=sqs[:], in_=xt[g][:], func=AF.Square),
                waits=[(pe, em.n(pe))], incs=[(sca, 1)])
            em.op("tensor", lambda e, g=g: e.matmul(
                P2[0:1, :], onesc[:], sqs[:], start=(g == 0), stop=(g == NCH - 1)),
                waits=[(sca, em.n(sca))] + ([(vec, em.n(vec))] if g == 0 else []),
                incs=[(pe, 1)])
        pe_done = em.n(pe)
        em.op("vector", lambda e: e.tensor_scalar_mul(
            out=nmr[:], in0=P1[0:1, :], scalar1=-1.0 / D), waits=[(pe, pe_done)])
        em.op("vector", lambda e: e.tensor_scalar_mul(
            out=mr[:], in0=P1[0:1, :], scalar1=1.0 / D))
        em.op("vector", lambda e: e.tensor_mul(m2r[:], mr[:], nmr[:]))
        em.op("vector", lambda e: e.scalar_tensor_tensor(
            out=vr[:], in0=P2[0:1, :], scalar=1.0 / D, in1=m2r[:],
            op0=ALU.mult, op1=ALU.add))
        em.op("vector", lambda e: e.tensor_scalar_add(out=vr[:], in0=vr[:],
                                                      scalar1=1e-5), incs=[(vec, 1)])
        em.op("scalar", lambda e: e.activation(out=rsr[:], in_=vr[:], func=AF.Sqrt),
              waits=[(vec, em.n(vec))], incs=[(sca, 1)])
        em.op("vector", lambda e: e.nop(), waits=[(sca, em.n(sca))])
        em.op("vector", lambda e: e.reciprocal(out=rsr[:], in_=rsr[:]),
              incs=[(vec, 1)])
        em.op("tensor", lambda e: e.matmul(P3[:], onesr[:], nmr[:],
                                           start=True, stop=True),
              waits=[(vec, em.n(vec))], incs=[(pe, 1)])
        em.op("tensor", lambda e: e.matmul(P4[:], onesr[:], rsr[:],
                                           start=True, stop=True), incs=[(pe, 1)])
        bc_done = em.n(pe)
        for g in range(NCH):
            em.op("vector", lambda e, g=g: e.tensor_add(sqs[:], xt[g][:], P3[:]),
                  waits=[(pe, bc_done)])
            em.op("vector", lambda e, g=g: e.tensor_mul(xlb[g][:], sqs[:], P4[:]),
                  incs=[(vec, 1)])
        xl_done = em.n(vec)
        x_free = em.n(pe)
        for fc in range(24):
            for wi, win in enumerate((wg_in, wu_in)):
                em.op("sync", lambda e, fc=fc, wi=wi, win=win: e.dma_start(
                    out=wgu_st[wi][:].rearrange("p (g c) -> p g c", g=NCH),
                    in_=win[fc].rearrange("(g p) c -> p g c", p=128)),
                    waits=[(sca, em.n(sca))], incs=[(lw, 16)])
                em.op("scalar", lambda e, wi=wi: e.copy(
                    out=wgu_b[wi][:], in_=wgu_st[wi][:]),
                    waits=[(lw, em.n(lw)), (pe, em.n(pe))], incs=[(sca, 1)])
            wld = em.n(sca)
            for gd in range(NCH):
                em.op("tensor", lambda e, fc=fc, gd=gd: e.matmul(
                    P1[:], wgu_b[0][:, gd * 128:(gd + 1) * 128], xlb[gd][:],
                    start=(gd == 0), stop=(gd == NCH - 1)),
                    waits=([(vec, xl_done), (sca, wld)] if gd == 0 else None),
                    incs=([(pe, 1)] if gd == NCH - 1 else None))
            em.op("scalar", lambda e, fc=fc: e.activation(
                out=sgt[fc][:], in_=P1[:], func=AF.Silu, scale=1.0 / 127.0),
                waits=[(pe, em.n(pe))], incs=[(sca, 1)])
            for gd in range(NCH):
                em.op("tensor", lambda e, fc=fc, gd=gd: e.matmul(
                    P2[:], wgu_b[1][:, gd * 128:(gd + 1) * 128], xlb[gd][:],
                    start=(gd == 0), stop=(gd == NCH - 1)),
                    waits=([(sca, em.n(sca)), (vec, em.n(vec))] if gd == 0 else None),
                    incs=([(pe, 1)] if gd == NCH - 1 else None))
            em.op("vector", lambda e, fc=fc: e.tensor_scalar_mul(
                out=ubt[:], in0=P2[:], scalar1=1.0 / 127.0),
                waits=[(pe, em.n(pe))])
            em.op("vector", lambda e, fc=fc: e.tensor_mul(
                sgt[fc][:], sgt[fc][:], ubt[:]), incs=[(vec, 1)])
        h_done = em.n(vec)
        for co in range(NCH):
            for fc in range(24):
                em.op("tensor", lambda e, co=co, fc=fc: e.matmul(
                    P3[:], wdb[fc][:, co * 128:(co + 1) * 128], sgt[fc][:],
                    start=(fc == 0), stop=(fc == 23)),
                    waits=([(vec, em.n(vec))] if fc == 0 else None),
                    incs=([(pe, 1)] if fc == 23 else None))
            em.op("vector", lambda e, co=co: e.scalar_tensor_tensor(
                out=x4t[:], in0=P3[:], scalar=1.0 / 127.0, in1=xt[co][:],
                op0=ALU.mult, op1=ALU.add),
                waits=[(pe, em.n(pe)), (st4, em.n(st4))], incs=[(vec, 1)])
            em.op("sync", lambda e, co=co, sl=sl: e.dma_start(
                out=x4_out[co][:, sl], in_=x4t[:]),
                waits=[(vec, em.n(vec))], incs=[(st4, 16)])
    em.op("sync", lambda e: e.wait_ge(st4, em.n(st4)))
    return p.finish(sim_order=sim_order)


_IDENT_BF16 = None


def _host_k1_inputs(inputs):
    import ml_dtypes
    global _IDENT_BF16
    if _IDENT_BF16 is None:
        _IDENT_BF16 = np.eye(128, dtype=ml_dtypes.bfloat16)
    wqT = np.ascontiguousarray(inputs["L0_attn_q_proj_w"].T)
    wkT = np.ascontiguousarray(inputs["L0_attn_k_proj_w"].T)
    wvT = np.ascontiguousarray(inputs["L0_attn_v_proj_w"].T)
    bpw = inputs["L0_attn_b_proj_w"]
    wbT = np.ascontiguousarray(
        bpw.reshape(4, NCH, 128).transpose(2, 1, 0).reshape(128, NCH * 4))
    def conv_r(wc):
        return np.ascontiguousarray(
            wc.reshape(NCH, 128, 4).transpose(1, 0, 2).reshape(128, NCH * 4))
    qcw = conv_r(inputs["L0_attn_q_conv_w"])
    kcw = conv_r(inputs["L0_attn_k_conv_w"])
    vcw = conv_r(inputs["L0_attn_v_conv_w"])
    maps = []
    for b in range(N_CORES):
        maps.append({
            "x": np.ascontiguousarray(inputs["L0_attn_hidden_states"][b]),
            "wqT": wqT, "wkT": wkT, "wvT": wvT, "wbT": wbT,
            "qcw": qcw, "kcw": kcw, "vcw": vcw,
            "ident": _IDENT_BF16,
        })
    return maps


def run_k1(inputs):
    nc = build_k1(sim_order=True)
    maps = _host_k1_inputs(inputs)
    res = run_bass_kernel_spmd(nc, maps, core_ids=list(range(N_CORES)))
    return res.results


def _ld_to_fm(a):
    return np.ascontiguousarray(a.reshape(L, NCH, 128).transpose(1, 2, 0))


def _run(nc, maps):
    res = run_bass_kernel_spmd(nc, maps, core_ids=list(range(N_CORES)))
    return res.results


def kernel(**inputs):
    import ml_dtypes
    x0 = np.asarray(inputs["L0_attn_hidden_states"], np.float32)

    r1 = _run(build_k1(sim_order=True), _host_k1_inputs(inputs))

    consts2 = _host_k2_consts()
    m2 = [dict(q_fm=r1[b]["q_fm"], k_fm=r1[b]["k_fm"], v_fm=r1[b]["v_fm"],
               beta=r1[b]["beta"], **consts2) for b in range(N_CORES)]
    r2 = _run(build_k2(sim_order=True), m2)

    h3a = _host_k3a_inputs(inputs)
    m3a = [dict(v_fm=r1[b]["v_fm"], delta_tm=r2[b]["delta_tm"], **h3a)
           for b in range(N_CORES)]
    r3a = _run(build_k3a(sim_order=True), m3a)

    onesc = np.ones((128, 1), np.float32)
    m3b = [dict(ls_fm=r3a[b]["ls_fm"], ll_fm=r3a[b]["ll_fm"],
                delta_fm=r3a[b]["delta_fm"], v_fm=r1[b]["v_fm"], onesc=onesc)
           for b in range(N_CORES)]
    r3b = _run(build_k3b(sim_order=True), m3b)

    h3c = _host_k3c_inputs(inputs)
    m3c = [dict(xln_fm=r1[b]["xln_fm"], stats=r3b[b]["stats"],
                ls_fm=r3a[b]["ls_fm"], ll_fm=r3a[b]["ll_fm"],
                delta_fm=r3a[b]["delta_fm"], v_fm=r1[b]["v_fm"],
                x_fm=_ld_to_fm(x0[b]), **h3c) for b in range(N_CORES)]
    r3c = _run(build_k3c(sim_order=True), m3c)

    h4 = _host_k4_inputs(inputs)
    m4a = [dict(x1_fm=r3c[b]["x1_fm"], wgr=h4["wgr"], onesc=h4["onesc"],
                onesr=h4["onesr"]) for b in range(N_CORES)]
    r4a = _run(build_k4a(sim_order=True), m4a)

    probs_all = np.concatenate([r4a[b]["probs"] for b in range(N_CORES)], axis=0)
    pr = np.zeros((128, 2048), np.float32)
    for e_ in range(8):
        for th in range(16):
            pr[e_ * 16 + th, :] = probs_all[th * 2048:(th + 1) * 2048, e_]
    m4b = [dict(x1_fm=r3c[b]["x1_fm"], xln1_fm=r4a[b]["xln1_fm"], probs_r=pr,
                own_p=np.ascontiguousarray(probs_all[b * L:(b + 1) * L, :]),
                w1f=h4["w1f"], w2f=h4["w2f"], emask=h4["emask"], emtT=h4["emtT"],
                onesr=h4["onesr"], identf=h4["identf"]) for b in range(N_CORES)]
    r4b = _run(build_k4b(sim_order=True), m4b)

    wdtT = np.ascontiguousarray(inputs["L1_attn_Wdt"].T)
    bdtr = np.ascontiguousarray(inputs["L1_attn_bdt"].reshape(NCH, 128).T)
    WB, WC = inputs["L1_attn_WB"], inputs["L1_attn_WC"]
    wbc = np.zeros((128, 32 * NCH), np.float32)
    for g in range(NCH):
        wbc[:, g * 32:g * 32 + 16] = WB[:, g * 128:(g + 1) * 128].T
        wbc[:, g * 32 + 16:g * 32 + 32] = WC[:, g * 128:(g + 1) * 128].T
    c0, r0, s0 = 0.0505, 0.0495, 0.5
    cvals = np.tile(np.array([[-c0 * s0 / r0, c0]], np.float32), (128, 1))
    m5a = [dict(x2_fm=r4b[b]["x2_fm"], wdtT=wdtT, bdtr=bdtr, wbcr=wbc,
                onesc=onesc, onesr=h4["onesr"], cvals=cvals)
           for b in range(N_CORES)]
    r5a = _run(build_k5a(sim_order=True), m5a)

    al = inputs["L1_attn_A_log"]
    alogr = np.zeros((128, 128), np.float32)
    for g in range(NCH):
        for n in range(16):
            alogr[g * 16 + n, :] = al[g * 128:(g + 1) * 128, n]
    g8m = np.zeros((8, 128), np.float32)
    mask8 = np.zeros((128, 8), np.float32)
    for g in range(NCH):
        g8m[g, g * 16:(g + 1) * 16] = 1.0
        mask8[g * 16:(g + 1) * 16, g] = 1.0
    m5b = []
    for b in range(N_CORES):
        dt3 = np.ascontiguousarray(
            r5a[b]["dt_fm"].transpose(0, 2, 1)).reshape(NCH, L * 128)
        dtx3 = np.ascontiguousarray(
            r5a[b]["dtx_fm"].transpose(0, 2, 1)).reshape(NCH, L * 128)
        brows = r5a[b]["b_rows"]
        crows = r5a[b]["c_rows"]
        bex2 = np.zeros((128, L), np.float32)
        cex = np.zeros((128, L), np.float32)
        for g in range(NCH):
            bex2[g * 16:(g + 1) * 16, :] = brows
            cex[g * 16:(g + 1) * 16, :] = crows
        m5b.append(dict(dt3=dt3, dtx3=dtx3, b_exp2=bex2, c_exp=cex,
                        alogr=alogr, g8=g8m, mask8=mask8,
                        onesb=np.ones((128, 128), ml_dtypes.bfloat16)))
    r5b = _run(build_k5b(sim_order=True, elide=True), m5b)

    m6 = []
    wgTc = np.ascontiguousarray(
        inputs["L1_ffn_w_gate"].T.reshape(D, 24, 128).transpose(1, 0, 2))
    wuTc = np.ascontiguousarray(
        inputs["L1_ffn_w_up"].T.reshape(D, 24, 128).transpose(1, 0, 2))
    wdT = np.ascontiguousarray(inputs["L1_ffn_w_down"].T)
    dpr = np.ascontiguousarray(np.asarray(inputs["L1_attn_Dp"]).reshape(NCH, 128).T)
    for b in range(N_CORES):
        y_d = r5b[b]["y_d"]
        s_d = r5b[b]["s_d"]
        yr = y_d.reshape(512, 8, 8, 128)
        y_fm = np.ascontiguousarray(yr.transpose(1, 3, 0, 2).reshape(NCH, 128, L))
        s_row = np.ascontiguousarray(s_d[:, 0, :].reshape(1, L))
        m6.append(dict(x2_fm=r4b[b]["x2_fm"], y_fm=y_fm, s_row=s_row,
                       xln2f_fm=r5a[b]["xln2f_fm"], dpr=dpr,
                       wgTc=wgTc, wuTc=wuTc, wdT=wdT,
                       onesc=onesc, onesr=h4["onesr"]))
    r6 = _run(build_k6(sim_order=True), m6)

    out = np.stack([r6[b]["x4_fm"].transpose(2, 0, 1).reshape(L, D)
                    for b in range(N_CORES)])
    return out.astype(np.float32)
